# version 27
# baseline (speedup 1.0000x reference)
import contextlib
import os
import numpy as np
import concourse.bass as bass
import concourse.mybir as mybir
from concourse.bass_utils import run_bass_kernel_spmd

F32 = mybir.dt.float32
BF16 = mybir.dt.bfloat16
AF = mybir.ActivationFunctionType
ALU = mybir.AluOpType

S = 2048
C = 1024
NH = 16
HD = 64
NP = 8
NCH = 4
CW = 512
NT = 16
NK = 8
EPS = 1e-5
U0 = 1920
STRIPW = 3968

_CACHED = {}


def build_nc(phases="123"):
    p1parts = os.environ.get("KP1", "all")
    p1ch = int(os.environ.get("KPCH", "4"))
    statlvl = int(os.environ.get("KPSTAT", "5"))
    nc = bass.Bass("TRN2", target_bir_lowering=False)

    xT = nc.dram_tensor("xT", [C, S], F32, kind="ExternalInput")
    wqkvT = nc.dram_tensor("wqkvT", [C, 3 * C], F32, kind="ExternalInput")
    bqkv = nc.dram_tensor("bqkv", [128, 24], F32, kind="ExternalInput")
    bqkv_row = nc.dram_tensor("bqkv_row", [1, 3 * C], F32, kind="ExternalInput")
    woT = nc.dram_tensor("woT", [C, C], F32, kind="ExternalInput")
    bo = nc.dram_tensor("bo", [128, 8], F32, kind="ExternalInput")
    ddv = nc.dram_tensor("ddv", [1, 1], F32, kind="ExternalInput")
    wrT = nc.dram_tensor("wrT", [C, C], F32, kind="ExternalInput")
    wkT = nc.dram_tensor("wkT", [C, C], F32, kind="ExternalInput")
    wvT2 = nc.dram_tensor("wvT2", [C, C], F32, kind="ExternalInput")
    mur = nc.dram_tensor("mur", [128, 8], F32, kind="ExternalInput")
    muk = nc.dram_tensor("muk", [128, 8], F32, kind="ExternalInput")
    dist = nc.dram_tensor("dist", [128, STRIPW], F32, kind="ExternalInput")
    lin = nc.dram_tensor("lin", [128, S], F32, kind="ExternalInput")
    linr = nc.dram_tensor("linr", [128, 1544], F32, kind="ExternalInput")
    kdbg = os.environ.get("KDBG", "0") == "1"
    if kdbg:
        dbg_etab = nc.dram_tensor("dbg_etab", [128, S], F32, kind="ExternalOutput")
        dbg_fmt = nc.dram_tensor("dbg_fmt", [128, 512], F32, kind="ExternalOutput")
        dbg_fpt = nc.dram_tensor("dbg_fpt", [128, 512], F32, kind="ExternalOutput")
        dbg_ks = nc.dram_tensor("dbg_ks", [128, S], F32, kind="ExternalOutput")
        dbg_qp = nc.dram_tensor("dbg_qp", [128, 1024], F32, kind="ExternalOutput")
        dbg_p = nc.dram_tensor("dbg_p", [128, 2048], F32, kind="ExternalOutput")
        dbg_p2 = nc.dram_tensor("dbg_p2", [128, 2048], F32, kind="ExternalOutput")
        dbg_och = nc.dram_tensor("dbg_och", [128, 4096], F32, kind="ExternalOutput")
    yT = nc.dram_tensor("yT", [C, S], F32, kind="ExternalOutput")

    ctx = contextlib.ExitStack()
    E = ctx.enter_context

    sems = {}
    cnt = {}
    for nm in ("dq", "dg", "dy", "st", "sv", "ss", "w0", "w1", "v0", "v1", "o0", "o1", "dx0", "dx1", "dx2", "dx3", "fk", "fr", "fv"):
        sems[nm] = E(nc.semaphore(nm))
        cnt[nm] = 0

    def inc(instr, nm, k=1):
        instr.then_inc(sems[nm], k)
        cnt[nm] += k
        return cnt[nm]

    waited = {}

    def wait(eng, nm, val):
        if val > 0 and waited.get((id(eng), nm), 0) < val:
            eng.wait_ge(sems[nm], val)
            waited[(id(eng), nm)] = val

    guard = {}

    def gwait(eng, *keys):
        for k in keys:
            if k in guard:
                wait(eng, *guard[k])

    def gset(key, nm, val):
        guard[key] = (nm, val)

    def barrier():
        vals = {nm: cnt[nm] for nm in ("st", "sv", "ss", "dq", "dg", "w0", "w1", "v0", "v1",
                                        "o0", "o1", "dx0", "dx1", "dx2", "dx3", "fk", "fr", "fv")}
        for eng in (nc.tensor, nc.vector, nc.scalar, nc.sync, nc.gpsimd):
            for nm, v in vals.items():
                wait(eng, nm, v)

    x1f = E(nc.sbuf_tensor("x1f", [128, NK * S], F32))
    bqkv_sb = E(nc.sbuf_tensor("bqkv_sb", [128, 24], F32))
    bo_sb = E(nc.sbuf_tensor("bo_sb", [128, 8], F32))
    mur_sb = E(nc.sbuf_tensor("mur_sb", [128, 8], F32))
    muk_sb = E(nc.sbuf_tensor("muk_sb", [128, 8], F32))
    dd_sb = E(nc.sbuf_tensor("dd_sb", [1, 1], F32))
    absd = E(nc.sbuf_tensor("absd", [1, 1], F32))
    nega = E(nc.sbuf_tensor("nega", [128, 1], F32))
    ones_c = E(nc.sbuf_tensor("ones_c", [128, 1], BF16))
    onesr_bf = E(nc.sbuf_tensor("onesr_bf", [1, 128], BF16))
    onesr32 = E(nc.sbuf_tensor("onesr32", [1, 128], F32))

    ps_s0 = E(nc.psum_tensor("ps_s0", [128, 1024], F32))
    ps_s1 = E(nc.psum_tensor("ps_s1", [128, 1024], F32))
    ps_o = E(nc.psum_tensor("ps_o", [128, 1024], F32))
    ps_mA = E(nc.psum_tensor("ps_mA", [128, 512], F32))
    ps_mB = E(nc.psum_tensor("ps_mB", [128, 512], F32))

    blk = E(nc.Block())

    @blk.sync
    def _(sync):
        T, V, Sc, Y, G = nc.tensor, nc.vector, nc.scalar, nc.sync, nc.gpsimd

        cp_xc = {}
        for xc in range(NCH):
            cp_xc[xc] = (f"dx{xc}", inc(Y.dma_start(
                bass.AP(x1f, xc * CW, [[NK * S, 128], [S, NK], [1, CW]]),
                bass.AP(xT, xc * CW, [[S, 128], [128 * S, NK], [1, CW]])), f"dx{xc}", 16))
        inc(Y.dma_start(bqkv_sb[:, :], bqkv[:, :]), "dq", 16)
        inc(Y.dma_start(bo_sb[:, :], bo[:, :]), "dq", 16)
        inc(Y.dma_start(mur_sb[:, :], mur[:, :]), "dq", 16)
        inc(Y.dma_start(muk_sb[:, :], muk[:, :]), "dq", 16)
        cp_dd = inc(Y.dma_start(dd_sb[:, :], ddv[:, :]), "dq", 16)

        V.memset(ones_c[:, :], 1.0)
        V.memset(onesr_bf[:, :], 1.0)
        cp_vones = inc(V.memset(onesr32[:, :], 1.0), "sv")

        wait(Sc, "dq", cp_dd)
        cp_abs = inc(Sc.activation(absd[:, :], dd_sb[:, :], AF.Abs), "ss")
        wait(T, "ss", cp_abs)
        wait(T, "sv", cp_vones)
        cp_negb = inc(T.matmul(ps_mA[0:128, 0:1], onesr32[:, :], absd[:, :], start=True, stop=True), "st")
        wait(V, "st", cp_negb)
        cp_nega = inc(V.tensor_scalar(nega[:, :], ps_mA[:, 0:1], -1.0, None, ALU.mult), "sv")
        gset("mA", "sv", cp_nega)

        kqv = contextlib.ExitStack()
        KT = kqv.enter_context(nc.sbuf_tensor("KT", [128, NK * S], BF16))
        QT = kqv.enter_context(nc.sbuf_tensor("QT", [128, NK * S], BF16))
        Vaug = kqv.enter_context(nc.sbuf_tensor("Vaug", [128, NT * 1024], BF16))

        with contextlib.ExitStack() as p1:
            E1 = p1.enter_context
            hTc = E1(nc.sbuf_tensor("hTc", [128, 2 * NK * CW], BF16))
            xtmp = E1(nc.sbuf_tensor("xtmp", [128, NK * 256], BF16))
            t1f = E1(nc.sbuf_tensor("t1f", [128, 512], F32))
            wblk = E1(nc.sbuf_tensor("wblk", [128, 2 * 2048], BF16))
            wvq = E1(nc.sbuf_tensor("wvq", [128, 2 * 2048], BF16))
            bvrow_bf = E1(nc.sbuf_tensor("bvrow_bf", [1, C], BF16))
            rows = E1(nc.sbuf_tensor("rows", [1, 1536], F32))

            cp_bv = inc(G.dma_start(bvrow_bf[0:1, :],
                                    bass.AP(bqkv_row, 2 * C, [[3 * C, 1], [1, C]])), "dg", 16)

            cp_qk = {}
            cp_wdma = {}
            cp_h = {}
            wait(V, "dq", cp_dd)
            wait(T, "dq", cp_dd)

            def ln1_half(ch, hf, first):
                wait(V, cp_xc[ch][0], cp_xc[ch][1])
                gwait(V, "xtmp")
                if first:
                    gwait(V, "mAr")
                for k in range(NK):
                    ii = V.tensor_copy(xtmp[:, k * 256:(k + 1) * 256],
                                       x1f[:, k * S + ch * CW + hf * 256: k * S + ch * CW + hf * 256 + 256])
                cp_xbf = inc(ii, "sv")
                wait(T, "sv", cp_xbf)
                if first:
                    gwait(T, "mA")
                for k in range(NK):
                    ii = T.matmul(ps_mA[0:1, hf * 256: hf * 256 + 256], ones_c[:, :],
                                  xtmp[:, k * 256:(k + 1) * 256],
                                  start=(k == 0), stop=(k == NK - 1))
                cp_sx = inc(ii, "st")
                wait(V, "st", cp_sx)
                for k in range(NK):
                    ii = V.tensor_tensor(xtmp[:, k * 256:(k + 1) * 256], xtmp[:, k * 256:(k + 1) * 256],
                                         xtmp[:, k * 256:(k + 1) * 256], ALU.mult)
                cp_xsq = inc(ii, "sv")
                wait(T, "sv", cp_xsq)
                if first:
                    gwait(T, "mB")
                for k in range(NK):
                    ii = T.matmul(ps_mB[0:1, hf * 256: hf * 256 + 256], ones_c[:, :],
                                  xtmp[:, k * 256:(k + 1) * 256],
                                  start=(k == 0), stop=(k == NK - 1))
                cp_sxx = inc(ii, "st")
                gset("xtmp", "st", cp_sxx)
                return cp_sxx

            def ln1_cast_sx(ch):
                return ln1_half(ch, 0, True)

            def ln1_sq_sxx(ch, cp_a):
                return ln1_half(ch, 1, False)

            def ln1_fin_bc(ch, cp_sxx):
                ra, rb, rc = rows[0:1, 0:512], rows[0:1, 512:1024], rows[0:1, 1024:1536]
                wait(V, "st", cp_sxx)
                V.tensor_scalar(ra, ps_mA[0:1, 0:512], 1.0 / C, None, ALU.mult)
                cw = inc(V.tensor_scalar(rc, ps_mB[0:1, 0:512], 1.0 / C, EPS, ALU.mult, ALU.add), "sv")
                wait(V, "sv", cw)
                cw = inc(V.tensor_tensor(rb, ra, ra, ALU.mult), "sv")
                wait(V, "sv", cw)
                cw = inc(V.tensor_tensor(rb, rc, rb, ALU.subtract), "sv")
                wait(V, "sv", cw)
                cp_rv = inc(V.reciprocal(rc, rb), "sv")
                wait(Sc, "sv", cp_rv)
                cp_rstd = inc(Sc.activation(rb, rc, AF.Sqrt), "ss")
                wait(T, "ss", cp_rstd)
                T.matmul(ps_mA[:, 0:512], onesr32[:, :], ra, start=True, stop=True)
                return inc(T.matmul(ps_mB[:, 0:512], onesr32[:, :], rb, start=True, stop=True), "st")

            def ln1_apply(ch, cp_bc):
                hb = (ch % 2) * NK * CW
                wait(V, "st", cp_bc)
                gwait(V, ("hTc", ch % 2))
                pairs = [(k, hf) for k in range(NK) for hf in range(2)]
                subcp = {}
                mulcp = {}

                def _sub(i):
                    k, hf = pairs[i]
                    sl = (i % 2) * 256
                    if i >= 2:
                        wait(V, "sv", mulcp[i - 2])
                    subcp[i] = inc(V.tensor_tensor(
                        t1f[:, sl:sl + 256],
                        x1f[:, k * S + ch * CW + hf * 256: k * S + ch * CW + hf * 256 + 256],
                        ps_mA[:, hf * 256:(hf + 1) * 256], ALU.subtract), "sv")

                def _mul(i, guard):
                    k, hf = pairs[i]
                    sl = (i % 2) * 256
                    if guard:
                        wait(V, "sv", subcp[i])
                    ii = V.tensor_tensor(
                        hTc[:, hb + k * CW + hf * 256: hb + k * CW + hf * 256 + 256],
                        t1f[:, sl:sl + 256], ps_mB[:, hf * 256:(hf + 1) * 256], ALU.mult)
                    mulcp[i] = inc(ii, "sv")
                    return ii

                _sub(0)
                for i in range(1, len(pairs)):
                    _sub(i)
                    _mul(i - 1, i == 1)
                ii = _mul(len(pairs) - 1, True)
                cp = mulcp[len(pairs) - 1]
                cp_h[ch] = cp
                gset("mAr", "sv", cp)
                gset("mA", "sv", cp)
                gset("mB", "sv", cp)
                return cp

            def emit_ln1_full(ch):
                a = ln1_cast_sx(ch)
                b = ln1_sq_sxx(ch, a)
                c_ = ln1_fin_bc(ch, b)
                ln1_apply(ch, c_)

            emit_ln1_full(0)

            for ch in range(NCH):
                hb = (ch % 2) * NK * CW
                for jp in range(2):
                    b = jp % 2
                    gwait(G, ("wblk", b))
                    cp_wdma[jp] = (f"w{b}", inc(G.dma_start(
                        bass.AP(wblk, b * 2048, [[4096, 128], [256, NK], [1, 256]]),
                        bass.AP(wqkvT, jp * 256, [[3 * C, 128], [128 * 3 * C, NK], [1, 256]])), f"w{b}", 16))

                wait(T, "sv", cp_h[ch])
                lnstate = {}
                for j in range(16):
                    jp, jh = j // 2, j % 2
                    b = jp % 2
                    bank = "s0a" if (j % 2 == 0) else "s0b"
                    pslice = ps_s0[:, 0:512] if (j % 2 == 0) else ps_s0[:, 512:1024]
                    if jh == 0:
                        wait(T, cp_wdma[jp][0], cp_wdma[jp][1])
                    gwait(T, bank)
                    for k in range(NK):
                        ii = T.matmul(pslice,
                                      wblk[:, b * 2048 + k * 256 + jh * 128: b * 2048 + k * 256 + jh * 128 + 128],
                                      hTc[:, hb + k * CW: hb + k * CW + CW], start=(k == 0), stop=(k == NK - 1))
                    cp_qk[j] = inc(ii, "st")
                    if jh == 1:
                        gset(("wblk", b), "st", cp_qk[j])
                        if jp + 2 < 8:
                            gwait(G, ("wblk", (jp + 2) % 2))
                            cp_wdma[jp + 2] = (f"w{(jp + 2) % 2}", inc(G.dma_start(
                                bass.AP(wblk, ((jp + 2) % 2) * 2048, [[4096, 128], [256, NK], [1, 256]]),
                                bass.AP(wqkvT, (jp + 2) * 256, [[3 * C, 128], [128 * 3 * C, NK], [1, 256]])),
                                f"w{(jp + 2) % 2}", 16))
                    tgt = QT if j < 8 else KT
                    jj = j % 8
                    wait(V, "st", cp_qk[j])
                    ii = V.tensor_scalar(tgt[:, jj * S + ch * CW: jj * S + (ch + 1) * CW],
                                         pslice, bqkv_sb[:, j:j + 1], None, ALU.add)
                    gset(bank, "sv", inc(ii, "sv"))
                    if ch + 1 < NCH:
                        if j == 5:
                            lnstate["a"] = ln1_cast_sx(ch + 1)
                        elif j == 8:
                            lnstate["b"] = ln1_sq_sxx(ch + 1, lnstate["a"])
                        elif j == 11:
                            lnstate["c"] = ln1_fin_bc(ch + 1, lnstate["b"])
                        elif j == 14:
                            ln1_apply(ch + 1, lnstate["c"])

                if ch == 0:
                    wait(T, "dg", cp_bv)
                cp_wv = {}
                for q in range(2):
                    b = q % 2
                    gwait(G, ("wvq", b))
                    cp_wv[q] = (f"v{b}", inc(G.dma_start(
                        bass.AP(wvq, b * 2048, [[4096, 128], [256, NK], [1, 256]]),
                        bass.AP(wqkvT, 2 * C + q * 256, [[3 * C, 128], [128 * 3 * C, NK], [1, 256]])),
                        f"v{b}", 16))
                for q in range(4):
                    b = q % 2
                    wait(T, cp_wv[q][0], cp_wv[q][1])
                    for tt in range(4):
                        tau = ch * 4 + tt
                        vb_ = (q * 4 + tt) % 2
                        bank = "s1a" if vb_ == 0 else "s1b"
                        pslice = ps_s1[:, 0:256] if vb_ == 0 else ps_s1[:, 512:768]
                        gwait(T, bank)
                        T.matmul(pslice, onesr_bf[:, :], bvrow_bf[0:1, q * 256:(q + 1) * 256],
                                 start=True, stop=False)
                        for k in range(NK):
                            ii = T.matmul(pslice, hTc[:, hb + k * CW + tt * 128: hb + k * CW + tt * 128 + 128],
                                          wvq[:, b * 2048 + k * 256: b * 2048 + (k + 1) * 256],
                                          start=False, stop=(k == NK - 1))
                        cp_v = inc(ii, "st")
                        gset(("wvq", b), "st", cp_v)
                        wait(V, "st", cp_v)
                        ii = V.tensor_copy(Vaug[:, tau * 1024 + 256 * q: tau * 1024 + 256 * (q + 1)],
                                           ps_s1[:, vb_ * 512: vb_ * 512 + 256])
                        gset(bank, "sv", inc(ii, "sv"))
                    if q + 2 < 4:
                        bnext = (q + 2) % 2
                        gwait(G, ("wvq", bnext))
                        cp_wv[q + 2] = (f"v{bnext}", inc(G.dma_start(
                            bass.AP(wvq, bnext * 2048, [[4096, 128], [256, NK], [1, 256]]),
                            bass.AP(wqkvT, 2 * C + (q + 2) * 256,
                                    [[3 * C, 128], [128 * 3 * C, NK], [1, 256]])), f"v{bnext}", 16))
                gset(("hTc", ch % 2), "st", cp_v)

        barrier()
        if "2" in phases:
          with contextlib.ExitStack() as p2:
            E2 = p2.enter_context
            SW2 = 896
            strip = E2(nc.sbuf_tensor("strip", [128, SW2], F32))
            etab = E2(nc.sbuf_tensor("etab", [128, S], BF16))
            rtab = E2(nc.sbuf_tensor("rtab", [128, 1544], BF16))
            fmt = E2(nc.sbuf_tensor("fmt", [128, 512], BF16))
            fpt = E2(nc.sbuf_tensor("fpt", [128, 512], BF16))
            kscale = E2(nc.sbuf_tensor("kscale", [128, S], BF16))
            qm = E2(nc.sbuf_tensor("qm", [128, 2 * 512], BF16))
            qp = E2(nc.sbuf_tensor("qp", [128, 2 * 512], BF16))
            Pbuf = E2(nc.sbuf_tensor("Pbuf", [128, 3 * 1024], BF16))
            Och = E2(nc.sbuf_tensor("Och", [128, NK * 512], BF16))
            wo_sb = E2(nc.sbuf_tensor("wo_sb", [128, 2 * 1024], BF16))
            recip_sb = E2(nc.sbuf_tensor("recip_sb", [128, 512], F32))
            ones32d = E2(nc.sbuf_tensor("ones32d", [128, 32], BF16))
            ind = E2(nc.sbuf_tensor("ind", [128, 128], F32))

            V.memset(recip_sb[:, :], 0.0)
            cw0 = inc(V.memset(ones32d[:, :], 0.0), "sv")
            wait(V, "sv", cw0)
            V.memset(ones32d[:, 0:1], 1.0)
            cw0 = inc(V.memset(ind[:, :], 0.0), "sv")
            wait(V, "sv", cw0)
            V.memset(ind[32:33, 0:64], 1.0)
            cp_rz = inc(V.memset(ind[0:1, 64:128], 1.0), "sv")
            wait(T, "sv", cp_rz)
            cp_dist = inc(Y.dma_start(strip[:, :],
                                      bass.AP(dist, 1536, [[STRIPW, 128], [1, SW2]])), "dq", 16)
            cp_lin = inc(G.dma_start(etab[:, :], lin[:, :]), "dg", 16)
            cp_linr = inc(G.dma_start(rtab[:, :], linr[:, :]), "dg", 16)
            wait(V, "dq", cp_dist)
            cp_s1 = inc(V.tensor_scalar(strip[:, :], strip[:, :], nega[:, 0:1], None, ALU.mult), "sv")
            wait(V, "dg", cp_linr)
            V.tensor_scalar(etab[:, :], etab[:, :], nega[:, 0:1], None, ALU.mult)
            cp_e1 = inc(V.tensor_scalar(rtab[:, :], rtab[:, :], nega[:, 0:1], None, ALU.mult), "sv")
            wait(Sc, "sv", cp_e1)
            cp_s2 = inc(Sc.activation(strip[:, :], strip[:, :], AF.Exp), "ss")
            Sc.activation(etab[:, :], etab[:, :], AF.Exp)
            cp_e2 = inc(Sc.activation(rtab[:, :], rtab[:, :], AF.Exp), "ss")
            wait(V, "ss", cp_e2)
            cp_strip = inc(V.tensor_scalar(strip[:, :], strip[:, :], 0.125, None, ALU.mult), "sv")
            V.tensor_scalar(fmt[:, :], etab[:, 0:512], 0.125, None, ALU.mult)
            cp_fmt = inc(V.tensor_scalar(fpt[:, :], rtab[:, 1025:1537], 0.125, None, ALU.mult), "sv")
            wait(T, "sv", cp_fmt)
            wait(Sc, "sv", cp_fmt)
            if kdbg:
                wait(G, "sv", cp_fmt)
                inc(G.dma_start(dbg_etab[:, :], etab[:, :]), "dg", 16)
                inc(G.dma_start(dbg_fmt[:, :], fmt[:, :]), "dg", 16)
                inc(G.dma_start(dbg_fpt[:, :], fpt[:, :]), "dg", 16)

            cp_wo = {}
            cp_oev = 0
            cp_smm_last = -1
            for ch in range(NCH):
                s0 = ch * CW
                lo_end = 4 * ch
                hi_beg = 4 * ch + 4
                for co in range(2):
                    b = co % 2
                    gwait(G, ("wo", b))
                    cp_wo[co] = (f"o{b}", inc(G.dma_start(
                        bass.AP(wo_sb, b * 1024, [[2048, 128], [128, NK], [1, 128]]),
                        bass.AP(woT, co * 128, [[C, 128], [128 * C, NK], [1, 128]])), f"o{b}", 16))

                def build_klo(gg):
                    return inc(V.tensor_tensor(
                        kscale[:, 0:s0], KT[:, gg * S: gg * S + s0],
                        bass.AP(rtab, 1536 - s0, [[1544, 128], [1, s0]]), ALU.mult), "sv")

                def build_khi(gg):
                    n = S - (s0 + CW)
                    return inc(V.tensor_tensor(
                        kscale[:, s0 + CW: S], KT[:, gg * S + s0 + CW: (gg + 1) * S],
                        bass.AP(etab, 1, [[S, 128], [1, n]]), ALU.mult), "sv")

                def build_q(gg):
                    sl = gg % 2
                    a = b_ = None
                    if lo_end > 0:
                        a = inc(V.tensor_tensor(qm[:, sl * 512:(sl + 1) * 512],
                                                QT[:, gg * S + s0: gg * S + s0 + CW],
                                                fmt[:, :], ALU.mult), "sv")
                    if hi_beg < NT:
                        b_ = inc(V.tensor_tensor(qp[:, sl * 512:(sl + 1) * 512],
                                                 QT[:, gg * S + s0: gg * S + s0 + CW],
                                                 fpt[:, :], ALU.mult), "sv")
                    return b_ or a

                if lo_end > 0:
                    wait(V, "st", cp_smm_last)
                    cp_klo = build_klo(0)
                else:
                    cp_klo = 0
                gwait(V, ("q", 0))
                if ch == 0:
                    wait(V, "sv", cp_fmt)
                cp_q = build_q(0)
                cp_khi = 0
                pending = None
                pend_cpb = None
                pend_cpc = [0]

                for g in range(NP):
                    cp_exp = {}
                    cp_pre = {}
                    cp_av = {}
                    ob = "oA" if g % 2 == 0 else "oB"
                    mb = "mA" if g % 2 == 0 else "mB"
                    pm = ps_mA if g % 2 == 0 else ps_mB
                    oc0 = 0 if g % 2 == 0 else 512
                    qsl = g % 2
                    cp_smm_lo = 0
                    cp_smm = 0
                    prev_smm_last = cp_smm_last

                    def emit_smm(tau):
                        nonlocal cp_smm_lo, cp_smm, cp_klo, cp_khi, cp_q
                        sl = tau % 2
                        pst = ps_s0 if sl == 0 else ps_s1
                        bko = ("s0a", "s0b") if sl == 0 else ("s1a", "s1b")
                        crossing = lo_end <= tau < hi_beg
                        gwait(T, bko[0], bko[1])
                        if tau < lo_end:
                            if tau == 0:
                                wait(T, "sv", max(cp_klo, cp_q))
                            lh, rh = kscale, qm
                            lo_, ro_ = tau * 128, qsl * 512
                        elif tau >= hi_beg:
                            if tau == hi_beg:
                                wait(T, "sv", max(cp_khi, cp_q))
                            lh, rh = kscale, qp
                            lo_, ro_ = tau * 128, qsl * 512
                        else:
                            lh, rh = KT, QT
                            lo_, ro_ = g * S + tau * 128, g * S + s0
                        T.matmul(pst[:, 0:512], lh[0:64, lo_: lo_ + 128], rh[0:64, ro_: ro_ + CW],
                                 start=True, stop=True, tile_position=(0, 0))
                        cp_smm = inc(T.matmul(pst[:, 512:1024], lh[64:128, lo_: lo_ + 128],
                                              rh[64:128, ro_: ro_ + CW],
                                              start=True, stop=True, tile_position=(64, 0)), "st")
                        if tau == lo_end - 1:
                            cp_smm_lo = cp_smm
                        if crossing:
                            off = 384 - (tau * 128 - s0)
                            wait(V, "st", cp_smm)
                            cp_mult = inc(V.tensor_tensor(
                                pst[:, 0:1024], pst[:, 0:1024],
                                bass.AP(strip, off, [[SW2, 128], [0, 2], [1, 512]]), ALU.mult), "sv")
                            if tau == lo_end:
                                if hi_beg < NT:
                                    wait(V, "st", prev_smm_last)
                                    cp_khi = build_khi(g)
                            elif tau == lo_end + 1 and g + 1 < NP:
                                if lo_end > 0:
                                    wait(V, "st", cp_smm_lo)
                                    cp_klo = build_klo(g + 1)
                            elif tau == lo_end + 2 and g + 1 < NP:
                                gwait(V, ("q", (g + 1) % 2))
                                cp_q = build_q(g + 1)
                            cp_pre[tau] = ("sv", cp_mult)
                        else:
                            cp_pre[tau] = ("st", cp_smm)

                    def emit_exp(tau):
                        sl = tau % 2
                        slp = tau % 3
                        pst = ps_s0 if sl == 0 else ps_s1
                        bko = ("s0a", "s0b") if sl == 0 else ("s1a", "s1b")
                        nm_, v_ = cp_pre[tau]
                        wait(Sc, nm_, v_)
                        gwait(Sc, ("P", slp))
                        cp_exp[tau] = inc(Sc.activation(Pbuf[:, slp * 1024:(slp + 1) * 1024],
                                                        pst[:, 0:1024], AF.Exp), "ss")
                        gset(bko[0], "ss", cp_exp[tau])
                        gset(bko[1], "ss", cp_exp[tau])

                    def emit_av(tau, covered=False):
                        slp = tau % 3
                        if not covered:
                            wait(T, "ss", cp_exp[tau])
                        if tau == 0:
                            gwait(T, ob, "mA")
                        T.matmul(ps_o[0:64, oc0:oc0 + 512],
                                 bass.AP(Vaug, tau * 1024 + 64 * (2 * g), [[NT * 1024, 128], [1, 64]]),
                                 Pbuf[:, slp * 1024: slp * 1024 + 512],
                                 start=(tau == 0), stop=(tau == NT - 1),
                                 skip_group_check=True)
                        T.matmul(ps_o[64:128, oc0:oc0 + 512],
                                 bass.AP(Vaug, tau * 1024 + 64 * (2 * g + 1), [[NT * 1024, 128], [1, 64]]),
                                 Pbuf[:, slp * 1024 + 512: (slp + 1) * 1024],
                                 start=(tau == 0), stop=(tau == NT - 1), tile_position=(0, 64),
                                 skip_group_check=True)
                        T.matmul(pm[32:64, 0:512], ones32d[:, :],
                                 Pbuf[:, slp * 1024: slp * 1024 + 512],
                                 start=(tau == 0), stop=(tau == NT - 1), tile_position=(0, 32),
                                 skip_group_check=True)
                        cp_av[tau] = inc(T.matmul(pm[0:32, 0:512], ones32d[:, :],
                                                  Pbuf[:, slp * 1024 + 512: (slp + 1) * 1024],
                                                  start=(tau == 0), stop=(tau == NT - 1),
                                                  tile_position=(0, 0),
                                                  skip_group_check=True), "st")
                        gset(("P", slp), "st", cp_av[tau])

                    def emit_tail_recip():
                        wait(V, "st", cp_av[NT - 1])
                        gwait(V, "recip")
                        wait(V, "sv", cp_oev)
                        V.reciprocal(recip_sb[32:33, 0:512], pm[32:33, 0:512])
                        return inc(V.reciprocal(recip_sb[0:1, 0:512], pm[0:1, 0:512]), "sv")

                    def emit_tail_bcast(pend):
                        pg, ppm, prc = pend
                        wait(T, "sv", prc)
                        cpb = inc(T.matmul(ppm[:, 0:512], ind[:, :],
                                           recip_sb[:, 0:512], start=True, stop=True), "st")
                        gset("recip", "st", cpb)
                        return cpb

                    def emit_tail_copy(pend, cpb):
                        pg, ppm, prc = pend
                        pmb = "mA" if pg % 2 == 0 else "mB"
                        wait(V, "st", cpb)
                        cpc = inc(V.tensor_copy(recip_sb[:, :], ppm[:, 0:512]), "sv")
                        gset(pmb, "sv", cpc)
                        pend_cpc[0] = cpc
                        return cpc

                    def emit_tail_evict(pend):
                        pg, ppm, prc = pend
                        poc0 = 0 if pg % 2 == 0 else 512
                        pob = "oA" if pg % 2 == 0 else "oB"
                        gwait(V, ("och", pg))
                        wait(V, "sv", pend_cpc[0])
                        cpo = inc(V.tensor_tensor(Och[:, pg * 512:(pg + 1) * 512],
                                                  ps_o[:, poc0:poc0 + 512],
                                                  recip_sb[:, :], ALU.mult), "sv")
                        gset(pob, "sv", cpo)
                        if kdbg and ch == 0 and pg == 1:
                            wait(G, "sv", cpo)
                            inc(G.dma_start(dbg_p2[:, :512], recip_sb[:, :]), "dg", 16)
                            inc(G.dma_start(dbg_och[:, :], Och[:, :]), "dg", 16)
                            inc(G.dma_start(dbg_p[:, :], Pbuf[:, :]), "dg", 16)
                        return cpo

                    for tau in range(NT):
                        emit_smm(tau)
                        if tau >= 2:
                            emit_av(tau - 2, covered=True)
                        emit_exp(tau)
                        if pending is not None:
                            if tau == 1:
                                pend_cpb = emit_tail_bcast(pending)
                            elif tau == 2:
                                emit_tail_copy(pending, pend_cpb)
                            elif tau == 3:
                                cp_oev = emit_tail_evict(pending)
                                pending = None
                    emit_av(NT - 2)
                    emit_av(NT - 1)
                    cp_smm_last = cp_smm
                    gset(("q", qsl), "st", cp_smm_last)
                    cp_rc = emit_tail_recip()
                    if kdbg and ch == 0 and g == 1:
                        wait(G, "sv", cp_rc)
                        cp_dk = inc(G.dma_start(dbg_p2[:, 512:1024], recip_sb[:, :]), "dg", 16)
                        wait(T, "dg", cp_dk)
                        inc(G.dma_start(dbg_p[:, :], Pbuf[:, :]), "dg", 16)
                    pending = (g, pm, cp_rc)
                    if g == NP - 1:
                        pend_cpb = emit_tail_bcast(pending)
                        emit_tail_copy(pending, pend_cpb)
                        cp_oev = emit_tail_evict(pending)
                        pending = None
                    if kdbg and ch == 0 and g == 0:
                        wait(G, "sv", cp_rc)
                        inc(G.dma_start(dbg_ks[:, :], kscale[:, :]), "dg", 16)
                        inc(G.dma_start(dbg_qp[:, :], qp[:, :]), "dg", 16)

                for co in range(8):
                    b = co % 2
                    bank = "s0a" if b == 0 else "s0b"
                    pslice = ps_s0[:, 0:512] if b == 0 else ps_s0[:, 512:1024]
                    wait(T, cp_wo[co][0], cp_wo[co][1])
                    wait(T, "sv", cp_oev)
                    gwait(T, bank)
                    for k in range(NK):
                        ii = T.matmul(pslice, wo_sb[:, b * 1024 + k * 128: b * 1024 + (k + 1) * 128],
                                      Och[:, k * 512:(k + 1) * 512], start=(k == 0), stop=(k == NK - 1))
                    cp_proj = inc(ii, "st")
                    gset(("wo", b), "st", cp_proj)
                    if co + 2 < 8:
                        gwait(G, ("wo", (co + 2) % 2))
                        cp_wo[co + 2] = (f"o{(co + 2) % 2}", inc(G.dma_start(
                            bass.AP(wo_sb, ((co + 2) % 2) * 1024, [[2048, 128], [128, NK], [1, 128]]),
                            bass.AP(woT, (co + 2) * 128, [[C, 128], [128 * C, NK], [1, 128]])),
                            f"o{(co + 2) % 2}", 16))
                    wait(V, "st", cp_proj)
                    ii = V.scalar_tensor_tensor(
                        x1f[:, co * S + ch * CW: co * S + (ch + 1) * CW], pslice, bo_sb[:, co:co + 1],
                        x1f[:, co * S + ch * CW: co * S + (ch + 1) * CW], ALU.add, ALU.add)
                    cp_x1 = inc(ii, "sv")
                    gset(bank, "sv", cp_x1)
                for g in range(NP):
                    gset(("och", g), "st", cp_proj)

        barrier()
        kqv.close()
        if "3" not in phases:
            cp_dbg = inc(Y.dma_start(
                bass.AP(yT, 0, [[S, 128], [128 * S, NK], [1, S]]),
                bass.AP(x1f, 0, [[NK * S, 128], [S, NK], [1, S]])), "dy", 16)
        if "3" in phases:
          with contextlib.ExitStack() as p3:
            E3 = p3.enter_context
            h2b = E3(nc.sbuf_tensor("h2b", [128, NK * 2052], BF16))
            wr_sb = E3(nc.sbuf_tensor("wr_sb", [128, NK * 1024], BF16))
            wk_sb = E3(nc.sbuf_tensor("wk_sb", [128, NK * 1024], BF16))
            wv2_sb = E3(nc.sbuf_tensor("wv2_sb", [128, NK * 1024], BF16))
            xrb = E3(nc.sbuf_tensor("xrb", [128, 2 * NK * CW], BF16))
            xkb = E3(nc.sbuf_tensor("xkb", [128, 2 * NK * CW], BF16))
            kact = E3(nc.sbuf_tensor("kact", [128, NK * CW], BF16))
            xtmp2 = E3(nc.sbuf_tensor("xtmp2", [128, NK * 256], BF16))
            t1g = E3(nc.sbuf_tensor("t1g", [128, 512], F32))
            dtmp = E3(nc.sbuf_tensor("dtmp", [128, 2 * CW], BF16))
            krelu = E3(nc.sbuf_tensor("krelu", [128, 2 * CW], BF16))
            rblk = E3(nc.sbuf_tensor("rblk", [128, 2 * CW], F32))
            t2f = E3(nc.sbuf_tensor("t2f", [128, CW], F32))
            rows = E3(nc.sbuf_tensor("rows3", [1, 1536], F32))

            cp_wk = inc(G.dma_start(
                bass.AP(wk_sb, 0, [[NK * 1024, 128], [1024, NK], [1, 1024]]),
                bass.AP(wkT, 0, [[C, 128], [128 * C, NK], [1, 1024]])), "fk", 16)
            cp_wr = inc(G.dma_start(
                bass.AP(wr_sb, 0, [[NK * 1024, 128], [1024, NK], [1, 1024]]),
                bass.AP(wrT, 0, [[C, 128], [128 * C, NK], [1, 1024]])), "fr", 16)
            cp_wv2 = inc(G.dma_start(
                bass.AP(wv2_sb, 0, [[NK * 1024, 128], [1024, NK], [1, 1024]]),
                bass.AP(wvT2, 0, [[C, 128], [128 * C, NK], [1, 1024]])), "fv", 16)

            cp_hb = inc(V.memset(bass.AP(h2b, 3, [[NK * 2052, 128], [2052, NK], [1, 1]]), 0.0), "sv")

            for ch in range(NCH):
                for hf in range(2):
                    gwait(V, "xtmp2")
                    for k in range(NK):
                        ii = V.tensor_copy(xtmp2[:, k * 256:(k + 1) * 256],
                                           x1f[:, k * S + ch * CW + hf * 256: k * S + ch * CW + hf * 256 + 256])
                    cp_xbf = inc(ii, "sv")
                    wait(T, "sv", cp_xbf)
                    if hf == 0:
                        gwait(T, "mA")
                    for k in range(NK):
                        ii = T.matmul(ps_mA[0:1, hf * 256: hf * 256 + 256], ones_c[:, :],
                                      xtmp2[:, k * 256:(k + 1) * 256], start=(k == 0), stop=(k == NK - 1))
                    cp_sx = inc(ii, "st")
                    wait(V, "st", cp_sx)
                    for k in range(NK):
                        ii = V.tensor_tensor(xtmp2[:, k * 256:(k + 1) * 256], xtmp2[:, k * 256:(k + 1) * 256],
                                             xtmp2[:, k * 256:(k + 1) * 256], ALU.mult)
                    cp_xsq = inc(ii, "sv")
                    wait(T, "sv", cp_xsq)
                    if hf == 0:
                        gwait(T, "mB")
                    for k in range(NK):
                        ii = T.matmul(ps_mB[0:1, hf * 256: hf * 256 + 256], ones_c[:, :],
                                      xtmp2[:, k * 256:(k + 1) * 256], start=(k == 0), stop=(k == NK - 1))
                    cp_sxx = inc(ii, "st")
                    gset("xtmp2", "st", cp_sxx)
                ra, rb, rc = rows[0:1, 0:512], rows[0:1, 512:1024], rows[0:1, 1024:1536]
                wait(V, "st", cp_sxx)
                V.tensor_scalar(ra, ps_mA[0:1, 0:512], 1.0 / C, None, ALU.mult)
                cw = inc(V.tensor_scalar(rc, ps_mB[0:1, 0:512], 1.0 / C, EPS, ALU.mult, ALU.add), "sv")
                wait(V, "sv", cw)
                cw = inc(V.tensor_tensor(rb, ra, ra, ALU.mult), "sv")
                wait(V, "sv", cw)
                cw = inc(V.tensor_tensor(rb, rc, rb, ALU.subtract), "sv")
                wait(V, "sv", cw)
                cp_rv = inc(V.reciprocal(rc, rb), "sv")
                wait(Sc, "sv", cp_rv)
                cp_rstd = inc(Sc.activation(rb, rc, AF.Sqrt), "ss")
                wait(T, "ss", cp_rstd)
                T.matmul(ps_mA[:, 0:512], onesr32[:, :], ra, start=True, stop=True)
                cp_bc = inc(T.matmul(ps_mB[:, 0:512], onesr32[:, :], rb, start=True, stop=True), "st")
                wait(V, "st", cp_bc)
                if ch == 0:
                    wait(V, "sv", cp_hb)
                pairs = [(k, hf) for k in range(NK) for hf in range(2)]
                subcp = {}
                mulcp = {}

                def _sub(i):
                    k, hf = pairs[i]
                    sl = (i % 2) * 256
                    if i >= 2:
                        wait(V, "sv", mulcp[i - 2])
                    subcp[i] = inc(V.tensor_tensor(
                        t1g[:, sl:sl + 256],
                        x1f[:, k * S + ch * CW + hf * 256: k * S + ch * CW + hf * 256 + 256],
                        ps_mA[:, hf * 256: hf * 256 + 256], ALU.subtract), "sv")

                def _mul(i, guard):
                    k, hf = pairs[i]
                    sl = (i % 2) * 256
                    if guard:
                        wait(V, "sv", subcp[i])
                    ii = V.tensor_tensor(
                        h2b[:, k * 2052 + 4 + ch * CW + hf * 256: k * 2052 + 4 + ch * CW + hf * 256 + 256],
                        t1g[:, sl:sl + 256], ps_mB[:, hf * 256: hf * 256 + 256], ALU.mult)
                    mulcp[i] = inc(ii, "sv")
                    return ii

                _sub(0)
                for i in range(1, len(pairs)):
                    _sub(i)
                    _mul(i - 1, i == 1)
                ii = _mul(len(pairs) - 1, True)
                cp_h2 = mulcp[len(pairs) - 1]
                gset("mA", "sv", cp_h2)
                gset("mB", "sv", cp_h2)

            wait(T, "fk", cp_wk)
            cp_blend = {}

            def emit_blends(ch):
                sb_ = (ch % 2) * NK * CW
                gwait(V, ("xrb", ch % 2), ("xkb", ch % 2))
                for k in range(NK):
                    cur = h2b[:, k * 2052 + 4 + ch * CW: k * 2052 + 4 + (ch + 1) * CW]
                    prev = h2b[:, k * 2052 + 3 + ch * CW: k * 2052 + 3 + (ch + 1) * CW]
                    dsl = (k % 2) * CW
                    cw_ = inc(V.tensor_tensor(dtmp[:, dsl:dsl + CW], cur, prev, ALU.subtract), "sv")
                    wait(V, "sv", cw_)
                    V.scalar_tensor_tensor(xrb[:, sb_ + k * CW: sb_ + (k + 1) * CW], dtmp[:, dsl:dsl + CW],
                                           mur_sb[:, k:k + 1], prev, ALU.mult, ALU.add)
                    ii = V.scalar_tensor_tensor(xkb[:, sb_ + k * CW: sb_ + (k + 1) * CW], dtmp[:, dsl:dsl + CW],
                                                muk_sb[:, k:k + 1], prev, ALU.mult, ALU.add)
                cp_blend[ch] = inc(ii, "sv")

            emit_blends(0)
            for ch in range(NCH):
                sb_ = (ch % 2) * NK * CW
                gwait(V, "kact")
                wait(T, "sv", cp_blend[ch])
                for j in range(NK):
                    b = j % 2
                    bank = "s0a" if b == 0 else "s0b"
                    pslice = ps_s0[:, 0:512] if b == 0 else ps_s0[:, 512:1024]
                    gwait(T, bank)
                    for k in range(NK):
                        ii = T.matmul(pslice, wk_sb[:, k * 1024 + j * 128: k * 1024 + (j + 1) * 128],
                                      xkb[:, sb_ + k * CW: sb_ + (k + 1) * CW],
                                      start=(k == 0), stop=(k == NK - 1))
                    cp_k = inc(ii, "st")
                    wait(V, "st", cp_k)
                    ksl = (j % 2) * CW
                    cw_ = inc(V.tensor_scalar(krelu[:, ksl:ksl + CW], pslice, 0.0, None, ALU.max), "sv")
                    wait(V, "sv", cw_)
                    ii = V.tensor_tensor(kact[:, j * CW:(j + 1) * CW], krelu[:, ksl:ksl + CW],
                                         krelu[:, ksl:ksl + CW], ALU.mult)
                    gset(bank, "sv", inc(ii, "sv"))
                cp_kact = cnt["sv"]
                if ch + 1 < NCH:
                    emit_blends(ch + 1)
                wait(T, "sv", cp_kact)
                if ch == 0:
                    wait(T, "fr", cp_wr)
                    wait(T, "fv", cp_wv2)

                for j in range(NK):
                    rb_ = j % 2
                    bank_r = "s1a" if rb_ == 0 else "s1b"
                    psr = ps_s1[:, 0:512] if rb_ == 0 else ps_s1[:, 512:1024]
                    bank_v = "oA" if rb_ == 0 else "oB"
                    psv = ps_o[:, 0:512] if rb_ == 0 else ps_o[:, 512:1024]
                    gwait(T, bank_r)
                    for k in range(NK):
                        ii = T.matmul(psr, wr_sb[:, k * 1024 + j * 128: k * 1024 + (j + 1) * 128],
                                      xrb[:, sb_ + k * CW: sb_ + (k + 1) * CW],
                                      start=(k == 0), stop=(k == NK - 1))
                    cp_r = inc(ii, "st")
                    gwait(T, bank_v)
                    for k in range(NK):
                        ii = T.matmul(psv, wv2_sb[:, k * 1024 + j * 128: k * 1024 + (j + 1) * 128],
                                      kact[:, k * CW:(k + 1) * CW], start=(k == 0), stop=(k == NK - 1))
                    cp_v2 = inc(ii, "st")
                    wait(Sc, "st", cp_r)
                    gwait(Sc, ("rblk", rb_))
                    cp_sig = inc(Sc.activation(rblk[:, rb_ * CW:(rb_ + 1) * CW], psr, AF.Sigmoid), "ss")
                    gset(bank_r, "ss", cp_sig)
                    wait(V, "ss", cp_sig)
                    wait(V, "st", cp_v2)
                    if j > 0:
                        wait(V, "sv", cp_y)
                    cw_ = inc(V.tensor_tensor(t2f[:, :], rblk[:, rb_ * CW:(rb_ + 1) * CW], psv, ALU.mult), "sv")
                    wait(V, "sv", cw_)
                    ii = V.tensor_tensor(x1f[:, j * S + ch * CW: j * S + (ch + 1) * CW],
                                         x1f[:, j * S + ch * CW: j * S + (ch + 1) * CW],
                                         t2f[:, :], ALU.add)
                    cp_y = inc(ii, "sv")
                    gset(bank_v, "sv", cp_y)
                    gset(("rblk", rb_), "sv", cp_y)
                gset(("xkb", ch % 2), "st", cp_k)
                gset(("xrb", ch % 2), "st", cp_r)
                gset("kact", "st", cp_v2)
                wait(Y, "sv", cp_y)
                inc(Y.dma_start(
                    bass.AP(yT, ch * CW, [[S, 128], [128 * S, NK], [1, CW]]),
                    bass.AP(x1f, ch * CW, [[NK * S, 128], [S, NK], [1, CW]])), "dy", 16)

        Y.wait_ge(sems["dy"], cnt["dy"])

    ctx.close()
    return nc


def kernel(L0_attn_x, L0_attn_Wqkv_w, L0_attn_Wqkv_b, L0_attn_out_w, L0_attn_out_b,
           L0_attn_distance_decay, L0_ffn_W_r, L0_ffn_W_k, L0_ffn_W_v, L0_ffn_mu_r, L0_ffn_mu_k):
    x = np.asarray(L0_attn_x, dtype=np.float32)
    wqkvT = np.ascontiguousarray(np.asarray(L0_attn_Wqkv_w, np.float32).T)
    bq = np.asarray(L0_attn_Wqkv_b, np.float32)
    bqkv = np.ascontiguousarray(bq.reshape(24, 128).T)
    bqkv_row = np.ascontiguousarray(bq.reshape(1, 3 * C))
    woT = np.ascontiguousarray(np.asarray(L0_attn_out_w, np.float32).T)
    bo = np.ascontiguousarray(np.asarray(L0_attn_out_b, np.float32).reshape(8, 128).T)
    ddv = np.asarray(L0_attn_distance_decay, np.float32).reshape(1, 1)
    wrT = np.ascontiguousarray(np.asarray(L0_ffn_W_r, np.float32).T)
    wkT = np.ascontiguousarray(np.asarray(L0_ffn_W_k, np.float32).T)
    wvT2 = np.ascontiguousarray(np.asarray(L0_ffn_W_v, np.float32).T)
    mur = np.ascontiguousarray(np.asarray(L0_ffn_mu_r, np.float32).reshape(8, 128).T)
    muk = np.ascontiguousarray(np.asarray(L0_ffn_mu_k, np.float32).reshape(8, 128).T)
    p = np.arange(128, dtype=np.float32)[:, None]
    u = np.arange(STRIPW, dtype=np.float32)[None, :]
    dist = np.abs(p - u + U0).astype(np.float32)
    lin = np.broadcast_to(np.arange(S, dtype=np.float32)[None, :], (128, S)).copy()
    lr = np.maximum(1536.0 - np.arange(1544, dtype=np.float32), 0.0)
    linr = np.broadcast_to(lr[None, :], (128, 1544)).copy()

    phases = os.environ.get("KPHASES", "123")
    if phases not in _CACHED:
        _CACHED[phases] = build_nc(phases)
    nc = _CACHED[phases]

    shared = dict(wqkvT=wqkvT, bqkv=bqkv, bqkv_row=bqkv_row, woT=woT, bo=bo, ddv=ddv,
                  wrT=wrT, wkT=wkT, wvT2=wvT2, mur=mur, muk=muk, dist=dist, lin=lin, linr=linr)
    in_maps = []
    for b in range(8):
        m = dict(shared)
        m["xT"] = np.ascontiguousarray(x[b].T)
        in_maps.append(m)

    res = run_bass_kernel_spmd(nc, in_maps, core_ids=list(range(8)))
    out = np.stack([np.ascontiguousarray(r["yT"].T) for r in res.results], axis=0)
    return out.astype(np.float32)


# version 28
# speedup vs baseline: 1.0006x; 1.0006x over previous
import contextlib
import os
import numpy as np
import concourse.bass as bass
import concourse.mybir as mybir
from concourse.bass_utils import run_bass_kernel_spmd

F32 = mybir.dt.float32
BF16 = mybir.dt.bfloat16
AF = mybir.ActivationFunctionType
ALU = mybir.AluOpType

S = 2048
C = 1024
NH = 16
HD = 64
NP = 8
NCH = 4
CW = 512
NT = 16
NK = 8
EPS = 1e-5
U0 = 1920
STRIPW = 3968

_CACHED = {}


def build_nc(phases="123"):
    p1parts = os.environ.get("KP1", "all")
    p1ch = int(os.environ.get("KPCH", "4"))
    statlvl = int(os.environ.get("KPSTAT", "5"))
    nc = bass.Bass("TRN2", target_bir_lowering=False)

    xT = nc.dram_tensor("xT", [C, S], F32, kind="ExternalInput")
    wqkvT = nc.dram_tensor("wqkvT", [C, 3 * C], F32, kind="ExternalInput")
    bqkv = nc.dram_tensor("bqkv", [128, 24], F32, kind="ExternalInput")
    bqkv_row = nc.dram_tensor("bqkv_row", [1, 3 * C], F32, kind="ExternalInput")
    woT = nc.dram_tensor("woT", [C, C], F32, kind="ExternalInput")
    bo = nc.dram_tensor("bo", [128, 8], F32, kind="ExternalInput")
    ddv = nc.dram_tensor("ddv", [1, 1], F32, kind="ExternalInput")
    wrT = nc.dram_tensor("wrT", [C, C], F32, kind="ExternalInput")
    wkT = nc.dram_tensor("wkT", [C, C], F32, kind="ExternalInput")
    wvT2 = nc.dram_tensor("wvT2", [C, C], F32, kind="ExternalInput")
    mur = nc.dram_tensor("mur", [128, 8], F32, kind="ExternalInput")
    muk = nc.dram_tensor("muk", [128, 8], F32, kind="ExternalInput")
    dist = nc.dram_tensor("dist", [128, STRIPW], F32, kind="ExternalInput")
    lin = nc.dram_tensor("lin", [128, S], F32, kind="ExternalInput")
    linr = nc.dram_tensor("linr", [128, 1544], F32, kind="ExternalInput")
    kdbg = os.environ.get("KDBG", "0") == "1"
    if kdbg:
        dbg_etab = nc.dram_tensor("dbg_etab", [128, S], F32, kind="ExternalOutput")
        dbg_fmt = nc.dram_tensor("dbg_fmt", [128, 512], F32, kind="ExternalOutput")
        dbg_fpt = nc.dram_tensor("dbg_fpt", [128, 512], F32, kind="ExternalOutput")
        dbg_ks = nc.dram_tensor("dbg_ks", [128, S], F32, kind="ExternalOutput")
        dbg_qp = nc.dram_tensor("dbg_qp", [128, 1024], F32, kind="ExternalOutput")
        dbg_p = nc.dram_tensor("dbg_p", [128, 2048], F32, kind="ExternalOutput")
        dbg_p2 = nc.dram_tensor("dbg_p2", [128, 2048], F32, kind="ExternalOutput")
        dbg_och = nc.dram_tensor("dbg_och", [128, 4096], F32, kind="ExternalOutput")
    yT = nc.dram_tensor("yT", [C, S], F32, kind="ExternalOutput")

    ctx = contextlib.ExitStack()
    E = ctx.enter_context

    sems = {}
    cnt = {}
    for nm in ("dq", "dg", "dy", "st", "sv", "ss", "w0", "w1", "v0", "v1", "o0", "o1", "dx0", "dx1", "dx2", "dx3", "fk", "fr", "fv"):
        sems[nm] = E(nc.semaphore(nm))
        cnt[nm] = 0

    def inc(instr, nm, k=1):
        instr.then_inc(sems[nm], k)
        cnt[nm] += k
        return cnt[nm]

    waited = {}

    def wait(eng, nm, val):
        if val > 0 and waited.get((id(eng), nm), 0) < val:
            eng.wait_ge(sems[nm], val)
            waited[(id(eng), nm)] = val

    guard = {}

    def gwait(eng, *keys):
        for k in keys:
            if k in guard:
                wait(eng, *guard[k])

    def gset(key, nm, val):
        guard[key] = (nm, val)

    def barrier():
        vals = {nm: cnt[nm] for nm in ("st", "sv", "ss", "dq", "dg", "w0", "w1", "v0", "v1",
                                        "o0", "o1", "dx0", "dx1", "dx2", "dx3", "fk", "fr", "fv")}
        for eng in (nc.tensor, nc.vector, nc.scalar, nc.sync, nc.gpsimd):
            for nm, v in vals.items():
                wait(eng, nm, v)

    x1f = E(nc.sbuf_tensor("x1f", [128, NK * S], F32))
    bqkv_sb = E(nc.sbuf_tensor("bqkv_sb", [128, 24], F32))
    bo_sb = E(nc.sbuf_tensor("bo_sb", [128, 8], F32))
    mur_sb = E(nc.sbuf_tensor("mur_sb", [128, 8], F32))
    muk_sb = E(nc.sbuf_tensor("muk_sb", [128, 8], F32))
    dd_sb = E(nc.sbuf_tensor("dd_sb", [1, 1], F32))
    absd = E(nc.sbuf_tensor("absd", [1, 1], F32))
    nega = E(nc.sbuf_tensor("nega", [128, 1], F32))
    ones_c = E(nc.sbuf_tensor("ones_c", [128, 1], BF16))
    onesr_bf = E(nc.sbuf_tensor("onesr_bf", [1, 128], BF16))
    onesr32 = E(nc.sbuf_tensor("onesr32", [1, 128], F32))

    ps_s0 = E(nc.psum_tensor("ps_s0", [128, 1024], F32))
    ps_s1 = E(nc.psum_tensor("ps_s1", [128, 1024], F32))
    ps_o = E(nc.psum_tensor("ps_o", [128, 1024], F32))
    ps_mA = E(nc.psum_tensor("ps_mA", [128, 512], F32))
    ps_mB = E(nc.psum_tensor("ps_mB", [128, 512], F32))

    blk = E(nc.Block())

    @blk.sync
    def _(sync):
        T, V, Sc, Y, G = nc.tensor, nc.vector, nc.scalar, nc.sync, nc.gpsimd

        cp_xc = {}
        for xc in range(NCH):
            cp_xc[xc] = (f"dx{xc}", inc(Y.dma_start(
                bass.AP(x1f, xc * CW, [[NK * S, 128], [S, NK], [1, CW]]),
                bass.AP(xT, xc * CW, [[S, 128], [128 * S, NK], [1, CW]])), f"dx{xc}", 16))
        inc(Y.dma_start(bqkv_sb[:, :], bqkv[:, :]), "dq", 16)
        inc(Y.dma_start(bo_sb[:, :], bo[:, :]), "dq", 16)
        inc(Y.dma_start(mur_sb[:, :], mur[:, :]), "dq", 16)
        inc(Y.dma_start(muk_sb[:, :], muk[:, :]), "dq", 16)
        cp_dd = inc(Y.dma_start(dd_sb[:, :], ddv[:, :]), "dq", 16)

        V.memset(ones_c[:, :], 1.0)
        V.memset(onesr_bf[:, :], 1.0)
        cp_vones = inc(V.memset(onesr32[:, :], 1.0), "sv")

        wait(Sc, "dq", cp_dd)
        cp_abs = inc(Sc.activation(absd[:, :], dd_sb[:, :], AF.Abs), "ss")
        wait(T, "ss", cp_abs)
        wait(T, "sv", cp_vones)
        cp_negb = inc(T.matmul(ps_mA[0:128, 0:1], onesr32[:, :], absd[:, :], start=True, stop=True), "st")
        wait(V, "st", cp_negb)
        cp_nega = inc(V.tensor_scalar(nega[:, :], ps_mA[:, 0:1], -1.0, None, ALU.mult), "sv")
        gset("mA", "sv", cp_nega)

        kqv = contextlib.ExitStack()
        KT = kqv.enter_context(nc.sbuf_tensor("KT", [128, NK * S], BF16))
        QT = kqv.enter_context(nc.sbuf_tensor("QT", [128, NK * S], BF16))
        Vaug = kqv.enter_context(nc.sbuf_tensor("Vaug", [128, NT * 1024], BF16))

        with contextlib.ExitStack() as p1:
            E1 = p1.enter_context
            hTc = E1(nc.sbuf_tensor("hTc", [128, 2 * NK * CW], BF16))
            xtmp = E1(nc.sbuf_tensor("xtmp", [128, NK * 256], BF16))
            t1f = E1(nc.sbuf_tensor("t1f", [128, 512], F32))
            wblk = E1(nc.sbuf_tensor("wblk", [128, 2 * 2048], BF16))
            wvq = E1(nc.sbuf_tensor("wvq", [128, 2 * 2048], BF16))
            bvrow_bf = E1(nc.sbuf_tensor("bvrow_bf", [1, C], BF16))
            rows = E1(nc.sbuf_tensor("rows", [1, 1536], F32))

            cp_bv = inc(G.dma_start(bvrow_bf[0:1, :],
                                    bass.AP(bqkv_row, 2 * C, [[3 * C, 1], [1, C]])), "dg", 16)

            cp_qk = {}
            cp_wdma = {}
            cp_h = {}
            wait(V, "dq", cp_dd)
            wait(T, "dq", cp_dd)

            def ln1_half(ch, hf, first):
                wait(V, cp_xc[ch][0], cp_xc[ch][1])
                gwait(V, "xtmp")
                if first:
                    gwait(V, "mAr")
                for k in range(NK):
                    ii = V.tensor_copy(xtmp[:, k * 256:(k + 1) * 256],
                                       x1f[:, k * S + ch * CW + hf * 256: k * S + ch * CW + hf * 256 + 256])
                cp_xbf = inc(ii, "sv")
                wait(T, "sv", cp_xbf)
                if first:
                    gwait(T, "mA")
                for k in range(NK):
                    ii = T.matmul(ps_mA[0:1, hf * 256: hf * 256 + 256], ones_c[:, :],
                                  xtmp[:, k * 256:(k + 1) * 256],
                                  start=(k == 0), stop=(k == NK - 1))
                cp_sx = inc(ii, "st")
                wait(V, "st", cp_sx)
                for k in range(NK):
                    ii = V.tensor_tensor(xtmp[:, k * 256:(k + 1) * 256], xtmp[:, k * 256:(k + 1) * 256],
                                         xtmp[:, k * 256:(k + 1) * 256], ALU.mult)
                cp_xsq = inc(ii, "sv")
                wait(T, "sv", cp_xsq)
                if first:
                    gwait(T, "mB")
                for k in range(NK):
                    ii = T.matmul(ps_mB[0:1, hf * 256: hf * 256 + 256], ones_c[:, :],
                                  xtmp[:, k * 256:(k + 1) * 256],
                                  start=(k == 0), stop=(k == NK - 1))
                cp_sxx = inc(ii, "st")
                gset("xtmp", "st", cp_sxx)
                return cp_sxx

            def ln1_cast_sx(ch):
                return ln1_half(ch, 0, True)

            def ln1_sq_sxx(ch, cp_a):
                return ln1_half(ch, 1, False)

            def ln1_fin_bc(ch, cp_sxx):
                ra, rb, rc = rows[0:1, 0:512], rows[0:1, 512:1024], rows[0:1, 1024:1536]
                wait(V, "st", cp_sxx)
                V.tensor_scalar(ra, ps_mA[0:1, 0:512], 1.0 / C, None, ALU.mult)
                cw = inc(V.tensor_scalar(rc, ps_mB[0:1, 0:512], 1.0 / C, EPS, ALU.mult, ALU.add), "sv")
                wait(V, "sv", cw)
                cw = inc(V.tensor_tensor(rb, ra, ra, ALU.mult), "sv")
                wait(V, "sv", cw)
                cw = inc(V.tensor_tensor(rb, rc, rb, ALU.subtract), "sv")
                wait(V, "sv", cw)
                cp_rv = inc(V.reciprocal(rc, rb), "sv")
                wait(Sc, "sv", cp_rv)
                cp_rstd = inc(Sc.activation(rb, rc, AF.Sqrt), "ss")
                wait(T, "ss", cp_rstd)
                T.matmul(ps_mA[:, 0:512], onesr32[:, :], ra, start=True, stop=True)
                return inc(T.matmul(ps_mB[:, 0:512], onesr32[:, :], rb, start=True, stop=True), "st")

            def ln1_apply(ch, cp_bc):
                hb = (ch % 2) * NK * CW
                wait(V, "st", cp_bc)
                gwait(V, ("hTc", ch % 2))
                pairs = [(k, hf) for k in range(NK) for hf in range(2)]
                subcp = {}
                mulcp = {}

                def _sub(i):
                    k, hf = pairs[i]
                    sl = (i % 2) * 256
                    if i >= 2:
                        wait(V, "sv", mulcp[i - 2])
                    subcp[i] = inc(V.tensor_tensor(
                        t1f[:, sl:sl + 256],
                        x1f[:, k * S + ch * CW + hf * 256: k * S + ch * CW + hf * 256 + 256],
                        ps_mA[:, hf * 256:(hf + 1) * 256], ALU.subtract), "sv")

                def _mul(i, guard):
                    k, hf = pairs[i]
                    sl = (i % 2) * 256
                    if guard:
                        wait(V, "sv", subcp[i])
                    ii = V.tensor_tensor(
                        hTc[:, hb + k * CW + hf * 256: hb + k * CW + hf * 256 + 256],
                        t1f[:, sl:sl + 256], ps_mB[:, hf * 256:(hf + 1) * 256], ALU.mult)
                    mulcp[i] = inc(ii, "sv")
                    return ii

                _sub(0)
                for i in range(1, len(pairs)):
                    _sub(i)
                    _mul(i - 1, i == 1)
                ii = _mul(len(pairs) - 1, True)
                cp = mulcp[len(pairs) - 1]
                cp_h[ch] = cp
                gset("mAr", "sv", cp)
                gset("mA", "sv", cp)
                gset("mB", "sv", cp)
                return cp

            def emit_ln1_full(ch):
                a = ln1_cast_sx(ch)
                b = ln1_sq_sxx(ch, a)
                c_ = ln1_fin_bc(ch, b)
                ln1_apply(ch, c_)

            emit_ln1_full(0)

            for ch in range(NCH):
                hb = (ch % 2) * NK * CW
                for jp in range(2):
                    b = jp % 2
                    gwait(G, ("wblk", b))
                    cp_wdma[jp] = (f"w{b}", inc(G.dma_start(
                        bass.AP(wblk, b * 2048, [[4096, 128], [256, NK], [1, 256]]),
                        bass.AP(wqkvT, jp * 256, [[3 * C, 128], [128 * 3 * C, NK], [1, 256]])), f"w{b}", 16))
                cp_wv = {}
                for q in range(2):
                    b = q % 2
                    gwait(G, ("wvq", b))
                    cp_wv[q] = (f"v{b}", inc(G.dma_start(
                        bass.AP(wvq, b * 2048, [[4096, 128], [256, NK], [1, 256]]),
                        bass.AP(wqkvT, 2 * C + q * 256, [[3 * C, 128], [128 * 3 * C, NK], [1, 256]])),
                        f"v{b}", 16))

                wait(T, "sv", cp_h[ch])
                lnstate = {}
                for j in range(16):
                    jp, jh = j // 2, j % 2
                    b = jp % 2
                    bank = "s0a" if (j % 2 == 0) else "s0b"
                    pslice = ps_s0[:, 0:512] if (j % 2 == 0) else ps_s0[:, 512:1024]
                    if jh == 0:
                        wait(T, cp_wdma[jp][0], cp_wdma[jp][1])
                    gwait(T, bank)
                    for k in range(NK):
                        ii = T.matmul(pslice,
                                      wblk[:, b * 2048 + k * 256 + jh * 128: b * 2048 + k * 256 + jh * 128 + 128],
                                      hTc[:, hb + k * CW: hb + k * CW + CW], start=(k == 0), stop=(k == NK - 1))
                    cp_qk[j] = inc(ii, "st")
                    if jh == 1:
                        gset(("wblk", b), "st", cp_qk[j])
                        if jp + 2 < 8:
                            gwait(G, ("wblk", (jp + 2) % 2))
                            cp_wdma[jp + 2] = (f"w{(jp + 2) % 2}", inc(G.dma_start(
                                bass.AP(wblk, ((jp + 2) % 2) * 2048, [[4096, 128], [256, NK], [1, 256]]),
                                bass.AP(wqkvT, (jp + 2) * 256, [[3 * C, 128], [128 * 3 * C, NK], [1, 256]])),
                                f"w{(jp + 2) % 2}", 16))
                    tgt = QT if j < 8 else KT
                    jj = j % 8
                    wait(V, "st", cp_qk[j])
                    ii = V.tensor_scalar(tgt[:, jj * S + ch * CW: jj * S + (ch + 1) * CW],
                                         pslice, bqkv_sb[:, j:j + 1], None, ALU.add)
                    gset(bank, "sv", inc(ii, "sv"))
                    if ch + 1 < NCH:
                        if j == 5:
                            lnstate["a"] = ln1_cast_sx(ch + 1)
                        elif j == 8:
                            lnstate["b"] = ln1_sq_sxx(ch + 1, lnstate["a"])
                        elif j == 11:
                            lnstate["c"] = ln1_fin_bc(ch + 1, lnstate["b"])
                        elif j == 14:
                            ln1_apply(ch + 1, lnstate["c"])

                if ch == 0:
                    wait(T, "dg", cp_bv)
                for q in range(4):
                    b = q % 2
                    wait(T, cp_wv[q][0], cp_wv[q][1])
                    for tt in range(4):
                        tau = ch * 4 + tt
                        vb_ = (q * 4 + tt) % 2
                        bank = "s1a" if vb_ == 0 else "s1b"
                        pslice = ps_s1[:, 0:256] if vb_ == 0 else ps_s1[:, 512:768]
                        gwait(T, bank)
                        T.matmul(pslice, onesr_bf[:, :], bvrow_bf[0:1, q * 256:(q + 1) * 256],
                                 start=True, stop=False)
                        for k in range(NK):
                            ii = T.matmul(pslice, hTc[:, hb + k * CW + tt * 128: hb + k * CW + tt * 128 + 128],
                                          wvq[:, b * 2048 + k * 256: b * 2048 + (k + 1) * 256],
                                          start=False, stop=(k == NK - 1))
                        cp_v = inc(ii, "st")
                        gset(("wvq", b), "st", cp_v)
                        wait(V, "st", cp_v)
                        ii = V.tensor_copy(Vaug[:, tau * 1024 + 256 * q: tau * 1024 + 256 * (q + 1)],
                                           ps_s1[:, vb_ * 512: vb_ * 512 + 256])
                        gset(bank, "sv", inc(ii, "sv"))
                    if q + 2 < 4:
                        bnext = (q + 2) % 2
                        gwait(G, ("wvq", bnext))
                        cp_wv[q + 2] = (f"v{bnext}", inc(G.dma_start(
                            bass.AP(wvq, bnext * 2048, [[4096, 128], [256, NK], [1, 256]]),
                            bass.AP(wqkvT, 2 * C + (q + 2) * 256,
                                    [[3 * C, 128], [128 * 3 * C, NK], [1, 256]])), f"v{bnext}", 16))
                gset(("hTc", ch % 2), "st", cp_v)

        barrier()
        if "2" in phases:
          with contextlib.ExitStack() as p2:
            E2 = p2.enter_context
            SW2 = 896
            strip = E2(nc.sbuf_tensor("strip", [128, SW2], F32))
            etab = E2(nc.sbuf_tensor("etab", [128, S], BF16))
            rtab = E2(nc.sbuf_tensor("rtab", [128, 1544], BF16))
            fmt = E2(nc.sbuf_tensor("fmt", [128, 512], BF16))
            fpt = E2(nc.sbuf_tensor("fpt", [128, 512], BF16))
            kscale = E2(nc.sbuf_tensor("kscale", [128, S], BF16))
            qm = E2(nc.sbuf_tensor("qm", [128, 2 * 512], BF16))
            qp = E2(nc.sbuf_tensor("qp", [128, 2 * 512], BF16))
            Pbuf = E2(nc.sbuf_tensor("Pbuf", [128, 3 * 1024], BF16))
            Och = E2(nc.sbuf_tensor("Och", [128, NK * 512], BF16))
            wo_sb = E2(nc.sbuf_tensor("wo_sb", [128, 2 * 1024], BF16))
            recip_sb = E2(nc.sbuf_tensor("recip_sb", [128, 512], F32))
            ones32d = E2(nc.sbuf_tensor("ones32d", [128, 32], BF16))
            ind = E2(nc.sbuf_tensor("ind", [128, 128], F32))

            V.memset(recip_sb[:, :], 0.0)
            cw0 = inc(V.memset(ones32d[:, :], 0.0), "sv")
            wait(V, "sv", cw0)
            V.memset(ones32d[:, 0:1], 1.0)
            cw0 = inc(V.memset(ind[:, :], 0.0), "sv")
            wait(V, "sv", cw0)
            V.memset(ind[32:33, 0:64], 1.0)
            cp_rz = inc(V.memset(ind[0:1, 64:128], 1.0), "sv")
            wait(T, "sv", cp_rz)
            cp_dist = inc(Y.dma_start(strip[:, :],
                                      bass.AP(dist, 1536, [[STRIPW, 128], [1, SW2]])), "dq", 16)
            cp_lin = inc(G.dma_start(etab[:, :], lin[:, :]), "dg", 16)
            cp_linr = inc(G.dma_start(rtab[:, :], linr[:, :]), "dg", 16)
            wait(V, "dq", cp_dist)
            cp_s1 = inc(V.tensor_scalar(strip[:, :], strip[:, :], nega[:, 0:1], None, ALU.mult), "sv")
            wait(V, "dg", cp_linr)
            V.tensor_scalar(etab[:, :], etab[:, :], nega[:, 0:1], None, ALU.mult)
            cp_e1 = inc(V.tensor_scalar(rtab[:, :], rtab[:, :], nega[:, 0:1], None, ALU.mult), "sv")
            wait(Sc, "sv", cp_e1)
            cp_s2 = inc(Sc.activation(strip[:, :], strip[:, :], AF.Exp), "ss")
            Sc.activation(etab[:, :], etab[:, :], AF.Exp)
            cp_e2 = inc(Sc.activation(rtab[:, :], rtab[:, :], AF.Exp), "ss")
            wait(V, "ss", cp_e2)
            cp_strip = inc(V.tensor_scalar(strip[:, :], strip[:, :], 0.125, None, ALU.mult), "sv")
            V.tensor_scalar(fmt[:, :], etab[:, 0:512], 0.125, None, ALU.mult)
            cp_fmt = inc(V.tensor_scalar(fpt[:, :], rtab[:, 1025:1537], 0.125, None, ALU.mult), "sv")
            wait(T, "sv", cp_fmt)
            wait(Sc, "sv", cp_fmt)
            if kdbg:
                wait(G, "sv", cp_fmt)
                inc(G.dma_start(dbg_etab[:, :], etab[:, :]), "dg", 16)
                inc(G.dma_start(dbg_fmt[:, :], fmt[:, :]), "dg", 16)
                inc(G.dma_start(dbg_fpt[:, :], fpt[:, :]), "dg", 16)

            cp_wo = {}
            cp_oev = 0
            cp_smm_last = -1
            for ch in range(NCH):
                s0 = ch * CW
                lo_end = 4 * ch
                hi_beg = 4 * ch + 4
                for co in range(2):
                    b = co % 2
                    gwait(G, ("wo", b))
                    cp_wo[co] = (f"o{b}", inc(G.dma_start(
                        bass.AP(wo_sb, b * 1024, [[2048, 128], [128, NK], [1, 128]]),
                        bass.AP(woT, co * 128, [[C, 128], [128 * C, NK], [1, 128]])), f"o{b}", 16))

                def build_klo(gg):
                    return inc(V.tensor_tensor(
                        kscale[:, 0:s0], KT[:, gg * S: gg * S + s0],
                        bass.AP(rtab, 1536 - s0, [[1544, 128], [1, s0]]), ALU.mult), "sv")

                def build_khi(gg):
                    n = S - (s0 + CW)
                    return inc(V.tensor_tensor(
                        kscale[:, s0 + CW: S], KT[:, gg * S + s0 + CW: (gg + 1) * S],
                        bass.AP(etab, 1, [[S, 128], [1, n]]), ALU.mult), "sv")

                def build_q(gg):
                    sl = gg % 2
                    a = b_ = None
                    if lo_end > 0:
                        a = inc(V.tensor_tensor(qm[:, sl * 512:(sl + 1) * 512],
                                                QT[:, gg * S + s0: gg * S + s0 + CW],
                                                fmt[:, :], ALU.mult), "sv")
                    if hi_beg < NT:
                        b_ = inc(V.tensor_tensor(qp[:, sl * 512:(sl + 1) * 512],
                                                 QT[:, gg * S + s0: gg * S + s0 + CW],
                                                 fpt[:, :], ALU.mult), "sv")
                    return b_ or a

                if lo_end > 0:
                    wait(V, "st", cp_smm_last)
                    cp_klo = build_klo(0)
                else:
                    cp_klo = 0
                gwait(V, ("q", 0))
                if ch == 0:
                    wait(V, "sv", cp_fmt)
                cp_q = build_q(0)
                cp_khi = 0
                pending = None
                pend_cpb = None
                pend_cpc = [0]

                for g in range(NP):
                    cp_exp = {}
                    cp_pre = {}
                    cp_av = {}
                    ob = "oA" if g % 2 == 0 else "oB"
                    mb = "mA" if g % 2 == 0 else "mB"
                    pm = ps_mA if g % 2 == 0 else ps_mB
                    oc0 = 0 if g % 2 == 0 else 512
                    qsl = g % 2
                    cp_smm_lo = 0
                    cp_smm = 0
                    prev_smm_last = cp_smm_last

                    def emit_smm(tau):
                        nonlocal cp_smm_lo, cp_smm, cp_klo, cp_khi, cp_q
                        sl = tau % 2
                        pst = ps_s0 if sl == 0 else ps_s1
                        bko = ("s0a", "s0b") if sl == 0 else ("s1a", "s1b")
                        crossing = lo_end <= tau < hi_beg
                        gwait(T, bko[0], bko[1])
                        if tau < lo_end:
                            if tau == 0:
                                wait(T, "sv", max(cp_klo, cp_q))
                            lh, rh = kscale, qm
                            lo_, ro_ = tau * 128, qsl * 512
                        elif tau >= hi_beg:
                            if tau == hi_beg:
                                wait(T, "sv", max(cp_khi, cp_q))
                            lh, rh = kscale, qp
                            lo_, ro_ = tau * 128, qsl * 512
                        else:
                            lh, rh = KT, QT
                            lo_, ro_ = g * S + tau * 128, g * S + s0
                        T.matmul(pst[:, 0:512], lh[0:64, lo_: lo_ + 128], rh[0:64, ro_: ro_ + CW],
                                 start=True, stop=True, tile_position=(0, 0))
                        cp_smm = inc(T.matmul(pst[:, 512:1024], lh[64:128, lo_: lo_ + 128],
                                              rh[64:128, ro_: ro_ + CW],
                                              start=True, stop=True, tile_position=(64, 0)), "st")
                        if tau == lo_end - 1:
                            cp_smm_lo = cp_smm
                        if crossing:
                            off = 384 - (tau * 128 - s0)
                            wait(V, "st", cp_smm)
                            cp_mult = inc(V.tensor_tensor(
                                pst[:, 0:1024], pst[:, 0:1024],
                                bass.AP(strip, off, [[SW2, 128], [0, 2], [1, 512]]), ALU.mult), "sv")
                            if tau == lo_end:
                                if hi_beg < NT:
                                    wait(V, "st", prev_smm_last)
                                    cp_khi = build_khi(g)
                            elif tau == lo_end + 1 and g + 1 < NP:
                                if lo_end > 0:
                                    wait(V, "st", cp_smm_lo)
                                    cp_klo = build_klo(g + 1)
                            elif tau == lo_end + 2 and g + 1 < NP:
                                gwait(V, ("q", (g + 1) % 2))
                                cp_q = build_q(g + 1)
                            cp_pre[tau] = ("sv", cp_mult)
                        else:
                            cp_pre[tau] = ("st", cp_smm)

                    def emit_exp(tau):
                        sl = tau % 2
                        slp = tau % 3
                        pst = ps_s0 if sl == 0 else ps_s1
                        bko = ("s0a", "s0b") if sl == 0 else ("s1a", "s1b")
                        nm_, v_ = cp_pre[tau]
                        wait(Sc, nm_, v_)
                        gwait(Sc, ("P", slp))
                        cp_exp[tau] = inc(Sc.activation(Pbuf[:, slp * 1024:(slp + 1) * 1024],
                                                        pst[:, 0:1024], AF.Exp), "ss")
                        gset(bko[0], "ss", cp_exp[tau])
                        gset(bko[1], "ss", cp_exp[tau])

                    def emit_av(tau):
                        slp = tau % 3
                        wait(T, "ss", cp_exp[tau])
                        if tau == 0:
                            gwait(T, ob, "mA")
                        T.matmul(ps_o[0:64, oc0:oc0 + 512],
                                 bass.AP(Vaug, tau * 1024 + 64 * (2 * g), [[NT * 1024, 128], [1, 64]]),
                                 Pbuf[:, slp * 1024: slp * 1024 + 512],
                                 start=(tau == 0), stop=(tau == NT - 1),
                                 skip_group_check=True)
                        T.matmul(ps_o[64:128, oc0:oc0 + 512],
                                 bass.AP(Vaug, tau * 1024 + 64 * (2 * g + 1), [[NT * 1024, 128], [1, 64]]),
                                 Pbuf[:, slp * 1024 + 512: (slp + 1) * 1024],
                                 start=(tau == 0), stop=(tau == NT - 1), tile_position=(0, 64),
                                 skip_group_check=True)
                        T.matmul(pm[32:64, 0:512], ones32d[:, :],
                                 Pbuf[:, slp * 1024: slp * 1024 + 512],
                                 start=(tau == 0), stop=(tau == NT - 1), tile_position=(0, 32),
                                 skip_group_check=True)
                        cp_av[tau] = inc(T.matmul(pm[0:32, 0:512], ones32d[:, :],
                                                  Pbuf[:, slp * 1024 + 512: (slp + 1) * 1024],
                                                  start=(tau == 0), stop=(tau == NT - 1),
                                                  tile_position=(0, 0),
                                                  skip_group_check=True), "st")
                        gset(("P", slp), "st", cp_av[tau])

                    def emit_tail_recip():
                        wait(V, "st", cp_av[NT - 1])
                        gwait(V, "recip")
                        wait(V, "sv", cp_oev)
                        V.reciprocal(recip_sb[32:33, 0:512], pm[32:33, 0:512])
                        return inc(V.reciprocal(recip_sb[0:1, 0:512], pm[0:1, 0:512]), "sv")

                    def emit_tail_bcast(pend):
                        pg, ppm, prc = pend
                        wait(T, "sv", prc)
                        cpb = inc(T.matmul(ppm[:, 0:512], ind[:, :],
                                           recip_sb[:, 0:512], start=True, stop=True), "st")
                        gset("recip", "st", cpb)
                        return cpb

                    def emit_tail_copy(pend, cpb):
                        pg, ppm, prc = pend
                        pmb = "mA" if pg % 2 == 0 else "mB"
                        wait(V, "st", cpb)
                        cpc = inc(V.tensor_copy(recip_sb[:, :], ppm[:, 0:512]), "sv")
                        gset(pmb, "sv", cpc)
                        pend_cpc[0] = cpc
                        return cpc

                    def emit_tail_evict(pend):
                        pg, ppm, prc = pend
                        poc0 = 0 if pg % 2 == 0 else 512
                        pob = "oA" if pg % 2 == 0 else "oB"
                        gwait(V, ("och", pg))
                        wait(V, "sv", pend_cpc[0])
                        cpo = inc(V.tensor_tensor(Och[:, pg * 512:(pg + 1) * 512],
                                                  ps_o[:, poc0:poc0 + 512],
                                                  recip_sb[:, :], ALU.mult), "sv")
                        gset(pob, "sv", cpo)
                        if kdbg and ch == 0 and pg == 1:
                            wait(G, "sv", cpo)
                            inc(G.dma_start(dbg_p2[:, :512], recip_sb[:, :]), "dg", 16)
                            inc(G.dma_start(dbg_och[:, :], Och[:, :]), "dg", 16)
                            inc(G.dma_start(dbg_p[:, :], Pbuf[:, :]), "dg", 16)
                        return cpo

                    for tau in range(NT):
                        emit_smm(tau)
                        if tau >= 2:
                            emit_av(tau - 2)
                        emit_exp(tau)
                        if pending is not None:
                            if tau == 1:
                                pend_cpb = emit_tail_bcast(pending)
                            elif tau == 2:
                                emit_tail_copy(pending, pend_cpb)
                            elif tau == 3:
                                cp_oev = emit_tail_evict(pending)
                                pending = None
                    emit_av(NT - 2)
                    emit_av(NT - 1)
                    cp_smm_last = cp_smm
                    gset(("q", qsl), "st", cp_smm_last)
                    cp_rc = emit_tail_recip()
                    if kdbg and ch == 0 and g == 1:
                        wait(G, "sv", cp_rc)
                        cp_dk = inc(G.dma_start(dbg_p2[:, 512:1024], recip_sb[:, :]), "dg", 16)
                        wait(T, "dg", cp_dk)
                        inc(G.dma_start(dbg_p[:, :], Pbuf[:, :]), "dg", 16)
                    pending = (g, pm, cp_rc)
                    if g == NP - 1:
                        pend_cpb = emit_tail_bcast(pending)
                        emit_tail_copy(pending, pend_cpb)
                        cp_oev = emit_tail_evict(pending)
                        pending = None
                    if kdbg and ch == 0 and g == 0:
                        wait(G, "sv", cp_rc)
                        inc(G.dma_start(dbg_ks[:, :], kscale[:, :]), "dg", 16)
                        inc(G.dma_start(dbg_qp[:, :], qp[:, :]), "dg", 16)

                for co in range(8):
                    b = co % 2
                    bank = "s0a" if b == 0 else "s0b"
                    pslice = ps_s0[:, 0:512] if b == 0 else ps_s0[:, 512:1024]
                    wait(T, cp_wo[co][0], cp_wo[co][1])
                    wait(T, "sv", cp_oev)
                    gwait(T, bank)
                    for k in range(NK):
                        ii = T.matmul(pslice, wo_sb[:, b * 1024 + k * 128: b * 1024 + (k + 1) * 128],
                                      Och[:, k * 512:(k + 1) * 512], start=(k == 0), stop=(k == NK - 1))
                    cp_proj = inc(ii, "st")
                    gset(("wo", b), "st", cp_proj)
                    if co + 2 < 8:
                        gwait(G, ("wo", (co + 2) % 2))
                        cp_wo[co + 2] = (f"o{(co + 2) % 2}", inc(G.dma_start(
                            bass.AP(wo_sb, ((co + 2) % 2) * 1024, [[2048, 128], [128, NK], [1, 128]]),
                            bass.AP(woT, (co + 2) * 128, [[C, 128], [128 * C, NK], [1, 128]])),
                            f"o{(co + 2) % 2}", 16))
                    wait(V, "st", cp_proj)
                    ii = V.scalar_tensor_tensor(
                        x1f[:, co * S + ch * CW: co * S + (ch + 1) * CW], pslice, bo_sb[:, co:co + 1],
                        x1f[:, co * S + ch * CW: co * S + (ch + 1) * CW], ALU.add, ALU.add)
                    cp_x1 = inc(ii, "sv")
                    gset(bank, "sv", cp_x1)
                for g in range(NP):
                    gset(("och", g), "st", cp_proj)

        barrier()
        kqv.close()
        if "3" not in phases:
            cp_dbg = inc(Y.dma_start(
                bass.AP(yT, 0, [[S, 128], [128 * S, NK], [1, S]]),
                bass.AP(x1f, 0, [[NK * S, 128], [S, NK], [1, S]])), "dy", 16)
        if "3" in phases:
          with contextlib.ExitStack() as p3:
            E3 = p3.enter_context
            h2b = E3(nc.sbuf_tensor("h2b", [128, NK * 2052], BF16))
            wr_sb = E3(nc.sbuf_tensor("wr_sb", [128, NK * 1024], BF16))
            wk_sb = E3(nc.sbuf_tensor("wk_sb", [128, NK * 1024], BF16))
            wv2_sb = E3(nc.sbuf_tensor("wv2_sb", [128, NK * 1024], BF16))
            xrb = E3(nc.sbuf_tensor("xrb", [128, 2 * NK * CW], BF16))
            xkb = E3(nc.sbuf_tensor("xkb", [128, 2 * NK * CW], BF16))
            kact = E3(nc.sbuf_tensor("kact", [128, NK * CW], BF16))
            xtmp2 = E3(nc.sbuf_tensor("xtmp2", [128, NK * 256], BF16))
            t1g = E3(nc.sbuf_tensor("t1g", [128, 512], F32))
            dtmp = E3(nc.sbuf_tensor("dtmp", [128, 2 * CW], BF16))
            krelu = E3(nc.sbuf_tensor("krelu", [128, 2 * CW], BF16))
            rblk = E3(nc.sbuf_tensor("rblk", [128, 2 * CW], F32))
            t2f = E3(nc.sbuf_tensor("t2f", [128, CW], F32))
            rows = E3(nc.sbuf_tensor("rows3", [1, 1536], F32))

            cp_wk = inc(G.dma_start(
                bass.AP(wk_sb, 0, [[NK * 1024, 128], [1024, NK], [1, 1024]]),
                bass.AP(wkT, 0, [[C, 128], [128 * C, NK], [1, 1024]])), "fk", 16)
            cp_wr = inc(G.dma_start(
                bass.AP(wr_sb, 0, [[NK * 1024, 128], [1024, NK], [1, 1024]]),
                bass.AP(wrT, 0, [[C, 128], [128 * C, NK], [1, 1024]])), "fr", 16)
            cp_wv2 = inc(G.dma_start(
                bass.AP(wv2_sb, 0, [[NK * 1024, 128], [1024, NK], [1, 1024]]),
                bass.AP(wvT2, 0, [[C, 128], [128 * C, NK], [1, 1024]])), "fv", 16)

            cp_hb = inc(V.memset(bass.AP(h2b, 3, [[NK * 2052, 128], [2052, NK], [1, 1]]), 0.0), "sv")

            for ch in range(NCH):
                for hf in range(2):
                    gwait(V, "xtmp2")
                    for k in range(NK):
                        ii = V.tensor_copy(xtmp2[:, k * 256:(k + 1) * 256],
                                           x1f[:, k * S + ch * CW + hf * 256: k * S + ch * CW + hf * 256 + 256])
                    cp_xbf = inc(ii, "sv")
                    wait(T, "sv", cp_xbf)
                    if hf == 0:
                        gwait(T, "mA")
                    for k in range(NK):
                        ii = T.matmul(ps_mA[0:1, hf * 256: hf * 256 + 256], ones_c[:, :],
                                      xtmp2[:, k * 256:(k + 1) * 256], start=(k == 0), stop=(k == NK - 1))
                    cp_sx = inc(ii, "st")
                    wait(V, "st", cp_sx)
                    for k in range(NK):
                        ii = V.tensor_tensor(xtmp2[:, k * 256:(k + 1) * 256], xtmp2[:, k * 256:(k + 1) * 256],
                                             xtmp2[:, k * 256:(k + 1) * 256], ALU.mult)
                    cp_xsq = inc(ii, "sv")
                    wait(T, "sv", cp_xsq)
                    if hf == 0:
                        gwait(T, "mB")
                    for k in range(NK):
                        ii = T.matmul(ps_mB[0:1, hf * 256: hf * 256 + 256], ones_c[:, :],
                                      xtmp2[:, k * 256:(k + 1) * 256], start=(k == 0), stop=(k == NK - 1))
                    cp_sxx = inc(ii, "st")
                    gset("xtmp2", "st", cp_sxx)
                ra, rb, rc = rows[0:1, 0:512], rows[0:1, 512:1024], rows[0:1, 1024:1536]
                wait(V, "st", cp_sxx)
                V.tensor_scalar(ra, ps_mA[0:1, 0:512], 1.0 / C, None, ALU.mult)
                cw = inc(V.tensor_scalar(rc, ps_mB[0:1, 0:512], 1.0 / C, EPS, ALU.mult, ALU.add), "sv")
                wait(V, "sv", cw)
                cw = inc(V.tensor_tensor(rb, ra, ra, ALU.mult), "sv")
                wait(V, "sv", cw)
                cw = inc(V.tensor_tensor(rb, rc, rb, ALU.subtract), "sv")
                wait(V, "sv", cw)
                cp_rv = inc(V.reciprocal(rc, rb), "sv")
                wait(Sc, "sv", cp_rv)
                cp_rstd = inc(Sc.activation(rb, rc, AF.Sqrt), "ss")
                wait(T, "ss", cp_rstd)
                T.matmul(ps_mA[:, 0:512], onesr32[:, :], ra, start=True, stop=True)
                cp_bc = inc(T.matmul(ps_mB[:, 0:512], onesr32[:, :], rb, start=True, stop=True), "st")
                wait(V, "st", cp_bc)
                if ch == 0:
                    wait(V, "sv", cp_hb)
                pairs = [(k, hf) for k in range(NK) for hf in range(2)]
                subcp = {}
                mulcp = {}

                def _sub(i):
                    k, hf = pairs[i]
                    sl = (i % 2) * 256
                    if i >= 2:
                        wait(V, "sv", mulcp[i - 2])
                    subcp[i] = inc(V.tensor_tensor(
                        t1g[:, sl:sl + 256],
                        x1f[:, k * S + ch * CW + hf * 256: k * S + ch * CW + hf * 256 + 256],
                        ps_mA[:, hf * 256: hf * 256 + 256], ALU.subtract), "sv")

                def _mul(i, guard):
                    k, hf = pairs[i]
                    sl = (i % 2) * 256
                    if guard:
                        wait(V, "sv", subcp[i])
                    ii = V.tensor_tensor(
                        h2b[:, k * 2052 + 4 + ch * CW + hf * 256: k * 2052 + 4 + ch * CW + hf * 256 + 256],
                        t1g[:, sl:sl + 256], ps_mB[:, hf * 256: hf * 256 + 256], ALU.mult)
                    mulcp[i] = inc(ii, "sv")
                    return ii

                _sub(0)
                for i in range(1, len(pairs)):
                    _sub(i)
                    _mul(i - 1, i == 1)
                ii = _mul(len(pairs) - 1, True)
                cp_h2 = mulcp[len(pairs) - 1]
                gset("mA", "sv", cp_h2)
                gset("mB", "sv", cp_h2)

            wait(T, "fk", cp_wk)
            cp_blend = {}

            def emit_blends(ch):
                sb_ = (ch % 2) * NK * CW
                gwait(V, ("xrb", ch % 2), ("xkb", ch % 2))
                for k in range(NK):
                    cur = h2b[:, k * 2052 + 4 + ch * CW: k * 2052 + 4 + (ch + 1) * CW]
                    prev = h2b[:, k * 2052 + 3 + ch * CW: k * 2052 + 3 + (ch + 1) * CW]
                    dsl = (k % 2) * CW
                    cw_ = inc(V.tensor_tensor(dtmp[:, dsl:dsl + CW], cur, prev, ALU.subtract), "sv")
                    wait(V, "sv", cw_)
                    V.scalar_tensor_tensor(xrb[:, sb_ + k * CW: sb_ + (k + 1) * CW], dtmp[:, dsl:dsl + CW],
                                           mur_sb[:, k:k + 1], prev, ALU.mult, ALU.add)
                    ii = V.scalar_tensor_tensor(xkb[:, sb_ + k * CW: sb_ + (k + 1) * CW], dtmp[:, dsl:dsl + CW],
                                                muk_sb[:, k:k + 1], prev, ALU.mult, ALU.add)
                cp_blend[ch] = inc(ii, "sv")

            emit_blends(0)
            for ch in range(NCH):
                sb_ = (ch % 2) * NK * CW
                gwait(V, "kact")
                wait(T, "sv", cp_blend[ch])
                for j in range(NK):
                    b = j % 2
                    bank = "s0a" if b == 0 else "s0b"
                    pslice = ps_s0[:, 0:512] if b == 0 else ps_s0[:, 512:1024]
                    gwait(T, bank)
                    for k in range(NK):
                        ii = T.matmul(pslice, wk_sb[:, k * 1024 + j * 128: k * 1024 + (j + 1) * 128],
                                      xkb[:, sb_ + k * CW: sb_ + (k + 1) * CW],
                                      start=(k == 0), stop=(k == NK - 1))
                    cp_k = inc(ii, "st")
                    wait(V, "st", cp_k)
                    ksl = (j % 2) * CW
                    cw_ = inc(V.tensor_scalar(krelu[:, ksl:ksl + CW], pslice, 0.0, None, ALU.max), "sv")
                    wait(V, "sv", cw_)
                    ii = V.tensor_tensor(kact[:, j * CW:(j + 1) * CW], krelu[:, ksl:ksl + CW],
                                         krelu[:, ksl:ksl + CW], ALU.mult)
                    gset(bank, "sv", inc(ii, "sv"))
                cp_kact = cnt["sv"]
                if ch + 1 < NCH:
                    emit_blends(ch + 1)
                wait(T, "sv", cp_kact)
                if ch == 0:
                    wait(T, "fr", cp_wr)
                    wait(T, "fv", cp_wv2)

                for j in range(NK):
                    rb_ = j % 2
                    bank_r = "s1a" if rb_ == 0 else "s1b"
                    psr = ps_s1[:, 0:512] if rb_ == 0 else ps_s1[:, 512:1024]
                    bank_v = "oA" if rb_ == 0 else "oB"
                    psv = ps_o[:, 0:512] if rb_ == 0 else ps_o[:, 512:1024]
                    gwait(T, bank_r)
                    for k in range(NK):
                        ii = T.matmul(psr, wr_sb[:, k * 1024 + j * 128: k * 1024 + (j + 1) * 128],
                                      xrb[:, sb_ + k * CW: sb_ + (k + 1) * CW],
                                      start=(k == 0), stop=(k == NK - 1))
                    cp_r = inc(ii, "st")
                    gwait(T, bank_v)
                    for k in range(NK):
                        ii = T.matmul(psv, wv2_sb[:, k * 1024 + j * 128: k * 1024 + (j + 1) * 128],
                                      kact[:, k * CW:(k + 1) * CW], start=(k == 0), stop=(k == NK - 1))
                    cp_v2 = inc(ii, "st")
                    wait(Sc, "st", cp_r)
                    gwait(Sc, ("rblk", rb_))
                    cp_sig = inc(Sc.activation(rblk[:, rb_ * CW:(rb_ + 1) * CW], psr, AF.Sigmoid), "ss")
                    gset(bank_r, "ss", cp_sig)
                    wait(V, "ss", cp_sig)
                    wait(V, "st", cp_v2)
                    if j > 0:
                        wait(V, "sv", cp_y)
                    cw_ = inc(V.tensor_tensor(t2f[:, :], rblk[:, rb_ * CW:(rb_ + 1) * CW], psv, ALU.mult), "sv")
                    wait(V, "sv", cw_)
                    ii = V.tensor_tensor(x1f[:, j * S + ch * CW: j * S + (ch + 1) * CW],
                                         x1f[:, j * S + ch * CW: j * S + (ch + 1) * CW],
                                         t2f[:, :], ALU.add)
                    cp_y = inc(ii, "sv")
                    gset(bank_v, "sv", cp_y)
                    gset(("rblk", rb_), "sv", cp_y)
                gset(("xkb", ch % 2), "st", cp_k)
                gset(("xrb", ch % 2), "st", cp_r)
                gset("kact", "st", cp_v2)
                wait(Y, "sv", cp_y)
                inc(Y.dma_start(
                    bass.AP(yT, ch * CW, [[S, 128], [128 * S, NK], [1, CW]]),
                    bass.AP(x1f, ch * CW, [[NK * S, 128], [S, NK], [1, CW]])), "dy", 16)

        Y.wait_ge(sems["dy"], cnt["dy"])

    ctx.close()
    return nc


def kernel(L0_attn_x, L0_attn_Wqkv_w, L0_attn_Wqkv_b, L0_attn_out_w, L0_attn_out_b,
           L0_attn_distance_decay, L0_ffn_W_r, L0_ffn_W_k, L0_ffn_W_v, L0_ffn_mu_r, L0_ffn_mu_k):
    x = np.asarray(L0_attn_x, dtype=np.float32)
    wqkvT = np.ascontiguousarray(np.asarray(L0_attn_Wqkv_w, np.float32).T)
    bq = np.asarray(L0_attn_Wqkv_b, np.float32)
    bqkv = np.ascontiguousarray(bq.reshape(24, 128).T)
    bqkv_row = np.ascontiguousarray(bq.reshape(1, 3 * C))
    woT = np.ascontiguousarray(np.asarray(L0_attn_out_w, np.float32).T)
    bo = np.ascontiguousarray(np.asarray(L0_attn_out_b, np.float32).reshape(8, 128).T)
    ddv = np.asarray(L0_attn_distance_decay, np.float32).reshape(1, 1)
    wrT = np.ascontiguousarray(np.asarray(L0_ffn_W_r, np.float32).T)
    wkT = np.ascontiguousarray(np.asarray(L0_ffn_W_k, np.float32).T)
    wvT2 = np.ascontiguousarray(np.asarray(L0_ffn_W_v, np.float32).T)
    mur = np.ascontiguousarray(np.asarray(L0_ffn_mu_r, np.float32).reshape(8, 128).T)
    muk = np.ascontiguousarray(np.asarray(L0_ffn_mu_k, np.float32).reshape(8, 128).T)
    p = np.arange(128, dtype=np.float32)[:, None]
    u = np.arange(STRIPW, dtype=np.float32)[None, :]
    dist = np.abs(p - u + U0).astype(np.float32)
    lin = np.broadcast_to(np.arange(S, dtype=np.float32)[None, :], (128, S)).copy()
    lr = np.maximum(1536.0 - np.arange(1544, dtype=np.float32), 0.0)
    linr = np.broadcast_to(lr[None, :], (128, 1544)).copy()

    phases = os.environ.get("KPHASES", "123")
    if phases not in _CACHED:
        _CACHED[phases] = build_nc(phases)
    nc = _CACHED[phases]

    shared = dict(wqkvT=wqkvT, bqkv=bqkv, bqkv_row=bqkv_row, woT=woT, bo=bo, ddv=ddv,
                  wrT=wrT, wkT=wkT, wvT2=wvT2, mur=mur, muk=muk, dist=dist, lin=lin, linr=linr)
    in_maps = []
    for b in range(8):
        m = dict(shared)
        m["xT"] = np.ascontiguousarray(x[b].T)
        in_maps.append(m)

    res = run_bass_kernel_spmd(nc, in_maps, core_ids=list(range(8)))
    out = np.stack([np.ascontiguousarray(r["yT"].T) for r in res.results], axis=0)
    return out.astype(np.float32)


# version 29
# speedup vs baseline: 1.0073x; 1.0067x over previous
import contextlib
import os
import numpy as np
import concourse.bass as bass
import concourse.mybir as mybir
from concourse.bass_utils import run_bass_kernel_spmd

F32 = mybir.dt.float32
BF16 = mybir.dt.bfloat16
AF = mybir.ActivationFunctionType
ALU = mybir.AluOpType

S = 2048
C = 1024
NH = 16
HD = 64
NP = 8
NCH = 4
CW = 512
NT = 16
NK = 8
EPS = 1e-5
U0 = 1920
STRIPW = 3968

_CACHED = {}


def build_nc(phases="123"):
    p1parts = os.environ.get("KP1", "all")
    p1ch = int(os.environ.get("KPCH", "4"))
    statlvl = int(os.environ.get("KPSTAT", "5"))
    nc = bass.Bass("TRN2", target_bir_lowering=False)

    xT = nc.dram_tensor("xT", [C, S], F32, kind="ExternalInput")
    wqkvT = nc.dram_tensor("wqkvT", [C, 3 * C], F32, kind="ExternalInput")
    bqkv = nc.dram_tensor("bqkv", [128, 24], F32, kind="ExternalInput")
    bqkv_row = nc.dram_tensor("bqkv_row", [1, 3 * C], F32, kind="ExternalInput")
    woT = nc.dram_tensor("woT", [C, C], F32, kind="ExternalInput")
    bo = nc.dram_tensor("bo", [128, 8], F32, kind="ExternalInput")
    ddv = nc.dram_tensor("ddv", [1, 1], F32, kind="ExternalInput")
    wrT = nc.dram_tensor("wrT", [C, C], F32, kind="ExternalInput")
    wkT = nc.dram_tensor("wkT", [C, C], F32, kind="ExternalInput")
    wvT2 = nc.dram_tensor("wvT2", [C, C], F32, kind="ExternalInput")
    mur = nc.dram_tensor("mur", [128, 8], F32, kind="ExternalInput")
    muk = nc.dram_tensor("muk", [128, 8], F32, kind="ExternalInput")
    dist = nc.dram_tensor("dist", [128, STRIPW], F32, kind="ExternalInput")
    lin = nc.dram_tensor("lin", [128, S], F32, kind="ExternalInput")
    linr = nc.dram_tensor("linr", [128, 1544], F32, kind="ExternalInput")
    kdbg = os.environ.get("KDBG", "0") == "1"
    if kdbg:
        dbg_etab = nc.dram_tensor("dbg_etab", [128, S], F32, kind="ExternalOutput")
        dbg_fmt = nc.dram_tensor("dbg_fmt", [128, 512], F32, kind="ExternalOutput")
        dbg_fpt = nc.dram_tensor("dbg_fpt", [128, 512], F32, kind="ExternalOutput")
        dbg_ks = nc.dram_tensor("dbg_ks", [128, S], F32, kind="ExternalOutput")
        dbg_qp = nc.dram_tensor("dbg_qp", [128, 1024], F32, kind="ExternalOutput")
        dbg_p = nc.dram_tensor("dbg_p", [128, 2048], F32, kind="ExternalOutput")
        dbg_p2 = nc.dram_tensor("dbg_p2", [128, 2048], F32, kind="ExternalOutput")
        dbg_och = nc.dram_tensor("dbg_och", [128, 4096], F32, kind="ExternalOutput")
    yT = nc.dram_tensor("yT", [C, S], F32, kind="ExternalOutput")

    ctx = contextlib.ExitStack()
    E = ctx.enter_context

    sems = {}
    cnt = {}
    for nm in ("dq", "dg", "dy", "st", "sv", "ss", "w0", "w1", "v0", "v1", "o0", "o1", "dx0", "dx1", "dx2", "dx3", "fk", "fr", "fv"):
        sems[nm] = E(nc.semaphore(nm))
        cnt[nm] = 0

    def inc(instr, nm, k=1):
        instr.then_inc(sems[nm], k)
        cnt[nm] += k
        return cnt[nm]

    waited = {}

    def wait(eng, nm, val):
        if val > 0 and waited.get((id(eng), nm), 0) < val:
            eng.wait_ge(sems[nm], val)
            waited[(id(eng), nm)] = val

    guard = {}

    def gwait(eng, *keys):
        for k in keys:
            if k in guard:
                wait(eng, *guard[k])

    def gset(key, nm, val):
        guard[key] = (nm, val)

    def barrier():
        vals = {nm: cnt[nm] for nm in ("st", "sv", "ss", "dq", "dg", "w0", "w1", "v0", "v1",
                                        "o0", "o1", "dx0", "dx1", "dx2", "dx3", "fk", "fr", "fv")}
        for eng in (nc.tensor, nc.vector, nc.scalar, nc.sync, nc.gpsimd):
            for nm, v in vals.items():
                wait(eng, nm, v)

    x1f = E(nc.sbuf_tensor("x1f", [128, NK * S], F32))
    bqkv_sb = E(nc.sbuf_tensor("bqkv_sb", [128, 24], F32))
    bo_sb = E(nc.sbuf_tensor("bo_sb", [128, 8], F32))
    mur_sb = E(nc.sbuf_tensor("mur_sb", [128, 8], F32))
    muk_sb = E(nc.sbuf_tensor("muk_sb", [128, 8], F32))
    dd_sb = E(nc.sbuf_tensor("dd_sb", [1, 1], F32))
    absd = E(nc.sbuf_tensor("absd", [1, 1], F32))
    nega = E(nc.sbuf_tensor("nega", [128, 1], F32))
    ones_c = E(nc.sbuf_tensor("ones_c", [128, 1], BF16))
    onesr_bf = E(nc.sbuf_tensor("onesr_bf", [1, 128], BF16))
    onesr32 = E(nc.sbuf_tensor("onesr32", [1, 128], F32))

    ps_s0 = E(nc.psum_tensor("ps_s0", [128, 1024], F32))
    ps_s1 = E(nc.psum_tensor("ps_s1", [128, 1024], F32))
    ps_o = E(nc.psum_tensor("ps_o", [128, 1024], F32))
    ps_mA = E(nc.psum_tensor("ps_mA", [128, 512], F32))
    ps_mB = E(nc.psum_tensor("ps_mB", [128, 512], F32))

    blk = E(nc.Block())

    @blk.sync
    def _(sync):
        T, V, Sc, Y, G = nc.tensor, nc.vector, nc.scalar, nc.sync, nc.gpsimd

        cp_xc = {}
        for xc in range(NCH):
            cp_xc[xc] = (f"dx{xc}", inc(Y.dma_start(
                bass.AP(x1f, xc * CW, [[NK * S, 128], [S, NK], [1, CW]]),
                bass.AP(xT, xc * CW, [[S, 128], [128 * S, NK], [1, CW]])), f"dx{xc}", 16))
        inc(Y.dma_start(bqkv_sb[:, :], bqkv[:, :]), "dq", 16)
        inc(Y.dma_start(bo_sb[:, :], bo[:, :]), "dq", 16)
        inc(Y.dma_start(mur_sb[:, :], mur[:, :]), "dq", 16)
        inc(Y.dma_start(muk_sb[:, :], muk[:, :]), "dq", 16)
        cp_dd = inc(Y.dma_start(dd_sb[:, :], ddv[:, :]), "dq", 16)

        V.memset(ones_c[:, :], 1.0)
        V.memset(onesr_bf[:, :], 1.0)
        cp_vones = inc(V.memset(onesr32[:, :], 1.0), "sv")

        wait(Sc, "dq", cp_dd)
        cp_abs = inc(Sc.activation(absd[:, :], dd_sb[:, :], AF.Abs), "ss")
        wait(T, "ss", cp_abs)
        wait(T, "sv", cp_vones)
        cp_negb = inc(T.matmul(ps_mA[0:128, 0:1], onesr32[:, :], absd[:, :], start=True, stop=True), "st")
        wait(V, "st", cp_negb)
        cp_nega = inc(V.tensor_scalar(nega[:, :], ps_mA[:, 0:1], -1.0, None, ALU.mult), "sv")
        gset("mA", "sv", cp_nega)

        kqv = contextlib.ExitStack()
        KT = kqv.enter_context(nc.sbuf_tensor("KT", [128, NK * S], BF16))
        QT = kqv.enter_context(nc.sbuf_tensor("QT", [128, NK * S], BF16))
        Vaug = kqv.enter_context(nc.sbuf_tensor("Vaug", [128, NT * 1024], BF16))

        with contextlib.ExitStack() as p1:
            E1 = p1.enter_context
            hTc = E1(nc.sbuf_tensor("hTc", [128, 2 * NK * CW], BF16))
            xtmp = E1(nc.sbuf_tensor("xtmp", [128, NK * 256], BF16))
            t1f = E1(nc.sbuf_tensor("t1f", [128, 512], F32))
            wblk = E1(nc.sbuf_tensor("wblk", [128, 2 * 2048], BF16))
            wvq = E1(nc.sbuf_tensor("wvq", [128, 2 * 2048], BF16))
            bvrow_bf = E1(nc.sbuf_tensor("bvrow_bf", [1, C], BF16))
            rows = E1(nc.sbuf_tensor("rows", [1, 1536], F32))

            cp_bv = inc(G.dma_start(bvrow_bf[0:1, :],
                                    bass.AP(bqkv_row, 2 * C, [[3 * C, 1], [1, C]])), "dg", 16)

            cp_qk = {}
            cp_wdma = {}
            cp_h = {}
            wait(V, "dq", cp_dd)
            wait(T, "dq", cp_dd)

            def ln1_half(ch, hf, first):
                wait(V, cp_xc[ch][0], cp_xc[ch][1])
                gwait(V, "xtmp")
                if first:
                    gwait(V, "mAr")
                for k in range(NK):
                    ii = V.tensor_copy(xtmp[:, k * 256:(k + 1) * 256],
                                       x1f[:, k * S + ch * CW + hf * 256: k * S + ch * CW + hf * 256 + 256])
                cp_xbf = inc(ii, "sv")
                wait(T, "sv", cp_xbf)
                if first:
                    gwait(T, "mA")
                for k in range(NK):
                    ii = T.matmul(ps_mA[0:1, hf * 256: hf * 256 + 256], ones_c[:, :],
                                  xtmp[:, k * 256:(k + 1) * 256],
                                  start=(k == 0), stop=(k == NK - 1))
                cp_sx = inc(ii, "st")
                wait(V, "st", cp_sx)
                for k in range(NK):
                    ii = V.tensor_tensor(xtmp[:, k * 256:(k + 1) * 256], xtmp[:, k * 256:(k + 1) * 256],
                                         xtmp[:, k * 256:(k + 1) * 256], ALU.mult)
                cp_xsq = inc(ii, "sv")
                wait(T, "sv", cp_xsq)
                if first:
                    gwait(T, "mB")
                for k in range(NK):
                    ii = T.matmul(ps_mB[0:1, hf * 256: hf * 256 + 256], ones_c[:, :],
                                  xtmp[:, k * 256:(k + 1) * 256],
                                  start=(k == 0), stop=(k == NK - 1))
                cp_sxx = inc(ii, "st")
                gset("xtmp", "st", cp_sxx)
                return cp_sxx

            def ln1_cast_sx(ch):
                return ln1_half(ch, 0, True)

            def ln1_sq_sxx(ch, cp_a):
                return ln1_half(ch, 1, False)

            def ln1_fin_bc(ch, cp_sxx):
                ra, rb, rc = rows[0:1, 0:512], rows[0:1, 512:1024], rows[0:1, 1024:1536]
                wait(V, "st", cp_sxx)
                V.tensor_scalar(ra, ps_mA[0:1, 0:512], 1.0 / C, None, ALU.mult)
                cw = inc(V.tensor_scalar(rc, ps_mB[0:1, 0:512], 1.0 / C, EPS, ALU.mult, ALU.add), "sv")
                wait(V, "sv", cw)
                cw = inc(V.tensor_tensor(rb, ra, ra, ALU.mult), "sv")
                wait(V, "sv", cw)
                cw = inc(V.tensor_tensor(rb, rc, rb, ALU.subtract), "sv")
                wait(V, "sv", cw)
                cp_rv = inc(V.reciprocal(rc, rb), "sv")
                wait(Sc, "sv", cp_rv)
                cp_rstd = inc(Sc.activation(rb, rc, AF.Sqrt), "ss")
                wait(T, "ss", cp_rstd)
                T.matmul(ps_mA[:, 0:512], onesr32[:, :], ra, start=True, stop=True)
                return inc(T.matmul(ps_mB[:, 0:512], onesr32[:, :], rb, start=True, stop=True), "st")

            def ln1_apply(ch, cp_bc):
                hb = (ch % 2) * NK * CW
                wait(V, "st", cp_bc)
                gwait(V, ("hTc", ch % 2))
                pairs = [(k, hf) for k in range(NK) for hf in range(2)]
                subcp = {}
                mulcp = {}

                def _sub(i):
                    k, hf = pairs[i]
                    sl = (i % 2) * 256
                    if i >= 2:
                        wait(V, "sv", mulcp[i - 2])
                    subcp[i] = inc(V.tensor_tensor(
                        t1f[:, sl:sl + 256],
                        x1f[:, k * S + ch * CW + hf * 256: k * S + ch * CW + hf * 256 + 256],
                        ps_mA[:, hf * 256:(hf + 1) * 256], ALU.subtract), "sv")

                def _mul(i, guard):
                    k, hf = pairs[i]
                    sl = (i % 2) * 256
                    if guard:
                        wait(V, "sv", subcp[i])
                    ii = V.tensor_tensor(
                        hTc[:, hb + k * CW + hf * 256: hb + k * CW + hf * 256 + 256],
                        t1f[:, sl:sl + 256], ps_mB[:, hf * 256:(hf + 1) * 256], ALU.mult)
                    mulcp[i] = inc(ii, "sv")
                    return ii

                _sub(0)
                for i in range(1, len(pairs)):
                    _sub(i)
                    _mul(i - 1, i == 1)
                ii = _mul(len(pairs) - 1, True)
                cp = mulcp[len(pairs) - 1]
                cp_h[ch] = cp
                gset("mAr", "sv", cp)
                gset("mA", "sv", cp)
                gset("mB", "sv", cp)
                return cp

            def emit_ln1_full(ch):
                a = ln1_cast_sx(ch)
                b = ln1_sq_sxx(ch, a)
                c_ = ln1_fin_bc(ch, b)
                ln1_apply(ch, c_)

            emit_ln1_full(0)

            for ch in range(NCH):
                hb = (ch % 2) * NK * CW
                for jp in range(2):
                    b = jp % 2
                    gwait(G, ("wblk", b))
                    cp_wdma[jp] = (f"w{b}", inc(G.dma_start(
                        bass.AP(wblk, b * 2048, [[4096, 128], [256, NK], [1, 256]]),
                        bass.AP(wqkvT, jp * 256, [[3 * C, 128], [128 * 3 * C, NK], [1, 256]])), f"w{b}", 16))
                cp_wv = {}
                for q in range(2):
                    b = q % 2
                    gwait(G, ("wvq", b))
                    cp_wv[q] = (f"v{b}", inc(G.dma_start(
                        bass.AP(wvq, b * 2048, [[4096, 128], [256, NK], [1, 256]]),
                        bass.AP(wqkvT, 2 * C + q * 256, [[3 * C, 128], [128 * 3 * C, NK], [1, 256]])),
                        f"v{b}", 16))

                wait(T, "sv", cp_h[ch])
                lnstate = {}
                for j in range(16):
                    jp, jh = j // 2, j % 2
                    b = jp % 2
                    bank = "s0a" if (j % 2 == 0) else "s0b"
                    pslice = ps_s0[:, 0:512] if (j % 2 == 0) else ps_s0[:, 512:1024]
                    if jh == 0:
                        wait(T, cp_wdma[jp][0], cp_wdma[jp][1])
                    gwait(T, bank)
                    for k in range(NK):
                        ii = T.matmul(pslice,
                                      wblk[:, b * 2048 + k * 256 + jh * 128: b * 2048 + k * 256 + jh * 128 + 128],
                                      hTc[:, hb + k * CW: hb + k * CW + CW], start=(k == 0), stop=(k == NK - 1))
                    cp_qk[j] = inc(ii, "st")
                    if jh == 1:
                        gset(("wblk", b), "st", cp_qk[j])
                        if jp + 2 < 8:
                            gwait(G, ("wblk", (jp + 2) % 2))
                            cp_wdma[jp + 2] = (f"w{(jp + 2) % 2}", inc(G.dma_start(
                                bass.AP(wblk, ((jp + 2) % 2) * 2048, [[4096, 128], [256, NK], [1, 256]]),
                                bass.AP(wqkvT, (jp + 2) * 256, [[3 * C, 128], [128 * 3 * C, NK], [1, 256]])),
                                f"w{(jp + 2) % 2}", 16))
                    tgt = QT if j < 8 else KT
                    jj = j % 8
                    wait(V, "st", cp_qk[j])
                    ii = V.tensor_scalar(tgt[:, jj * S + ch * CW: jj * S + (ch + 1) * CW],
                                         pslice, bqkv_sb[:, j:j + 1], None, ALU.add)
                    gset(bank, "sv", inc(ii, "sv"))
                    if ch + 1 < NCH:
                        if j == 5:
                            lnstate["a"] = ln1_cast_sx(ch + 1)
                        elif j == 8:
                            lnstate["b"] = ln1_sq_sxx(ch + 1, lnstate["a"])
                        elif j == 11:
                            lnstate["c"] = ln1_fin_bc(ch + 1, lnstate["b"])
                        elif j == 14:
                            ln1_apply(ch + 1, lnstate["c"])

                if ch == 0:
                    wait(T, "dg", cp_bv)
                for q in range(4):
                    b = q % 2
                    wait(T, cp_wv[q][0], cp_wv[q][1])
                    for tt in range(4):
                        tau = ch * 4 + tt
                        vb_ = (q * 4 + tt) % 2
                        bank = "s1a" if vb_ == 0 else "s1b"
                        pslice = ps_s1[:, 0:256] if vb_ == 0 else ps_s1[:, 512:768]
                        gwait(T, bank)
                        T.matmul(pslice, onesr_bf[:, :], bvrow_bf[0:1, q * 256:(q + 1) * 256],
                                 start=True, stop=False)
                        for k in range(NK):
                            ii = T.matmul(pslice, hTc[:, hb + k * CW + tt * 128: hb + k * CW + tt * 128 + 128],
                                          wvq[:, b * 2048 + k * 256: b * 2048 + (k + 1) * 256],
                                          start=False, stop=(k == NK - 1))
                        cp_v = inc(ii, "st")
                        gset(("wvq", b), "st", cp_v)
                        wait(V, "st", cp_v)
                        ii = V.tensor_copy(Vaug[:, tau * 1024 + 256 * q: tau * 1024 + 256 * (q + 1)],
                                           ps_s1[:, vb_ * 512: vb_ * 512 + 256])
                        gset(bank, "sv", inc(ii, "sv"))
                    if q + 2 < 4:
                        bnext = (q + 2) % 2
                        gwait(G, ("wvq", bnext))
                        cp_wv[q + 2] = (f"v{bnext}", inc(G.dma_start(
                            bass.AP(wvq, bnext * 2048, [[4096, 128], [256, NK], [1, 256]]),
                            bass.AP(wqkvT, 2 * C + (q + 2) * 256,
                                    [[3 * C, 128], [128 * 3 * C, NK], [1, 256]])), f"v{bnext}", 16))
                gset(("hTc", ch % 2), "st", cp_v)

        barrier()
        if "2" in phases:
          with contextlib.ExitStack() as p2:
            E2 = p2.enter_context
            SW2 = 896
            strip = E2(nc.sbuf_tensor("strip", [128, SW2], F32))
            etab = E2(nc.sbuf_tensor("etab", [128, S], BF16))
            rtab = E2(nc.sbuf_tensor("rtab", [128, 1544], BF16))
            fmt = E2(nc.sbuf_tensor("fmt", [128, 512], BF16))
            fpt = E2(nc.sbuf_tensor("fpt", [128, 512], BF16))
            kscale = E2(nc.sbuf_tensor("kscale", [128, S], BF16))
            qm = E2(nc.sbuf_tensor("qm", [128, 2 * 512], BF16))
            qp = E2(nc.sbuf_tensor("qp", [128, 2 * 512], BF16))
            Pbuf = E2(nc.sbuf_tensor("Pbuf", [128, 4 * 1024], BF16))
            Och = E2(nc.sbuf_tensor("Och", [128, NK * 512], BF16))
            wo_sb = E2(nc.sbuf_tensor("wo_sb", [128, 2 * 1024], BF16))
            recip_sb = E2(nc.sbuf_tensor("recip_sb", [128, 512], F32))
            ones32d = E2(nc.sbuf_tensor("ones32d", [128, 32], BF16))
            ind = E2(nc.sbuf_tensor("ind", [128, 128], F32))

            V.memset(recip_sb[:, :], 0.0)
            cw0 = inc(V.memset(ones32d[:, :], 0.0), "sv")
            wait(V, "sv", cw0)
            V.memset(ones32d[:, 0:1], 1.0)
            cw0 = inc(V.memset(ind[:, :], 0.0), "sv")
            wait(V, "sv", cw0)
            V.memset(ind[32:33, 0:64], 1.0)
            cp_rz = inc(V.memset(ind[0:1, 64:128], 1.0), "sv")
            wait(T, "sv", cp_rz)
            cp_dist = inc(Y.dma_start(strip[:, :],
                                      bass.AP(dist, 1536, [[STRIPW, 128], [1, SW2]])), "dq", 16)
            cp_lin = inc(G.dma_start(etab[:, :], lin[:, :]), "dg", 16)
            cp_linr = inc(G.dma_start(rtab[:, :], linr[:, :]), "dg", 16)
            wait(V, "dq", cp_dist)
            cp_s1 = inc(V.tensor_scalar(strip[:, :], strip[:, :], nega[:, 0:1], None, ALU.mult), "sv")
            wait(V, "dg", cp_linr)
            V.tensor_scalar(etab[:, :], etab[:, :], nega[:, 0:1], None, ALU.mult)
            cp_e1 = inc(V.tensor_scalar(rtab[:, :], rtab[:, :], nega[:, 0:1], None, ALU.mult), "sv")
            wait(Sc, "sv", cp_e1)
            cp_s2 = inc(Sc.activation(strip[:, :], strip[:, :], AF.Exp), "ss")
            Sc.activation(etab[:, :], etab[:, :], AF.Exp)
            cp_e2 = inc(Sc.activation(rtab[:, :], rtab[:, :], AF.Exp), "ss")
            wait(V, "ss", cp_e2)
            cp_strip = inc(V.tensor_scalar(strip[:, :], strip[:, :], 0.125, None, ALU.mult), "sv")
            V.tensor_scalar(fmt[:, :], etab[:, 0:512], 0.125, None, ALU.mult)
            cp_fmt = inc(V.tensor_scalar(fpt[:, :], rtab[:, 1025:1537], 0.125, None, ALU.mult), "sv")
            wait(T, "sv", cp_fmt)
            wait(Sc, "sv", cp_fmt)
            if kdbg:
                wait(G, "sv", cp_fmt)
                inc(G.dma_start(dbg_etab[:, :], etab[:, :]), "dg", 16)
                inc(G.dma_start(dbg_fmt[:, :], fmt[:, :]), "dg", 16)
                inc(G.dma_start(dbg_fpt[:, :], fpt[:, :]), "dg", 16)

            cp_wo = {}
            cp_oev = 0
            cp_smm_last = -1
            for ch in range(NCH):
                s0 = ch * CW
                lo_end = 4 * ch
                hi_beg = 4 * ch + 4
                for co in range(2):
                    b = co % 2
                    gwait(G, ("wo", b))
                    cp_wo[co] = (f"o{b}", inc(G.dma_start(
                        bass.AP(wo_sb, b * 1024, [[2048, 128], [128, NK], [1, 128]]),
                        bass.AP(woT, co * 128, [[C, 128], [128 * C, NK], [1, 128]])), f"o{b}", 16))

                def build_klo(gg):
                    return inc(V.tensor_tensor(
                        kscale[:, 0:s0], KT[:, gg * S: gg * S + s0],
                        bass.AP(rtab, 1536 - s0, [[1544, 128], [1, s0]]), ALU.mult), "sv")

                def build_khi(gg):
                    n = S - (s0 + CW)
                    return inc(V.tensor_tensor(
                        kscale[:, s0 + CW: S], KT[:, gg * S + s0 + CW: (gg + 1) * S],
                        bass.AP(etab, 1, [[S, 128], [1, n]]), ALU.mult), "sv")

                def build_q(gg):
                    sl = gg % 2
                    a = b_ = None
                    if lo_end > 0:
                        a = inc(V.tensor_tensor(qm[:, sl * 512:(sl + 1) * 512],
                                                QT[:, gg * S + s0: gg * S + s0 + CW],
                                                fmt[:, :], ALU.mult), "sv")
                    if hi_beg < NT:
                        b_ = inc(V.tensor_tensor(qp[:, sl * 512:(sl + 1) * 512],
                                                 QT[:, gg * S + s0: gg * S + s0 + CW],
                                                 fpt[:, :], ALU.mult), "sv")
                    return b_ or a

                if lo_end > 0:
                    wait(V, "st", cp_smm_last)
                    cp_klo = build_klo(0)
                else:
                    cp_klo = 0
                gwait(V, ("q", 0))
                if ch == 0:
                    wait(V, "sv", cp_fmt)
                cp_q = build_q(0)
                cp_khi = 0
                pending = None
                pend_cpb = None
                pend_cpc = [0]

                for g in range(NP):
                    cp_exp = {}
                    cp_pre = {}
                    cp_av = {}
                    ob = "oA" if g % 2 == 0 else "oB"
                    mb = "mA" if g % 2 == 0 else "mB"
                    pm = ps_mA if g % 2 == 0 else ps_mB
                    oc0 = 0 if g % 2 == 0 else 512
                    qsl = g % 2
                    cp_smm_lo = 0
                    cp_smm = 0
                    prev_smm_last = cp_smm_last

                    def emit_smm(tau):
                        nonlocal cp_smm_lo, cp_smm, cp_klo, cp_khi, cp_q
                        sl = tau % 2
                        pst = ps_s0 if sl == 0 else ps_s1
                        bko = ("s0a", "s0b") if sl == 0 else ("s1a", "s1b")
                        crossing = lo_end <= tau < hi_beg
                        gwait(T, bko[0], bko[1])
                        if tau < lo_end:
                            if tau == 0:
                                wait(T, "sv", max(cp_klo, cp_q))
                            lh, rh = kscale, qm
                            lo_, ro_ = tau * 128, qsl * 512
                        elif tau >= hi_beg:
                            if tau == hi_beg:
                                wait(T, "sv", max(cp_khi, cp_q))
                            lh, rh = kscale, qp
                            lo_, ro_ = tau * 128, qsl * 512
                        else:
                            lh, rh = KT, QT
                            lo_, ro_ = g * S + tau * 128, g * S + s0
                        T.matmul(pst[:, 0:512], lh[0:64, lo_: lo_ + 128], rh[0:64, ro_: ro_ + CW],
                                 start=True, stop=True, tile_position=(0, 0))
                        cp_smm = inc(T.matmul(pst[:, 512:1024], lh[64:128, lo_: lo_ + 128],
                                              rh[64:128, ro_: ro_ + CW],
                                              start=True, stop=True, tile_position=(64, 0)), "st")
                        if tau == lo_end - 1:
                            cp_smm_lo = cp_smm
                        if crossing:
                            off = 384 - (tau * 128 - s0)
                            wait(V, "st", cp_smm)
                            cp_mult = inc(V.tensor_tensor(
                                pst[:, 0:1024], pst[:, 0:1024],
                                bass.AP(strip, off, [[SW2, 128], [0, 2], [1, 512]]), ALU.mult), "sv")
                            if tau == lo_end:
                                if hi_beg < NT:
                                    wait(V, "st", prev_smm_last)
                                    cp_khi = build_khi(g)
                            elif tau == lo_end + 1 and g + 1 < NP:
                                if lo_end > 0:
                                    wait(V, "st", cp_smm_lo)
                                    cp_klo = build_klo(g + 1)
                            elif tau == lo_end + 2 and g + 1 < NP:
                                gwait(V, ("q", (g + 1) % 2))
                                cp_q = build_q(g + 1)
                            cp_pre[tau] = ("sv", cp_mult)
                        else:
                            cp_pre[tau] = ("st", cp_smm)

                    def emit_exp(tau):
                        sl = tau % 2
                        slp = tau % 4
                        pst = ps_s0 if sl == 0 else ps_s1
                        bko = ("s0a", "s0b") if sl == 0 else ("s1a", "s1b")
                        nm_, v_ = cp_pre[tau]
                        wait(Sc, nm_, v_)
                        gwait(Sc, ("P", slp))
                        cp_exp[tau] = inc(Sc.activation(Pbuf[:, slp * 1024:(slp + 1) * 1024],
                                                        pst[:, 0:1024], AF.Exp), "ss")
                        gset(bko[0], "ss", cp_exp[tau])
                        gset(bko[1], "ss", cp_exp[tau])

                    def emit_av(tau):
                        slp = tau % 4
                        wait(T, "ss", cp_exp[tau])
                        if tau == 0:
                            gwait(T, ob, "mA")
                        T.matmul(ps_o[0:64, oc0:oc0 + 512],
                                 bass.AP(Vaug, tau * 1024 + 64 * (2 * g), [[NT * 1024, 128], [1, 64]]),
                                 Pbuf[:, slp * 1024: slp * 1024 + 512],
                                 start=(tau == 0), stop=(tau == NT - 1),
                                 skip_group_check=True)
                        T.matmul(ps_o[64:128, oc0:oc0 + 512],
                                 bass.AP(Vaug, tau * 1024 + 64 * (2 * g + 1), [[NT * 1024, 128], [1, 64]]),
                                 Pbuf[:, slp * 1024 + 512: (slp + 1) * 1024],
                                 start=(tau == 0), stop=(tau == NT - 1), tile_position=(0, 64),
                                 skip_group_check=True)
                        T.matmul(pm[32:64, 0:512], ones32d[:, :],
                                 Pbuf[:, slp * 1024: slp * 1024 + 512],
                                 start=(tau == 0), stop=(tau == NT - 1), tile_position=(0, 32),
                                 skip_group_check=True)
                        cp_av[tau] = inc(T.matmul(pm[0:32, 0:512], ones32d[:, :],
                                                  Pbuf[:, slp * 1024 + 512: (slp + 1) * 1024],
                                                  start=(tau == 0), stop=(tau == NT - 1),
                                                  tile_position=(0, 0),
                                                  skip_group_check=True), "st")
                        gset(("P", slp), "st", cp_av[tau])

                    def emit_tail_recip():
                        wait(V, "st", cp_av[NT - 1])
                        gwait(V, "recip")
                        wait(V, "sv", cp_oev)
                        V.reciprocal(recip_sb[32:33, 0:512], pm[32:33, 0:512])
                        return inc(V.reciprocal(recip_sb[0:1, 0:512], pm[0:1, 0:512]), "sv")

                    def emit_tail_bcast(pend):
                        pg, ppm, prc = pend
                        wait(T, "sv", prc)
                        cpb = inc(T.matmul(ppm[:, 0:512], ind[:, :],
                                           recip_sb[:, 0:512], start=True, stop=True), "st")
                        gset("recip", "st", cpb)
                        return cpb

                    def emit_tail_copy(pend, cpb):
                        pg, ppm, prc = pend
                        pmb = "mA" if pg % 2 == 0 else "mB"
                        wait(V, "st", cpb)
                        cpc = inc(V.tensor_copy(recip_sb[:, :], ppm[:, 0:512]), "sv")
                        gset(pmb, "sv", cpc)
                        pend_cpc[0] = cpc
                        return cpc

                    def emit_tail_evict(pend):
                        pg, ppm, prc = pend
                        poc0 = 0 if pg % 2 == 0 else 512
                        pob = "oA" if pg % 2 == 0 else "oB"
                        gwait(V, ("och", pg))
                        wait(V, "sv", pend_cpc[0])
                        cpo = inc(V.tensor_tensor(Och[:, pg * 512:(pg + 1) * 512],
                                                  ps_o[:, poc0:poc0 + 512],
                                                  recip_sb[:, :], ALU.mult), "sv")
                        gset(pob, "sv", cpo)
                        if kdbg and ch == 0 and pg == 1:
                            wait(G, "sv", cpo)
                            inc(G.dma_start(dbg_p2[:, :512], recip_sb[:, :]), "dg", 16)
                            inc(G.dma_start(dbg_och[:, :], Och[:, :]), "dg", 16)
                            inc(G.dma_start(dbg_p[:, :], Pbuf[:, :]), "dg", 16)
                        return cpo

                    for tau in range(NT):
                        emit_smm(tau)
                        if tau >= 2:
                            emit_av(tau - 2)
                        emit_exp(tau)
                        if pending is not None:
                            if tau == 1:
                                pend_cpb = emit_tail_bcast(pending)
                            elif tau == 2:
                                emit_tail_copy(pending, pend_cpb)
                            elif tau == 3:
                                cp_oev = emit_tail_evict(pending)
                                pending = None
                    emit_av(NT - 2)
                    emit_av(NT - 1)
                    cp_smm_last = cp_smm
                    gset(("q", qsl), "st", cp_smm_last)
                    cp_rc = emit_tail_recip()
                    if kdbg and ch == 0 and g == 1:
                        wait(G, "sv", cp_rc)
                        cp_dk = inc(G.dma_start(dbg_p2[:, 512:1024], recip_sb[:, :]), "dg", 16)
                        wait(T, "dg", cp_dk)
                        inc(G.dma_start(dbg_p[:, :], Pbuf[:, :]), "dg", 16)
                    pending = (g, pm, cp_rc)
                    if g == NP - 1:
                        pend_cpb = emit_tail_bcast(pending)
                        emit_tail_copy(pending, pend_cpb)
                        cp_oev = emit_tail_evict(pending)
                        pending = None
                    if kdbg and ch == 0 and g == 0:
                        wait(G, "sv", cp_rc)
                        inc(G.dma_start(dbg_ks[:, :], kscale[:, :]), "dg", 16)
                        inc(G.dma_start(dbg_qp[:, :], qp[:, :]), "dg", 16)

                for co in range(8):
                    b = co % 2
                    bank = "s0a" if b == 0 else "s0b"
                    pslice = ps_s0[:, 0:512] if b == 0 else ps_s0[:, 512:1024]
                    wait(T, cp_wo[co][0], cp_wo[co][1])
                    wait(T, "sv", cp_oev)
                    gwait(T, bank)
                    for k in range(NK):
                        ii = T.matmul(pslice, wo_sb[:, b * 1024 + k * 128: b * 1024 + (k + 1) * 128],
                                      Och[:, k * 512:(k + 1) * 512], start=(k == 0), stop=(k == NK - 1))
                    cp_proj = inc(ii, "st")
                    gset(("wo", b), "st", cp_proj)
                    if co + 2 < 8:
                        gwait(G, ("wo", (co + 2) % 2))
                        cp_wo[co + 2] = (f"o{(co + 2) % 2}", inc(G.dma_start(
                            bass.AP(wo_sb, ((co + 2) % 2) * 1024, [[2048, 128], [128, NK], [1, 128]]),
                            bass.AP(woT, (co + 2) * 128, [[C, 128], [128 * C, NK], [1, 128]])),
                            f"o{(co + 2) % 2}", 16))
                    wait(V, "st", cp_proj)
                    ii = V.scalar_tensor_tensor(
                        x1f[:, co * S + ch * CW: co * S + (ch + 1) * CW], pslice, bo_sb[:, co:co + 1],
                        x1f[:, co * S + ch * CW: co * S + (ch + 1) * CW], ALU.add, ALU.add)
                    cp_x1 = inc(ii, "sv")
                    gset(bank, "sv", cp_x1)
                for g in range(NP):
                    gset(("och", g), "st", cp_proj)

        barrier()
        kqv.close()
        if "3" not in phases:
            cp_dbg = inc(Y.dma_start(
                bass.AP(yT, 0, [[S, 128], [128 * S, NK], [1, S]]),
                bass.AP(x1f, 0, [[NK * S, 128], [S, NK], [1, S]])), "dy", 16)
        if "3" in phases:
          with contextlib.ExitStack() as p3:
            E3 = p3.enter_context
            h2b = E3(nc.sbuf_tensor("h2b", [128, NK * 2052], BF16))
            wr_sb = E3(nc.sbuf_tensor("wr_sb", [128, NK * 1024], BF16))
            wk_sb = E3(nc.sbuf_tensor("wk_sb", [128, NK * 1024], BF16))
            wv2_sb = E3(nc.sbuf_tensor("wv2_sb", [128, NK * 1024], BF16))
            xrb = E3(nc.sbuf_tensor("xrb", [128, 2 * NK * CW], BF16))
            xkb = E3(nc.sbuf_tensor("xkb", [128, 2 * NK * CW], BF16))
            kact = E3(nc.sbuf_tensor("kact", [128, NK * CW], BF16))
            xtmp2 = E3(nc.sbuf_tensor("xtmp2", [128, NK * 256], BF16))
            t1g = E3(nc.sbuf_tensor("t1g", [128, 512], F32))
            dtmp = E3(nc.sbuf_tensor("dtmp", [128, 2 * CW], BF16))
            krelu = E3(nc.sbuf_tensor("krelu", [128, 2 * CW], BF16))
            rblk = E3(nc.sbuf_tensor("rblk", [128, 2 * CW], F32))
            t2f = E3(nc.sbuf_tensor("t2f", [128, CW], F32))
            rows = E3(nc.sbuf_tensor("rows3", [1, 1536], F32))

            cp_wk = inc(G.dma_start(
                bass.AP(wk_sb, 0, [[NK * 1024, 128], [1024, NK], [1, 1024]]),
                bass.AP(wkT, 0, [[C, 128], [128 * C, NK], [1, 1024]])), "fk", 16)
            cp_wr = inc(G.dma_start(
                bass.AP(wr_sb, 0, [[NK * 1024, 128], [1024, NK], [1, 1024]]),
                bass.AP(wrT, 0, [[C, 128], [128 * C, NK], [1, 1024]])), "fr", 16)
            cp_wv2 = inc(G.dma_start(
                bass.AP(wv2_sb, 0, [[NK * 1024, 128], [1024, NK], [1, 1024]]),
                bass.AP(wvT2, 0, [[C, 128], [128 * C, NK], [1, 1024]])), "fv", 16)

            cp_hb = inc(V.memset(bass.AP(h2b, 3, [[NK * 2052, 128], [2052, NK], [1, 1]]), 0.0), "sv")

            for ch in range(NCH):
                for hf in range(2):
                    gwait(V, "xtmp2")
                    for k in range(NK):
                        ii = V.tensor_copy(xtmp2[:, k * 256:(k + 1) * 256],
                                           x1f[:, k * S + ch * CW + hf * 256: k * S + ch * CW + hf * 256 + 256])
                    cp_xbf = inc(ii, "sv")
                    wait(T, "sv", cp_xbf)
                    if hf == 0:
                        gwait(T, "mA")
                    for k in range(NK):
                        ii = T.matmul(ps_mA[0:1, hf * 256: hf * 256 + 256], ones_c[:, :],
                                      xtmp2[:, k * 256:(k + 1) * 256], start=(k == 0), stop=(k == NK - 1))
                    cp_sx = inc(ii, "st")
                    wait(V, "st", cp_sx)
                    for k in range(NK):
                        ii = V.tensor_tensor(xtmp2[:, k * 256:(k + 1) * 256], xtmp2[:, k * 256:(k + 1) * 256],
                                             xtmp2[:, k * 256:(k + 1) * 256], ALU.mult)
                    cp_xsq = inc(ii, "sv")
                    wait(T, "sv", cp_xsq)
                    if hf == 0:
                        gwait(T, "mB")
                    for k in range(NK):
                        ii = T.matmul(ps_mB[0:1, hf * 256: hf * 256 + 256], ones_c[:, :],
                                      xtmp2[:, k * 256:(k + 1) * 256], start=(k == 0), stop=(k == NK - 1))
                    cp_sxx = inc(ii, "st")
                    gset("xtmp2", "st", cp_sxx)
                ra, rb, rc = rows[0:1, 0:512], rows[0:1, 512:1024], rows[0:1, 1024:1536]
                wait(V, "st", cp_sxx)
                V.tensor_scalar(ra, ps_mA[0:1, 0:512], 1.0 / C, None, ALU.mult)
                cw = inc(V.tensor_scalar(rc, ps_mB[0:1, 0:512], 1.0 / C, EPS, ALU.mult, ALU.add), "sv")
                wait(V, "sv", cw)
                cw = inc(V.tensor_tensor(rb, ra, ra, ALU.mult), "sv")
                wait(V, "sv", cw)
                cw = inc(V.tensor_tensor(rb, rc, rb, ALU.subtract), "sv")
                wait(V, "sv", cw)
                cp_rv = inc(V.reciprocal(rc, rb), "sv")
                wait(Sc, "sv", cp_rv)
                cp_rstd = inc(Sc.activation(rb, rc, AF.Sqrt), "ss")
                wait(T, "ss", cp_rstd)
                T.matmul(ps_mA[:, 0:512], onesr32[:, :], ra, start=True, stop=True)
                cp_bc = inc(T.matmul(ps_mB[:, 0:512], onesr32[:, :], rb, start=True, stop=True), "st")
                wait(V, "st", cp_bc)
                if ch == 0:
                    wait(V, "sv", cp_hb)
                pairs = [(k, hf) for k in range(NK) for hf in range(2)]
                subcp = {}
                mulcp = {}

                def _sub(i):
                    k, hf = pairs[i]
                    sl = (i % 2) * 256
                    if i >= 2:
                        wait(V, "sv", mulcp[i - 2])
                    subcp[i] = inc(V.tensor_tensor(
                        t1g[:, sl:sl + 256],
                        x1f[:, k * S + ch * CW + hf * 256: k * S + ch * CW + hf * 256 + 256],
                        ps_mA[:, hf * 256: hf * 256 + 256], ALU.subtract), "sv")

                def _mul(i, guard):
                    k, hf = pairs[i]
                    sl = (i % 2) * 256
                    if guard:
                        wait(V, "sv", subcp[i])
                    ii = V.tensor_tensor(
                        h2b[:, k * 2052 + 4 + ch * CW + hf * 256: k * 2052 + 4 + ch * CW + hf * 256 + 256],
                        t1g[:, sl:sl + 256], ps_mB[:, hf * 256: hf * 256 + 256], ALU.mult)
                    mulcp[i] = inc(ii, "sv")
                    return ii

                _sub(0)
                for i in range(1, len(pairs)):
                    _sub(i)
                    _mul(i - 1, i == 1)
                ii = _mul(len(pairs) - 1, True)
                cp_h2 = mulcp[len(pairs) - 1]
                gset("mA", "sv", cp_h2)
                gset("mB", "sv", cp_h2)

            wait(T, "fk", cp_wk)
            cp_blend = {}

            def emit_blends(ch):
                sb_ = (ch % 2) * NK * CW
                gwait(V, ("xrb", ch % 2), ("xkb", ch % 2))
                for k in range(NK):
                    cur = h2b[:, k * 2052 + 4 + ch * CW: k * 2052 + 4 + (ch + 1) * CW]
                    prev = h2b[:, k * 2052 + 3 + ch * CW: k * 2052 + 3 + (ch + 1) * CW]
                    dsl = (k % 2) * CW
                    cw_ = inc(V.tensor_tensor(dtmp[:, dsl:dsl + CW], cur, prev, ALU.subtract), "sv")
                    wait(V, "sv", cw_)
                    V.scalar_tensor_tensor(xrb[:, sb_ + k * CW: sb_ + (k + 1) * CW], dtmp[:, dsl:dsl + CW],
                                           mur_sb[:, k:k + 1], prev, ALU.mult, ALU.add)
                    ii = V.scalar_tensor_tensor(xkb[:, sb_ + k * CW: sb_ + (k + 1) * CW], dtmp[:, dsl:dsl + CW],
                                                muk_sb[:, k:k + 1], prev, ALU.mult, ALU.add)
                cp_blend[ch] = inc(ii, "sv")

            emit_blends(0)
            for ch in range(NCH):
                sb_ = (ch % 2) * NK * CW
                gwait(V, "kact")
                wait(T, "sv", cp_blend[ch])
                for j in range(NK):
                    b = j % 2
                    bank = "s0a" if b == 0 else "s0b"
                    pslice = ps_s0[:, 0:512] if b == 0 else ps_s0[:, 512:1024]
                    gwait(T, bank)
                    for k in range(NK):
                        ii = T.matmul(pslice, wk_sb[:, k * 1024 + j * 128: k * 1024 + (j + 1) * 128],
                                      xkb[:, sb_ + k * CW: sb_ + (k + 1) * CW],
                                      start=(k == 0), stop=(k == NK - 1))
                    cp_k = inc(ii, "st")
                    wait(V, "st", cp_k)
                    ksl = (j % 2) * CW
                    cw_ = inc(V.tensor_scalar(krelu[:, ksl:ksl + CW], pslice, 0.0, None, ALU.max), "sv")
                    wait(V, "sv", cw_)
                    ii = V.tensor_tensor(kact[:, j * CW:(j + 1) * CW], krelu[:, ksl:ksl + CW],
                                         krelu[:, ksl:ksl + CW], ALU.mult)
                    gset(bank, "sv", inc(ii, "sv"))
                cp_kact = cnt["sv"]
                if ch + 1 < NCH:
                    emit_blends(ch + 1)
                wait(T, "sv", cp_kact)
                if ch == 0:
                    wait(T, "fr", cp_wr)
                    wait(T, "fv", cp_wv2)

                for j in range(NK):
                    rb_ = j % 2
                    bank_r = "s1a" if rb_ == 0 else "s1b"
                    psr = ps_s1[:, 0:512] if rb_ == 0 else ps_s1[:, 512:1024]
                    bank_v = "oA" if rb_ == 0 else "oB"
                    psv = ps_o[:, 0:512] if rb_ == 0 else ps_o[:, 512:1024]
                    gwait(T, bank_r)
                    for k in range(NK):
                        ii = T.matmul(psr, wr_sb[:, k * 1024 + j * 128: k * 1024 + (j + 1) * 128],
                                      xrb[:, sb_ + k * CW: sb_ + (k + 1) * CW],
                                      start=(k == 0), stop=(k == NK - 1))
                    cp_r = inc(ii, "st")
                    gwait(T, bank_v)
                    for k in range(NK):
                        ii = T.matmul(psv, wv2_sb[:, k * 1024 + j * 128: k * 1024 + (j + 1) * 128],
                                      kact[:, k * CW:(k + 1) * CW], start=(k == 0), stop=(k == NK - 1))
                    cp_v2 = inc(ii, "st")
                    wait(Sc, "st", cp_r)
                    gwait(Sc, ("rblk", rb_))
                    cp_sig = inc(Sc.activation(rblk[:, rb_ * CW:(rb_ + 1) * CW], psr, AF.Sigmoid), "ss")
                    gset(bank_r, "ss", cp_sig)
                    wait(V, "ss", cp_sig)
                    wait(V, "st", cp_v2)
                    if j > 0:
                        wait(V, "sv", cp_y)
                    cw_ = inc(V.tensor_tensor(t2f[:, :], rblk[:, rb_ * CW:(rb_ + 1) * CW], psv, ALU.mult), "sv")
                    wait(V, "sv", cw_)
                    ii = V.tensor_tensor(x1f[:, j * S + ch * CW: j * S + (ch + 1) * CW],
                                         x1f[:, j * S + ch * CW: j * S + (ch + 1) * CW],
                                         t2f[:, :], ALU.add)
                    cp_y = inc(ii, "sv")
                    gset(bank_v, "sv", cp_y)
                    gset(("rblk", rb_), "sv", cp_y)
                gset(("xkb", ch % 2), "st", cp_k)
                gset(("xrb", ch % 2), "st", cp_r)
                gset("kact", "st", cp_v2)
                wait(Y, "sv", cp_y)
                inc(Y.dma_start(
                    bass.AP(yT, ch * CW, [[S, 128], [128 * S, NK], [1, CW]]),
                    bass.AP(x1f, ch * CW, [[NK * S, 128], [S, NK], [1, CW]])), "dy", 16)

        Y.wait_ge(sems["dy"], cnt["dy"])

    ctx.close()
    return nc


def kernel(L0_attn_x, L0_attn_Wqkv_w, L0_attn_Wqkv_b, L0_attn_out_w, L0_attn_out_b,
           L0_attn_distance_decay, L0_ffn_W_r, L0_ffn_W_k, L0_ffn_W_v, L0_ffn_mu_r, L0_ffn_mu_k):
    x = np.asarray(L0_attn_x, dtype=np.float32)
    wqkvT = np.ascontiguousarray(np.asarray(L0_attn_Wqkv_w, np.float32).T)
    bq = np.asarray(L0_attn_Wqkv_b, np.float32)
    bqkv = np.ascontiguousarray(bq.reshape(24, 128).T)
    bqkv_row = np.ascontiguousarray(bq.reshape(1, 3 * C))
    woT = np.ascontiguousarray(np.asarray(L0_attn_out_w, np.float32).T)
    bo = np.ascontiguousarray(np.asarray(L0_attn_out_b, np.float32).reshape(8, 128).T)
    ddv = np.asarray(L0_attn_distance_decay, np.float32).reshape(1, 1)
    wrT = np.ascontiguousarray(np.asarray(L0_ffn_W_r, np.float32).T)
    wkT = np.ascontiguousarray(np.asarray(L0_ffn_W_k, np.float32).T)
    wvT2 = np.ascontiguousarray(np.asarray(L0_ffn_W_v, np.float32).T)
    mur = np.ascontiguousarray(np.asarray(L0_ffn_mu_r, np.float32).reshape(8, 128).T)
    muk = np.ascontiguousarray(np.asarray(L0_ffn_mu_k, np.float32).reshape(8, 128).T)
    p = np.arange(128, dtype=np.float32)[:, None]
    u = np.arange(STRIPW, dtype=np.float32)[None, :]
    dist = np.abs(p - u + U0).astype(np.float32)
    lin = np.broadcast_to(np.arange(S, dtype=np.float32)[None, :], (128, S)).copy()
    lr = np.maximum(1536.0 - np.arange(1544, dtype=np.float32), 0.0)
    linr = np.broadcast_to(lr[None, :], (128, 1544)).copy()

    phases = os.environ.get("KPHASES", "123")
    if phases not in _CACHED:
        _CACHED[phases] = build_nc(phases)
    nc = _CACHED[phases]

    shared = dict(wqkvT=wqkvT, bqkv=bqkv, bqkv_row=bqkv_row, woT=woT, bo=bo, ddv=ddv,
                  wrT=wrT, wkT=wkT, wvT2=wvT2, mur=mur, muk=muk, dist=dist, lin=lin, linr=linr)
    in_maps = []
    for b in range(8):
        m = dict(shared)
        m["xT"] = np.ascontiguousarray(x[b].T)
        in_maps.append(m)

    res = run_bass_kernel_spmd(nc, in_maps, core_ids=list(range(8)))
    out = np.stack([np.ascontiguousarray(r["yT"].T) for r in res.results], axis=0)
    return out.astype(np.float32)


# version 30
# speedup vs baseline: 1.0076x; 1.0003x over previous
import contextlib
import os
import numpy as np
import concourse.bass as bass
import concourse.mybir as mybir
from concourse.bass_utils import run_bass_kernel_spmd

F32 = mybir.dt.float32
BF16 = mybir.dt.bfloat16
AF = mybir.ActivationFunctionType
ALU = mybir.AluOpType

S = 2048
C = 1024
NH = 16
HD = 64
NP = 8
NCH = 4
CW = 512
NT = 16
NK = 8
EPS = 1e-5
U0 = 1920
STRIPW = 3968

_CACHED = {}


def build_nc(phases="123"):
    p1parts = os.environ.get("KP1", "all")
    p1ch = int(os.environ.get("KPCH", "4"))
    statlvl = int(os.environ.get("KPSTAT", "5"))
    nc = bass.Bass("TRN2", target_bir_lowering=False)

    xT = nc.dram_tensor("xT", [C, S], F32, kind="ExternalInput")
    wqkvT = nc.dram_tensor("wqkvT", [C, 3 * C], F32, kind="ExternalInput")
    bqkv = nc.dram_tensor("bqkv", [128, 24], F32, kind="ExternalInput")
    bqkv_row = nc.dram_tensor("bqkv_row", [1, 3 * C], F32, kind="ExternalInput")
    woT = nc.dram_tensor("woT", [C, C], F32, kind="ExternalInput")
    bo = nc.dram_tensor("bo", [128, 8], F32, kind="ExternalInput")
    ddv = nc.dram_tensor("ddv", [1, 1], F32, kind="ExternalInput")
    wrT = nc.dram_tensor("wrT", [C, C], F32, kind="ExternalInput")
    wkT = nc.dram_tensor("wkT", [C, C], F32, kind="ExternalInput")
    wvT2 = nc.dram_tensor("wvT2", [C, C], F32, kind="ExternalInput")
    mur = nc.dram_tensor("mur", [128, 8], F32, kind="ExternalInput")
    muk = nc.dram_tensor("muk", [128, 8], F32, kind="ExternalInput")
    dist = nc.dram_tensor("dist", [128, STRIPW], F32, kind="ExternalInput")
    lin = nc.dram_tensor("lin", [128, S], F32, kind="ExternalInput")
    linr = nc.dram_tensor("linr", [128, 1544], F32, kind="ExternalInput")
    kdbg = os.environ.get("KDBG", "0") == "1"
    if kdbg:
        dbg_etab = nc.dram_tensor("dbg_etab", [128, S], F32, kind="ExternalOutput")
        dbg_fmt = nc.dram_tensor("dbg_fmt", [128, 512], F32, kind="ExternalOutput")
        dbg_fpt = nc.dram_tensor("dbg_fpt", [128, 512], F32, kind="ExternalOutput")
        dbg_ks = nc.dram_tensor("dbg_ks", [128, S], F32, kind="ExternalOutput")
        dbg_qp = nc.dram_tensor("dbg_qp", [128, 1024], F32, kind="ExternalOutput")
        dbg_p = nc.dram_tensor("dbg_p", [128, 2048], F32, kind="ExternalOutput")
        dbg_p2 = nc.dram_tensor("dbg_p2", [128, 2048], F32, kind="ExternalOutput")
        dbg_och = nc.dram_tensor("dbg_och", [128, 4096], F32, kind="ExternalOutput")
    yT = nc.dram_tensor("yT", [C, S], F32, kind="ExternalOutput")

    ctx = contextlib.ExitStack()
    E = ctx.enter_context

    sems = {}
    cnt = {}
    for nm in ("dq", "dg", "dy", "st", "sv", "ss", "w0", "w1", "v0", "v1", "o0", "o1", "dx0", "dx1", "dx2", "dx3", "fk", "fr", "fv"):
        sems[nm] = E(nc.semaphore(nm))
        cnt[nm] = 0

    def inc(instr, nm, k=1):
        instr.then_inc(sems[nm], k)
        cnt[nm] += k
        return cnt[nm]

    waited = {}

    def wait(eng, nm, val):
        if val > 0 and waited.get((id(eng), nm), 0) < val:
            eng.wait_ge(sems[nm], val)
            waited[(id(eng), nm)] = val

    guard = {}

    def gwait(eng, *keys):
        for k in keys:
            if k in guard:
                wait(eng, *guard[k])

    def gset(key, nm, val):
        guard[key] = (nm, val)

    def barrier():
        vals = {nm: cnt[nm] for nm in ("st", "sv", "ss", "dq", "dg", "w0", "w1", "v0", "v1",
                                        "o0", "o1", "dx0", "dx1", "dx2", "dx3", "fk", "fr", "fv")}
        for eng in (nc.tensor, nc.vector, nc.scalar, nc.sync, nc.gpsimd):
            for nm, v in vals.items():
                wait(eng, nm, v)

    x1f = E(nc.sbuf_tensor("x1f", [128, NK * S], F32))
    bqkv_sb = E(nc.sbuf_tensor("bqkv_sb", [128, 24], F32))
    bo_sb = E(nc.sbuf_tensor("bo_sb", [128, 8], F32))
    mur_sb = E(nc.sbuf_tensor("mur_sb", [128, 8], F32))
    muk_sb = E(nc.sbuf_tensor("muk_sb", [128, 8], F32))
    dd_sb = E(nc.sbuf_tensor("dd_sb", [1, 1], F32))
    absd = E(nc.sbuf_tensor("absd", [1, 1], F32))
    nega = E(nc.sbuf_tensor("nega", [128, 1], F32))
    ones_c = E(nc.sbuf_tensor("ones_c", [128, 1], BF16))
    onesr_bf = E(nc.sbuf_tensor("onesr_bf", [1, 128], BF16))
    onesr32 = E(nc.sbuf_tensor("onesr32", [1, 128], F32))

    ps_s0 = E(nc.psum_tensor("ps_s0", [128, 1024], F32))
    ps_s1 = E(nc.psum_tensor("ps_s1", [128, 1024], F32))
    ps_o = E(nc.psum_tensor("ps_o", [128, 1024], F32))
    ps_mA = E(nc.psum_tensor("ps_mA", [128, 512], F32))
    ps_mB = E(nc.psum_tensor("ps_mB", [128, 512], F32))

    blk = E(nc.Block())

    @blk.sync
    def _(sync):
        T, V, Sc, Y, G = nc.tensor, nc.vector, nc.scalar, nc.sync, nc.gpsimd

        cp_xc = {}
        for xc in range(NCH):
            cp_xc[xc] = (f"dx{xc}", inc(Y.dma_start(
                bass.AP(x1f, xc * CW, [[NK * S, 128], [S, NK], [1, CW]]),
                bass.AP(xT, xc * CW, [[S, 128], [128 * S, NK], [1, CW]])), f"dx{xc}", 16))
        inc(Y.dma_start(bqkv_sb[:, :], bqkv[:, :]), "dq", 16)
        inc(Y.dma_start(bo_sb[:, :], bo[:, :]), "dq", 16)
        inc(Y.dma_start(mur_sb[:, :], mur[:, :]), "dq", 16)
        inc(Y.dma_start(muk_sb[:, :], muk[:, :]), "dq", 16)
        cp_dd = inc(Y.dma_start(dd_sb[:, :], ddv[:, :]), "dq", 16)

        V.memset(ones_c[:, :], 1.0)
        V.memset(onesr_bf[:, :], 1.0)
        cp_vones = inc(V.memset(onesr32[:, :], 1.0), "sv")

        wait(Sc, "dq", cp_dd)
        cp_abs = inc(Sc.activation(absd[:, :], dd_sb[:, :], AF.Abs), "ss")
        wait(T, "ss", cp_abs)
        wait(T, "sv", cp_vones)
        cp_negb = inc(T.matmul(ps_mA[0:128, 0:1], onesr32[:, :], absd[:, :], start=True, stop=True), "st")
        wait(V, "st", cp_negb)
        cp_nega = inc(V.tensor_scalar(nega[:, :], ps_mA[:, 0:1], -1.0, None, ALU.mult), "sv")
        gset("mA", "sv", cp_nega)

        kqv = contextlib.ExitStack()
        KT = kqv.enter_context(nc.sbuf_tensor("KT", [128, NK * S], BF16))
        QT = kqv.enter_context(nc.sbuf_tensor("QT", [128, NK * S], BF16))
        Vaug = kqv.enter_context(nc.sbuf_tensor("Vaug", [128, NT * 1024], BF16))

        with contextlib.ExitStack() as p1:
            E1 = p1.enter_context
            hTc = E1(nc.sbuf_tensor("hTc", [128, 2 * NK * CW], BF16))
            xtmp = E1(nc.sbuf_tensor("xtmp", [128, NK * 256], BF16))
            t1f = E1(nc.sbuf_tensor("t1f", [128, 512], F32))
            wblk = E1(nc.sbuf_tensor("wblk", [128, 2 * 2048], BF16))
            wvq = E1(nc.sbuf_tensor("wvq", [128, 2 * 2048], BF16))
            bvrow_bf = E1(nc.sbuf_tensor("bvrow_bf", [1, C], BF16))
            rows = E1(nc.sbuf_tensor("rows", [1, 1536], F32))

            cp_bv = inc(G.dma_start(bvrow_bf[0:1, :],
                                    bass.AP(bqkv_row, 2 * C, [[3 * C, 1], [1, C]])), "dg", 16)

            cp_qk = {}
            cp_wdma = {}
            cp_h = {}
            wait(V, "dq", cp_dd)
            wait(T, "dq", cp_dd)

            def ln1_half(ch, hf, first):
                wait(V, cp_xc[ch][0], cp_xc[ch][1])
                gwait(V, "xtmp")
                if first:
                    gwait(V, "mAr")
                for k in range(NK):
                    ii = V.tensor_copy(xtmp[:, k * 256:(k + 1) * 256],
                                       x1f[:, k * S + ch * CW + hf * 256: k * S + ch * CW + hf * 256 + 256])
                cp_xbf = inc(ii, "sv")
                wait(T, "sv", cp_xbf)
                if first:
                    gwait(T, "mA")
                for k in range(NK):
                    ii = T.matmul(ps_mA[0:1, hf * 256: hf * 256 + 256], ones_c[:, :],
                                  xtmp[:, k * 256:(k + 1) * 256],
                                  start=(k == 0), stop=(k == NK - 1))
                cp_sx = inc(ii, "st")
                wait(V, "st", cp_sx)
                for k in range(NK):
                    ii = V.tensor_tensor(xtmp[:, k * 256:(k + 1) * 256], xtmp[:, k * 256:(k + 1) * 256],
                                         xtmp[:, k * 256:(k + 1) * 256], ALU.mult)
                cp_xsq = inc(ii, "sv")
                wait(T, "sv", cp_xsq)
                if first:
                    gwait(T, "mB")
                for k in range(NK):
                    ii = T.matmul(ps_mB[0:1, hf * 256: hf * 256 + 256], ones_c[:, :],
                                  xtmp[:, k * 256:(k + 1) * 256],
                                  start=(k == 0), stop=(k == NK - 1))
                cp_sxx = inc(ii, "st")
                gset("xtmp", "st", cp_sxx)
                return cp_sxx

            def ln1_cast_sx(ch):
                return ln1_half(ch, 0, True)

            def ln1_sq_sxx(ch, cp_a):
                return ln1_half(ch, 1, False)

            def ln1_fin_bc(ch, cp_sxx):
                ra, rb, rc = rows[0:1, 0:512], rows[0:1, 512:1024], rows[0:1, 1024:1536]
                wait(V, "st", cp_sxx)
                V.tensor_scalar(ra, ps_mA[0:1, 0:512], 1.0 / C, None, ALU.mult)
                cw = inc(V.tensor_scalar(rc, ps_mB[0:1, 0:512], 1.0 / C, EPS, ALU.mult, ALU.add), "sv")
                wait(V, "sv", cw)
                cw = inc(V.tensor_tensor(rb, ra, ra, ALU.mult), "sv")
                wait(V, "sv", cw)
                cw = inc(V.tensor_tensor(rb, rc, rb, ALU.subtract), "sv")
                wait(V, "sv", cw)
                cp_rv = inc(V.reciprocal(rc, rb), "sv")
                wait(Sc, "sv", cp_rv)
                cp_rstd = inc(Sc.activation(rb, rc, AF.Sqrt), "ss")
                wait(T, "ss", cp_rstd)
                T.matmul(ps_mA[:, 0:512], onesr32[:, :], ra, start=True, stop=True)
                return inc(T.matmul(ps_mB[:, 0:512], onesr32[:, :], rb, start=True, stop=True), "st")

            def ln1_apply(ch, cp_bc):
                hb = (ch % 2) * NK * CW
                wait(V, "st", cp_bc)
                gwait(V, ("hTc", ch % 2))
                pairs = [(k, hf) for k in range(NK) for hf in range(2)]
                subcp = {}
                mulcp = {}

                def _sub(i):
                    k, hf = pairs[i]
                    sl = (i % 2) * 256
                    if i >= 2:
                        wait(V, "sv", mulcp[i - 2])
                    subcp[i] = inc(V.tensor_tensor(
                        t1f[:, sl:sl + 256],
                        x1f[:, k * S + ch * CW + hf * 256: k * S + ch * CW + hf * 256 + 256],
                        ps_mA[:, hf * 256:(hf + 1) * 256], ALU.subtract), "sv")

                def _mul(i, guard):
                    k, hf = pairs[i]
                    sl = (i % 2) * 256
                    if guard:
                        wait(V, "sv", subcp[i])
                    ii = V.tensor_tensor(
                        hTc[:, hb + k * CW + hf * 256: hb + k * CW + hf * 256 + 256],
                        t1f[:, sl:sl + 256], ps_mB[:, hf * 256:(hf + 1) * 256], ALU.mult)
                    mulcp[i] = inc(ii, "sv")
                    return ii

                _sub(0)
                for i in range(1, len(pairs)):
                    _sub(i)
                    _mul(i - 1, i == 1)
                ii = _mul(len(pairs) - 1, True)
                cp = mulcp[len(pairs) - 1]
                cp_h[ch] = cp
                gset("mAr", "sv", cp)
                gset("mA", "sv", cp)
                gset("mB", "sv", cp)
                return cp

            def emit_ln1_full(ch):
                a = ln1_cast_sx(ch)
                b = ln1_sq_sxx(ch, a)
                c_ = ln1_fin_bc(ch, b)
                ln1_apply(ch, c_)

            emit_ln1_full(0)

            for ch in range(NCH):
                hb = (ch % 2) * NK * CW
                for jp in range(2):
                    b = jp % 2
                    gwait(G, ("wblk", b))
                    cp_wdma[jp] = (f"w{b}", inc(G.dma_start(
                        bass.AP(wblk, b * 2048, [[4096, 128], [256, NK], [1, 256]]),
                        bass.AP(wqkvT, jp * 256, [[3 * C, 128], [128 * 3 * C, NK], [1, 256]])), f"w{b}", 16))
                cp_wv = {}
                for q in range(2):
                    b = q % 2
                    gwait(G, ("wvq", b))
                    cp_wv[q] = (f"v{b}", inc(G.dma_start(
                        bass.AP(wvq, b * 2048, [[4096, 128], [256, NK], [1, 256]]),
                        bass.AP(wqkvT, 2 * C + q * 256, [[3 * C, 128], [128 * 3 * C, NK], [1, 256]])),
                        f"v{b}", 16))

                wait(T, "sv", cp_h[ch])
                lnstate = {}
                for j in range(16):
                    jp, jh = j // 2, j % 2
                    b = jp % 2
                    bank = "s0a" if (j % 2 == 0) else "s0b"
                    pslice = ps_s0[:, 0:512] if (j % 2 == 0) else ps_s0[:, 512:1024]
                    if jh == 0:
                        wait(T, cp_wdma[jp][0], cp_wdma[jp][1])
                    gwait(T, bank)
                    for k in range(NK):
                        ii = T.matmul(pslice,
                                      wblk[:, b * 2048 + k * 256 + jh * 128: b * 2048 + k * 256 + jh * 128 + 128],
                                      hTc[:, hb + k * CW: hb + k * CW + CW], start=(k == 0), stop=(k == NK - 1))
                    cp_qk[j] = inc(ii, "st")
                    if jh == 1:
                        gset(("wblk", b), "st", cp_qk[j])
                        if jp + 2 < 8:
                            gwait(G, ("wblk", (jp + 2) % 2))
                            cp_wdma[jp + 2] = (f"w{(jp + 2) % 2}", inc(G.dma_start(
                                bass.AP(wblk, ((jp + 2) % 2) * 2048, [[4096, 128], [256, NK], [1, 256]]),
                                bass.AP(wqkvT, (jp + 2) * 256, [[3 * C, 128], [128 * 3 * C, NK], [1, 256]])),
                                f"w{(jp + 2) % 2}", 16))
                    tgt = QT if j < 8 else KT
                    jj = j % 8
                    wait(V, "st", cp_qk[j])
                    ii = V.tensor_scalar(tgt[:, jj * S + ch * CW: jj * S + (ch + 1) * CW],
                                         pslice, bqkv_sb[:, j:j + 1], None, ALU.add)
                    gset(bank, "sv", inc(ii, "sv"))
                    if ch + 1 < NCH:
                        if j == 5:
                            lnstate["a"] = ln1_cast_sx(ch + 1)
                        elif j == 8:
                            lnstate["b"] = ln1_sq_sxx(ch + 1, lnstate["a"])
                        elif j == 11:
                            lnstate["c"] = ln1_fin_bc(ch + 1, lnstate["b"])
                        elif j == 14:
                            ln1_apply(ch + 1, lnstate["c"])

                if ch == 0:
                    wait(T, "dg", cp_bv)
                for q in range(4):
                    b = q % 2
                    wait(T, cp_wv[q][0], cp_wv[q][1])
                    for tt in range(4):
                        tau = ch * 4 + tt
                        vb_ = (q * 4 + tt) % 2
                        bank = "s1a" if vb_ == 0 else "s1b"
                        pslice = ps_s1[:, 0:256] if vb_ == 0 else ps_s1[:, 512:768]
                        gwait(T, bank)
                        T.matmul(pslice, onesr_bf[:, :], bvrow_bf[0:1, q * 256:(q + 1) * 256],
                                 start=True, stop=False)
                        for k in range(NK):
                            ii = T.matmul(pslice, hTc[:, hb + k * CW + tt * 128: hb + k * CW + tt * 128 + 128],
                                          wvq[:, b * 2048 + k * 256: b * 2048 + (k + 1) * 256],
                                          start=False, stop=(k == NK - 1))
                        cp_v = inc(ii, "st")
                        gset(("wvq", b), "st", cp_v)
                        wait(V, "st", cp_v)
                        ii = V.tensor_copy(Vaug[:, tau * 1024 + 256 * q: tau * 1024 + 256 * (q + 1)],
                                           ps_s1[:, vb_ * 512: vb_ * 512 + 256])
                        gset(bank, "sv", inc(ii, "sv"))
                    if q + 2 < 4:
                        bnext = (q + 2) % 2
                        gwait(G, ("wvq", bnext))
                        cp_wv[q + 2] = (f"v{bnext}", inc(G.dma_start(
                            bass.AP(wvq, bnext * 2048, [[4096, 128], [256, NK], [1, 256]]),
                            bass.AP(wqkvT, 2 * C + (q + 2) * 256,
                                    [[3 * C, 128], [128 * 3 * C, NK], [1, 256]])), f"v{bnext}", 16))
                gset(("hTc", ch % 2), "st", cp_v)

        barrier()
        if "2" in phases:
          with contextlib.ExitStack() as p2:
            E2 = p2.enter_context
            SW2 = 896
            strip = E2(nc.sbuf_tensor("strip", [128, SW2], F32))
            etab = E2(nc.sbuf_tensor("etab", [128, S], BF16))
            rtab = E2(nc.sbuf_tensor("rtab", [128, 1544], BF16))
            fmt = E2(nc.sbuf_tensor("fmt", [128, 512], BF16))
            fpt = E2(nc.sbuf_tensor("fpt", [128, 512], BF16))
            kscale = E2(nc.sbuf_tensor("kscale", [128, S], BF16))
            qm = E2(nc.sbuf_tensor("qm", [128, 2 * 512], BF16))
            qp = E2(nc.sbuf_tensor("qp", [128, 2 * 512], BF16))
            Pbuf = E2(nc.sbuf_tensor("Pbuf", [128, 3 * 1024], BF16))
            Och = E2(nc.sbuf_tensor("Och", [128, NK * 512], BF16))
            wo_sb = E2(nc.sbuf_tensor("wo_sb", [128, 2 * 1024], BF16))
            recip_sb = E2(nc.sbuf_tensor("recip_sb", [128, 512], F32))
            ones32d = E2(nc.sbuf_tensor("ones32d", [128, 32], BF16))
            ind = E2(nc.sbuf_tensor("ind", [128, 128], F32))

            V.memset(recip_sb[:, :], 0.0)
            cw0 = inc(V.memset(ones32d[:, :], 0.0), "sv")
            wait(V, "sv", cw0)
            V.memset(ones32d[:, 0:1], 1.0)
            cw0 = inc(V.memset(ind[:, :], 0.0), "sv")
            wait(V, "sv", cw0)
            V.memset(ind[32:33, 0:64], 1.0)
            cp_rz = inc(V.memset(ind[0:1, 64:128], 1.0), "sv")
            wait(T, "sv", cp_rz)
            cp_dist = inc(Y.dma_start(strip[:, :],
                                      bass.AP(dist, 1536, [[STRIPW, 128], [1, SW2]])), "dq", 16)
            cp_lin = inc(G.dma_start(etab[:, :], lin[:, :]), "dg", 16)
            cp_linr = inc(G.dma_start(rtab[:, :], linr[:, :]), "dg", 16)
            wait(V, "dq", cp_dist)
            cp_s1 = inc(V.tensor_scalar(strip[:, :], strip[:, :], nega[:, 0:1], None, ALU.mult), "sv")
            wait(V, "dg", cp_linr)
            V.tensor_scalar(etab[:, :], etab[:, :], nega[:, 0:1], None, ALU.mult)
            cp_e1 = inc(V.tensor_scalar(rtab[:, :], rtab[:, :], nega[:, 0:1], None, ALU.mult), "sv")
            wait(Sc, "sv", cp_e1)
            cp_s2 = inc(Sc.activation(strip[:, :], strip[:, :], AF.Exp), "ss")
            Sc.activation(etab[:, :], etab[:, :], AF.Exp)
            cp_e2 = inc(Sc.activation(rtab[:, :], rtab[:, :], AF.Exp), "ss")
            wait(V, "ss", cp_e2)
            cp_strip = inc(V.tensor_scalar(strip[:, :], strip[:, :], 0.125, None, ALU.mult), "sv")
            V.tensor_scalar(fmt[:, :], etab[:, 0:512], 0.125, None, ALU.mult)
            cp_fmt = inc(V.tensor_scalar(fpt[:, :], rtab[:, 1025:1537], 0.125, None, ALU.mult), "sv")
            wait(T, "sv", cp_fmt)
            wait(Sc, "sv", cp_fmt)
            if kdbg:
                wait(G, "sv", cp_fmt)
                inc(G.dma_start(dbg_etab[:, :], etab[:, :]), "dg", 16)
                inc(G.dma_start(dbg_fmt[:, :], fmt[:, :]), "dg", 16)
                inc(G.dma_start(dbg_fpt[:, :], fpt[:, :]), "dg", 16)

            cp_wo = {}
            cp_oev = 0
            cp_smm_last = -1
            for ch in range(NCH):
                s0 = ch * CW
                lo_end = 4 * ch
                hi_beg = 4 * ch + 4
                for co in range(2):
                    b = co % 2
                    gwait(G, ("wo", b))
                    cp_wo[co] = (f"o{b}", inc(G.dma_start(
                        bass.AP(wo_sb, b * 1024, [[2048, 128], [128, NK], [1, 128]]),
                        bass.AP(woT, co * 128, [[C, 128], [128 * C, NK], [1, 128]])), f"o{b}", 16))

                def build_klo(gg):
                    return inc(V.tensor_tensor(
                        kscale[:, 0:s0], KT[:, gg * S: gg * S + s0],
                        bass.AP(rtab, 1536 - s0, [[1544, 128], [1, s0]]), ALU.mult), "sv")

                def build_khi(gg):
                    n = S - (s0 + CW)
                    return inc(V.tensor_tensor(
                        kscale[:, s0 + CW: S], KT[:, gg * S + s0 + CW: (gg + 1) * S],
                        bass.AP(etab, 1, [[S, 128], [1, n]]), ALU.mult), "sv")

                def build_q(gg):
                    sl = gg % 2
                    a = b_ = None
                    if lo_end > 0:
                        a = inc(V.tensor_tensor(qm[:, sl * 512:(sl + 1) * 512],
                                                QT[:, gg * S + s0: gg * S + s0 + CW],
                                                fmt[:, :], ALU.mult), "sv")
                    if hi_beg < NT:
                        b_ = inc(V.tensor_tensor(qp[:, sl * 512:(sl + 1) * 512],
                                                 QT[:, gg * S + s0: gg * S + s0 + CW],
                                                 fpt[:, :], ALU.mult), "sv")
                    return b_ or a

                if lo_end > 0:
                    wait(V, "st", cp_smm_last)
                    cp_klo = build_klo(0)
                else:
                    cp_klo = 0
                gwait(V, ("q", 0))
                if ch == 0:
                    wait(V, "sv", cp_fmt)
                cp_q = build_q(0)
                cp_khi = 0
                pending = None
                pend_cpb = None
                pend_cpc = [0]

                for g in range(NP):
                    cp_exp = {}
                    cp_pre = {}
                    cp_av = {}
                    ob = "oA" if g % 2 == 0 else "oB"
                    mb = "mA" if g % 2 == 0 else "mB"
                    pm = ps_mA if g % 2 == 0 else ps_mB
                    oc0 = 0 if g % 2 == 0 else 512
                    qsl = g % 2
                    cp_smm_lo = 0
                    cp_smm = 0
                    prev_smm_last = cp_smm_last

                    def emit_smm(tau):
                        nonlocal cp_smm_lo, cp_smm, cp_klo, cp_khi, cp_q
                        sl = tau % 2
                        pst = ps_s0 if sl == 0 else ps_s1
                        bko = ("s0a", "s0b") if sl == 0 else ("s1a", "s1b")
                        crossing = lo_end <= tau < hi_beg
                        gwait(T, bko[0], bko[1])
                        if tau < lo_end:
                            if tau == 0:
                                wait(T, "sv", max(cp_klo, cp_q))
                            lh, rh = kscale, qm
                            lo_, ro_ = tau * 128, qsl * 512
                        elif tau >= hi_beg:
                            if tau == hi_beg:
                                wait(T, "sv", max(cp_khi, cp_q))
                            lh, rh = kscale, qp
                            lo_, ro_ = tau * 128, qsl * 512
                        else:
                            lh, rh = KT, QT
                            lo_, ro_ = g * S + tau * 128, g * S + s0
                        T.matmul(pst[:, 0:512], lh[0:64, lo_: lo_ + 128], rh[0:64, ro_: ro_ + CW],
                                 start=True, stop=True, tile_position=(0, 0))
                        cp_smm = inc(T.matmul(pst[:, 512:1024], lh[64:128, lo_: lo_ + 128],
                                              rh[64:128, ro_: ro_ + CW],
                                              start=True, stop=True, tile_position=(64, 0)), "st")
                        if tau == lo_end - 1:
                            cp_smm_lo = cp_smm
                        if crossing:
                            off = 384 - (tau * 128 - s0)
                            wait(V, "st", cp_smm)
                            cp_mult = inc(V.tensor_tensor(
                                pst[:, 0:1024], pst[:, 0:1024],
                                bass.AP(strip, off, [[SW2, 128], [0, 2], [1, 512]]), ALU.mult), "sv")
                            if tau == lo_end:
                                if hi_beg < NT:
                                    wait(V, "st", prev_smm_last)
                                    cp_khi = build_khi(g)
                            elif tau == lo_end + 1 and g + 1 < NP:
                                if lo_end > 0:
                                    wait(V, "st", cp_smm_lo)
                                    cp_klo = build_klo(g + 1)
                            elif tau == lo_end + 2 and g + 1 < NP:
                                gwait(V, ("q", (g + 1) % 2))
                                cp_q = build_q(g + 1)
                            cp_pre[tau] = ("sv", cp_mult)
                        else:
                            cp_pre[tau] = ("st", cp_smm)

                    def emit_exp(tau):
                        sl = tau % 2
                        slp = tau % 3
                        pst = ps_s0 if sl == 0 else ps_s1
                        bko = ("s0a", "s0b") if sl == 0 else ("s1a", "s1b")
                        nm_, v_ = cp_pre[tau]
                        wait(Sc, nm_, v_)
                        gwait(Sc, ("P", slp))
                        cp_exp[tau] = inc(Sc.activation(Pbuf[:, slp * 1024:(slp + 1) * 1024],
                                                        pst[:, 0:1024], AF.Exp), "ss")
                        gset(bko[0], "ss", cp_exp[tau])
                        gset(bko[1], "ss", cp_exp[tau])

                    def emit_av(tau):
                        slp = tau % 3
                        wait(T, "ss", cp_exp[tau])
                        if tau == 0:
                            gwait(T, ob, "mA")
                        T.matmul(ps_o[0:64, oc0:oc0 + 512],
                                 bass.AP(Vaug, tau * 1024 + 64 * (2 * g), [[NT * 1024, 128], [1, 64]]),
                                 Pbuf[:, slp * 1024: slp * 1024 + 512],
                                 start=(tau == 0), stop=(tau == NT - 1),
                                 skip_group_check=True)
                        T.matmul(ps_o[64:128, oc0:oc0 + 512],
                                 bass.AP(Vaug, tau * 1024 + 64 * (2 * g + 1), [[NT * 1024, 128], [1, 64]]),
                                 Pbuf[:, slp * 1024 + 512: (slp + 1) * 1024],
                                 start=(tau == 0), stop=(tau == NT - 1), tile_position=(0, 64),
                                 skip_group_check=True)
                        T.matmul(pm[32:64, 0:512], ones32d[:, :],
                                 Pbuf[:, slp * 1024: slp * 1024 + 512],
                                 start=(tau == 0), stop=(tau == NT - 1), tile_position=(0, 32),
                                 skip_group_check=True)
                        cp_av[tau] = inc(T.matmul(pm[0:32, 0:512], ones32d[:, :],
                                                  Pbuf[:, slp * 1024 + 512: (slp + 1) * 1024],
                                                  start=(tau == 0), stop=(tau == NT - 1),
                                                  tile_position=(0, 0),
                                                  skip_group_check=True), "st")
                        gset(("P", slp), "st", cp_av[tau])

                    def emit_tail_recip():
                        wait(V, "st", cp_av[NT - 1])
                        gwait(V, "recip")
                        wait(V, "sv", cp_oev)
                        V.reciprocal(recip_sb[32:33, 0:512], pm[32:33, 0:512])
                        return inc(V.reciprocal(recip_sb[0:1, 0:512], pm[0:1, 0:512]), "sv")

                    def emit_tail_bcast(pend):
                        pg, ppm, prc = pend
                        wait(T, "sv", prc)
                        cpb = inc(T.matmul(ppm[:, 0:512], ind[:, :],
                                           recip_sb[:, 0:512], start=True, stop=True), "st")
                        gset("recip", "st", cpb)
                        return cpb

                    def emit_tail_copy(pend, cpb):
                        pg, ppm, prc = pend
                        pmb = "mA" if pg % 2 == 0 else "mB"
                        wait(V, "st", cpb)
                        cpc = inc(V.tensor_copy(recip_sb[:, :], ppm[:, 0:512]), "sv")
                        gset(pmb, "sv", cpc)
                        pend_cpc[0] = cpc
                        return cpc

                    def emit_tail_evict(pend):
                        pg, ppm, prc = pend
                        poc0 = 0 if pg % 2 == 0 else 512
                        pob = "oA" if pg % 2 == 0 else "oB"
                        gwait(V, ("och", pg))
                        wait(V, "sv", pend_cpc[0])
                        cpo = inc(V.tensor_tensor(Och[:, pg * 512:(pg + 1) * 512],
                                                  ps_o[:, poc0:poc0 + 512],
                                                  recip_sb[:, :], ALU.mult), "sv")
                        gset(pob, "sv", cpo)
                        if kdbg and ch == 0 and pg == 1:
                            wait(G, "sv", cpo)
                            inc(G.dma_start(dbg_p2[:, :512], recip_sb[:, :]), "dg", 16)
                            inc(G.dma_start(dbg_och[:, :], Och[:, :]), "dg", 16)
                            inc(G.dma_start(dbg_p[:, :], Pbuf[:, :]), "dg", 16)
                        return cpo

                    hk = (5, 6, 7) if lo_end == 0 else (1, 2, 3)
                    for tau in range(NT):
                        emit_smm(tau)
                        if tau >= 2:
                            emit_av(tau - 2)
                        emit_exp(tau)
                        if pending is not None:
                            if tau == hk[0]:
                                pend_cpb = emit_tail_bcast(pending)
                            elif tau == hk[1]:
                                emit_tail_copy(pending, pend_cpb)
                            elif tau == hk[2]:
                                cp_oev = emit_tail_evict(pending)
                                pending = None
                    emit_av(NT - 2)
                    emit_av(NT - 1)
                    cp_smm_last = cp_smm
                    gset(("q", qsl), "st", cp_smm_last)
                    cp_rc = emit_tail_recip()
                    if kdbg and ch == 0 and g == 1:
                        wait(G, "sv", cp_rc)
                        cp_dk = inc(G.dma_start(dbg_p2[:, 512:1024], recip_sb[:, :]), "dg", 16)
                        wait(T, "dg", cp_dk)
                        inc(G.dma_start(dbg_p[:, :], Pbuf[:, :]), "dg", 16)
                    pending = (g, pm, cp_rc)
                    if g == NP - 1:
                        pend_cpb = emit_tail_bcast(pending)
                        emit_tail_copy(pending, pend_cpb)
                        cp_oev = emit_tail_evict(pending)
                        pending = None
                    if kdbg and ch == 0 and g == 0:
                        wait(G, "sv", cp_rc)
                        inc(G.dma_start(dbg_ks[:, :], kscale[:, :]), "dg", 16)
                        inc(G.dma_start(dbg_qp[:, :], qp[:, :]), "dg", 16)

                for co in range(8):
                    b = co % 2
                    bank = "s0a" if b == 0 else "s0b"
                    pslice = ps_s0[:, 0:512] if b == 0 else ps_s0[:, 512:1024]
                    wait(T, cp_wo[co][0], cp_wo[co][1])
                    wait(T, "sv", cp_oev)
                    gwait(T, bank)
                    for k in range(NK):
                        ii = T.matmul(pslice, wo_sb[:, b * 1024 + k * 128: b * 1024 + (k + 1) * 128],
                                      Och[:, k * 512:(k + 1) * 512], start=(k == 0), stop=(k == NK - 1))
                    cp_proj = inc(ii, "st")
                    gset(("wo", b), "st", cp_proj)
                    if co + 2 < 8:
                        gwait(G, ("wo", (co + 2) % 2))
                        cp_wo[co + 2] = (f"o{(co + 2) % 2}", inc(G.dma_start(
                            bass.AP(wo_sb, ((co + 2) % 2) * 1024, [[2048, 128], [128, NK], [1, 128]]),
                            bass.AP(woT, (co + 2) * 128, [[C, 128], [128 * C, NK], [1, 128]])),
                            f"o{(co + 2) % 2}", 16))
                    wait(V, "st", cp_proj)
                    ii = V.scalar_tensor_tensor(
                        x1f[:, co * S + ch * CW: co * S + (ch + 1) * CW], pslice, bo_sb[:, co:co + 1],
                        x1f[:, co * S + ch * CW: co * S + (ch + 1) * CW], ALU.add, ALU.add)
                    cp_x1 = inc(ii, "sv")
                    gset(bank, "sv", cp_x1)
                for g in range(NP):
                    gset(("och", g), "st", cp_proj)

        barrier()
        kqv.close()
        if "3" not in phases:
            cp_dbg = inc(Y.dma_start(
                bass.AP(yT, 0, [[S, 128], [128 * S, NK], [1, S]]),
                bass.AP(x1f, 0, [[NK * S, 128], [S, NK], [1, S]])), "dy", 16)
        if "3" in phases:
          with contextlib.ExitStack() as p3:
            E3 = p3.enter_context
            h2b = E3(nc.sbuf_tensor("h2b", [128, NK * 2052], BF16))
            wr_sb = E3(nc.sbuf_tensor("wr_sb", [128, NK * 1024], BF16))
            wk_sb = E3(nc.sbuf_tensor("wk_sb", [128, NK * 1024], BF16))
            wv2_sb = E3(nc.sbuf_tensor("wv2_sb", [128, NK * 1024], BF16))
            xrb = E3(nc.sbuf_tensor("xrb", [128, 2 * NK * CW], BF16))
            xkb = E3(nc.sbuf_tensor("xkb", [128, 2 * NK * CW], BF16))
            kact = E3(nc.sbuf_tensor("kact", [128, NK * CW], BF16))
            xtmp2 = E3(nc.sbuf_tensor("xtmp2", [128, NK * 256], BF16))
            t1g = E3(nc.sbuf_tensor("t1g", [128, 512], F32))
            dtmp = E3(nc.sbuf_tensor("dtmp", [128, 2 * CW], BF16))
            krelu = E3(nc.sbuf_tensor("krelu", [128, 2 * CW], BF16))
            rblk = E3(nc.sbuf_tensor("rblk", [128, 2 * CW], F32))
            t2f = E3(nc.sbuf_tensor("t2f", [128, CW], F32))
            rows = E3(nc.sbuf_tensor("rows3", [1, 1536], F32))

            cp_wk = inc(G.dma_start(
                bass.AP(wk_sb, 0, [[NK * 1024, 128], [1024, NK], [1, 1024]]),
                bass.AP(wkT, 0, [[C, 128], [128 * C, NK], [1, 1024]])), "fk", 16)
            cp_wr = inc(G.dma_start(
                bass.AP(wr_sb, 0, [[NK * 1024, 128], [1024, NK], [1, 1024]]),
                bass.AP(wrT, 0, [[C, 128], [128 * C, NK], [1, 1024]])), "fr", 16)
            cp_wv2 = inc(G.dma_start(
                bass.AP(wv2_sb, 0, [[NK * 1024, 128], [1024, NK], [1, 1024]]),
                bass.AP(wvT2, 0, [[C, 128], [128 * C, NK], [1, 1024]])), "fv", 16)

            cp_hb = inc(V.memset(bass.AP(h2b, 3, [[NK * 2052, 128], [2052, NK], [1, 1]]), 0.0), "sv")

            for ch in range(NCH):
                for hf in range(2):
                    gwait(V, "xtmp2")
                    for k in range(NK):
                        ii = V.tensor_copy(xtmp2[:, k * 256:(k + 1) * 256],
                                           x1f[:, k * S + ch * CW + hf * 256: k * S + ch * CW + hf * 256 + 256])
                    cp_xbf = inc(ii, "sv")
                    wait(T, "sv", cp_xbf)
                    if hf == 0:
                        gwait(T, "mA")
                    for k in range(NK):
                        ii = T.matmul(ps_mA[0:1, hf * 256: hf * 256 + 256], ones_c[:, :],
                                      xtmp2[:, k * 256:(k + 1) * 256], start=(k == 0), stop=(k == NK - 1))
                    cp_sx = inc(ii, "st")
                    wait(V, "st", cp_sx)
                    for k in range(NK):
                        ii = V.tensor_tensor(xtmp2[:, k * 256:(k + 1) * 256], xtmp2[:, k * 256:(k + 1) * 256],
                                             xtmp2[:, k * 256:(k + 1) * 256], ALU.mult)
                    cp_xsq = inc(ii, "sv")
                    wait(T, "sv", cp_xsq)
                    if hf == 0:
                        gwait(T, "mB")
                    for k in range(NK):
                        ii = T.matmul(ps_mB[0:1, hf * 256: hf * 256 + 256], ones_c[:, :],
                                      xtmp2[:, k * 256:(k + 1) * 256], start=(k == 0), stop=(k == NK - 1))
                    cp_sxx = inc(ii, "st")
                    gset("xtmp2", "st", cp_sxx)
                ra, rb, rc = rows[0:1, 0:512], rows[0:1, 512:1024], rows[0:1, 1024:1536]
                wait(V, "st", cp_sxx)
                V.tensor_scalar(ra, ps_mA[0:1, 0:512], 1.0 / C, None, ALU.mult)
                cw = inc(V.tensor_scalar(rc, ps_mB[0:1, 0:512], 1.0 / C, EPS, ALU.mult, ALU.add), "sv")
                wait(V, "sv", cw)
                cw = inc(V.tensor_tensor(rb, ra, ra, ALU.mult), "sv")
                wait(V, "sv", cw)
                cw = inc(V.tensor_tensor(rb, rc, rb, ALU.subtract), "sv")
                wait(V, "sv", cw)
                cp_rv = inc(V.reciprocal(rc, rb), "sv")
                wait(Sc, "sv", cp_rv)
                cp_rstd = inc(Sc.activation(rb, rc, AF.Sqrt), "ss")
                wait(T, "ss", cp_rstd)
                T.matmul(ps_mA[:, 0:512], onesr32[:, :], ra, start=True, stop=True)
                cp_bc = inc(T.matmul(ps_mB[:, 0:512], onesr32[:, :], rb, start=True, stop=True), "st")
                wait(V, "st", cp_bc)
                if ch == 0:
                    wait(V, "sv", cp_hb)
                pairs = [(k, hf) for k in range(NK) for hf in range(2)]
                subcp = {}
                mulcp = {}

                def _sub(i):
                    k, hf = pairs[i]
                    sl = (i % 2) * 256
                    if i >= 2:
                        wait(V, "sv", mulcp[i - 2])
                    subcp[i] = inc(V.tensor_tensor(
                        t1g[:, sl:sl + 256],
                        x1f[:, k * S + ch * CW + hf * 256: k * S + ch * CW + hf * 256 + 256],
                        ps_mA[:, hf * 256: hf * 256 + 256], ALU.subtract), "sv")

                def _mul(i, guard):
                    k, hf = pairs[i]
                    sl = (i % 2) * 256
                    if guard:
                        wait(V, "sv", subcp[i])
                    ii = V.tensor_tensor(
                        h2b[:, k * 2052 + 4 + ch * CW + hf * 256: k * 2052 + 4 + ch * CW + hf * 256 + 256],
                        t1g[:, sl:sl + 256], ps_mB[:, hf * 256: hf * 256 + 256], ALU.mult)
                    mulcp[i] = inc(ii, "sv")
                    return ii

                _sub(0)
                for i in range(1, len(pairs)):
                    _sub(i)
                    _mul(i - 1, i == 1)
                ii = _mul(len(pairs) - 1, True)
                cp_h2 = mulcp[len(pairs) - 1]
                gset("mA", "sv", cp_h2)
                gset("mB", "sv", cp_h2)

            wait(T, "fk", cp_wk)
            cp_blend = {}

            def emit_blends(ch):
                sb_ = (ch % 2) * NK * CW
                gwait(V, ("xrb", ch % 2), ("xkb", ch % 2))
                for k in range(NK):
                    cur = h2b[:, k * 2052 + 4 + ch * CW: k * 2052 + 4 + (ch + 1) * CW]
                    prev = h2b[:, k * 2052 + 3 + ch * CW: k * 2052 + 3 + (ch + 1) * CW]
                    dsl = (k % 2) * CW
                    cw_ = inc(V.tensor_tensor(dtmp[:, dsl:dsl + CW], cur, prev, ALU.subtract), "sv")
                    wait(V, "sv", cw_)
                    V.scalar_tensor_tensor(xrb[:, sb_ + k * CW: sb_ + (k + 1) * CW], dtmp[:, dsl:dsl + CW],
                                           mur_sb[:, k:k + 1], prev, ALU.mult, ALU.add)
                    ii = V.scalar_tensor_tensor(xkb[:, sb_ + k * CW: sb_ + (k + 1) * CW], dtmp[:, dsl:dsl + CW],
                                                muk_sb[:, k:k + 1], prev, ALU.mult, ALU.add)
                cp_blend[ch] = inc(ii, "sv")

            emit_blends(0)
            for ch in range(NCH):
                sb_ = (ch % 2) * NK * CW
                gwait(V, "kact")
                wait(T, "sv", cp_blend[ch])
                for j in range(NK):
                    b = j % 2
                    bank = "s0a" if b == 0 else "s0b"
                    pslice = ps_s0[:, 0:512] if b == 0 else ps_s0[:, 512:1024]
                    gwait(T, bank)
                    for k in range(NK):
                        ii = T.matmul(pslice, wk_sb[:, k * 1024 + j * 128: k * 1024 + (j + 1) * 128],
                                      xkb[:, sb_ + k * CW: sb_ + (k + 1) * CW],
                                      start=(k == 0), stop=(k == NK - 1))
                    cp_k = inc(ii, "st")
                    wait(V, "st", cp_k)
                    ksl = (j % 2) * CW
                    cw_ = inc(V.tensor_scalar(krelu[:, ksl:ksl + CW], pslice, 0.0, None, ALU.max), "sv")
                    wait(V, "sv", cw_)
                    ii = V.tensor_tensor(kact[:, j * CW:(j + 1) * CW], krelu[:, ksl:ksl + CW],
                                         krelu[:, ksl:ksl + CW], ALU.mult)
                    gset(bank, "sv", inc(ii, "sv"))
                cp_kact = cnt["sv"]
                if ch + 1 < NCH:
                    emit_blends(ch + 1)
                wait(T, "sv", cp_kact)
                if ch == 0:
                    wait(T, "fr", cp_wr)
                    wait(T, "fv", cp_wv2)

                for j in range(NK):
                    rb_ = j % 2
                    bank_r = "s1a" if rb_ == 0 else "s1b"
                    psr = ps_s1[:, 0:512] if rb_ == 0 else ps_s1[:, 512:1024]
                    bank_v = "oA" if rb_ == 0 else "oB"
                    psv = ps_o[:, 0:512] if rb_ == 0 else ps_o[:, 512:1024]
                    gwait(T, bank_r)
                    for k in range(NK):
                        ii = T.matmul(psr, wr_sb[:, k * 1024 + j * 128: k * 1024 + (j + 1) * 128],
                                      xrb[:, sb_ + k * CW: sb_ + (k + 1) * CW],
                                      start=(k == 0), stop=(k == NK - 1))
                    cp_r = inc(ii, "st")
                    gwait(T, bank_v)
                    for k in range(NK):
                        ii = T.matmul(psv, wv2_sb[:, k * 1024 + j * 128: k * 1024 + (j + 1) * 128],
                                      kact[:, k * CW:(k + 1) * CW], start=(k == 0), stop=(k == NK - 1))
                    cp_v2 = inc(ii, "st")
                    wait(Sc, "st", cp_r)
                    gwait(Sc, ("rblk", rb_))
                    cp_sig = inc(Sc.activation(rblk[:, rb_ * CW:(rb_ + 1) * CW], psr, AF.Sigmoid), "ss")
                    gset(bank_r, "ss", cp_sig)
                    wait(V, "ss", cp_sig)
                    wait(V, "st", cp_v2)
                    if j > 0:
                        wait(V, "sv", cp_y)
                    cw_ = inc(V.tensor_tensor(t2f[:, :], rblk[:, rb_ * CW:(rb_ + 1) * CW], psv, ALU.mult), "sv")
                    wait(V, "sv", cw_)
                    ii = V.tensor_tensor(x1f[:, j * S + ch * CW: j * S + (ch + 1) * CW],
                                         x1f[:, j * S + ch * CW: j * S + (ch + 1) * CW],
                                         t2f[:, :], ALU.add)
                    cp_y = inc(ii, "sv")
                    gset(bank_v, "sv", cp_y)
                    gset(("rblk", rb_), "sv", cp_y)
                gset(("xkb", ch % 2), "st", cp_k)
                gset(("xrb", ch % 2), "st", cp_r)
                gset("kact", "st", cp_v2)
                wait(Y, "sv", cp_y)
                inc(Y.dma_start(
                    bass.AP(yT, ch * CW, [[S, 128], [128 * S, NK], [1, CW]]),
                    bass.AP(x1f, ch * CW, [[NK * S, 128], [S, NK], [1, CW]])), "dy", 16)

        Y.wait_ge(sems["dy"], cnt["dy"])

    ctx.close()
    return nc


def kernel(L0_attn_x, L0_attn_Wqkv_w, L0_attn_Wqkv_b, L0_attn_out_w, L0_attn_out_b,
           L0_attn_distance_decay, L0_ffn_W_r, L0_ffn_W_k, L0_ffn_W_v, L0_ffn_mu_r, L0_ffn_mu_k):
    x = np.asarray(L0_attn_x, dtype=np.float32)
    wqkvT = np.ascontiguousarray(np.asarray(L0_attn_Wqkv_w, np.float32).T)
    bq = np.asarray(L0_attn_Wqkv_b, np.float32)
    bqkv = np.ascontiguousarray(bq.reshape(24, 128).T)
    bqkv_row = np.ascontiguousarray(bq.reshape(1, 3 * C))
    woT = np.ascontiguousarray(np.asarray(L0_attn_out_w, np.float32).T)
    bo = np.ascontiguousarray(np.asarray(L0_attn_out_b, np.float32).reshape(8, 128).T)
    ddv = np.asarray(L0_attn_distance_decay, np.float32).reshape(1, 1)
    wrT = np.ascontiguousarray(np.asarray(L0_ffn_W_r, np.float32).T)
    wkT = np.ascontiguousarray(np.asarray(L0_ffn_W_k, np.float32).T)
    wvT2 = np.ascontiguousarray(np.asarray(L0_ffn_W_v, np.float32).T)
    mur = np.ascontiguousarray(np.asarray(L0_ffn_mu_r, np.float32).reshape(8, 128).T)
    muk = np.ascontiguousarray(np.asarray(L0_ffn_mu_k, np.float32).reshape(8, 128).T)
    p = np.arange(128, dtype=np.float32)[:, None]
    u = np.arange(STRIPW, dtype=np.float32)[None, :]
    dist = np.abs(p - u + U0).astype(np.float32)
    lin = np.broadcast_to(np.arange(S, dtype=np.float32)[None, :], (128, S)).copy()
    lr = np.maximum(1536.0 - np.arange(1544, dtype=np.float32), 0.0)
    linr = np.broadcast_to(lr[None, :], (128, 1544)).copy()

    phases = os.environ.get("KPHASES", "123")
    if phases not in _CACHED:
        _CACHED[phases] = build_nc(phases)
    nc = _CACHED[phases]

    shared = dict(wqkvT=wqkvT, bqkv=bqkv, bqkv_row=bqkv_row, woT=woT, bo=bo, ddv=ddv,
                  wrT=wrT, wkT=wkT, wvT2=wvT2, mur=mur, muk=muk, dist=dist, lin=lin, linr=linr)
    in_maps = []
    for b in range(8):
        m = dict(shared)
        m["xT"] = np.ascontiguousarray(x[b].T)
        in_maps.append(m)

    res = run_bass_kernel_spmd(nc, in_maps, core_ids=list(range(8)))
    out = np.stack([np.ascontiguousarray(r["yT"].T) for r in res.results], axis=0)
    return out.astype(np.float32)


# version 31
# speedup vs baseline: 1.0653x; 1.0573x over previous
import contextlib
import os
import numpy as np
import concourse.bass as bass
import concourse.mybir as mybir
from concourse.bass_utils import run_bass_kernel_spmd

F32 = mybir.dt.float32
BF16 = mybir.dt.bfloat16
AF = mybir.ActivationFunctionType
ALU = mybir.AluOpType

S = 2048
C = 1024
NH = 16
HD = 64
NP = 8
NCH = 4
CW = 512
NT = 16
NK = 8
EPS = 1e-5
U0 = 1920
STRIPW = 3968

_CACHED = {}


def build_nc(phases="123"):
    p1parts = os.environ.get("KP1", "all")
    p1ch = int(os.environ.get("KPCH", "4"))
    statlvl = int(os.environ.get("KPSTAT", "5"))
    nc = bass.Bass("TRN2", target_bir_lowering=False)

    xT = nc.dram_tensor("xT", [C, S], F32, kind="ExternalInput")
    wqkvT = nc.dram_tensor("wqkvT", [C, 3 * C], F32, kind="ExternalInput")
    bqkv = nc.dram_tensor("bqkv", [128, 24], F32, kind="ExternalInput")
    bqkv_row = nc.dram_tensor("bqkv_row", [1, 3 * C], F32, kind="ExternalInput")
    woT = nc.dram_tensor("woT", [C, C], F32, kind="ExternalInput")
    bo = nc.dram_tensor("bo", [128, 8], F32, kind="ExternalInput")
    ddv = nc.dram_tensor("ddv", [1, 1], F32, kind="ExternalInput")
    wrT = nc.dram_tensor("wrT", [C, C], F32, kind="ExternalInput")
    wkT = nc.dram_tensor("wkT", [C, C], F32, kind="ExternalInput")
    wvT2 = nc.dram_tensor("wvT2", [C, C], F32, kind="ExternalInput")
    mur = nc.dram_tensor("mur", [128, 8], F32, kind="ExternalInput")
    muk = nc.dram_tensor("muk", [128, 8], F32, kind="ExternalInput")
    dist = nc.dram_tensor("dist", [128, STRIPW], F32, kind="ExternalInput")
    lin = nc.dram_tensor("lin", [128, S], F32, kind="ExternalInput")
    linr = nc.dram_tensor("linr", [128, 1544], F32, kind="ExternalInput")
    kdbg = os.environ.get("KDBG", "0") == "1"
    if kdbg:
        dbg_etab = nc.dram_tensor("dbg_etab", [128, S], F32, kind="ExternalOutput")
        dbg_fmt = nc.dram_tensor("dbg_fmt", [128, 512], F32, kind="ExternalOutput")
        dbg_fpt = nc.dram_tensor("dbg_fpt", [128, 512], F32, kind="ExternalOutput")
        dbg_ks = nc.dram_tensor("dbg_ks", [128, S], F32, kind="ExternalOutput")
        dbg_qp = nc.dram_tensor("dbg_qp", [128, 1024], F32, kind="ExternalOutput")
        dbg_p = nc.dram_tensor("dbg_p", [128, 2048], F32, kind="ExternalOutput")
        dbg_p2 = nc.dram_tensor("dbg_p2", [128, 2048], F32, kind="ExternalOutput")
        dbg_och = nc.dram_tensor("dbg_och", [128, 4096], F32, kind="ExternalOutput")
    yT = nc.dram_tensor("yT", [C, S], F32, kind="ExternalOutput")

    ctx = contextlib.ExitStack()
    E = ctx.enter_context

    sems = {}
    cnt = {}
    for nm in ("dq", "dg", "dy", "st", "sv", "ss", "w0", "w1", "v0", "v1", "o0", "o1", "dx0", "dx1", "dx2", "dx3", "fk", "fr", "fv"):
        sems[nm] = E(nc.semaphore(nm))
        cnt[nm] = 0

    def inc(instr, nm, k=1):
        instr.then_inc(sems[nm], k)
        cnt[nm] += k
        return cnt[nm]

    waited = {}

    def wait(eng, nm, val):
        if val > 0 and waited.get((id(eng), nm), 0) < val:
            eng.wait_ge(sems[nm], val)
            waited[(id(eng), nm)] = val

    guard = {}

    def gwait(eng, *keys):
        for k in keys:
            if k in guard:
                wait(eng, *guard[k])

    def gset(key, nm, val):
        guard[key] = (nm, val)

    def barrier():
        vals = {nm: cnt[nm] for nm in ("st", "sv", "ss", "dq", "dg", "w0", "w1", "v0", "v1",
                                        "o0", "o1", "dx0", "dx1", "dx2", "dx3", "fk", "fr", "fv")}
        for eng in (nc.tensor, nc.vector, nc.scalar, nc.sync, nc.gpsimd):
            for nm, v in vals.items():
                wait(eng, nm, v)

    x1f = E(nc.sbuf_tensor("x1f", [128, NK * S], F32))
    bqkv_sb = E(nc.sbuf_tensor("bqkv_sb", [128, 24], F32))
    bo_sb = E(nc.sbuf_tensor("bo_sb", [128, 8], F32))
    mur_sb = E(nc.sbuf_tensor("mur_sb", [128, 8], F32))
    muk_sb = E(nc.sbuf_tensor("muk_sb", [128, 8], F32))
    dd_sb = E(nc.sbuf_tensor("dd_sb", [1, 1], F32))
    absd = E(nc.sbuf_tensor("absd", [1, 1], F32))
    nega = E(nc.sbuf_tensor("nega", [128, 1], F32))
    ones_c = E(nc.sbuf_tensor("ones_c", [128, 1], BF16))
    onesr_bf = E(nc.sbuf_tensor("onesr_bf", [1, 128], BF16))
    onesr32 = E(nc.sbuf_tensor("onesr32", [1, 128], F32))

    ps_s0 = E(nc.psum_tensor("ps_s0", [128, 1024], F32))
    ps_s1 = E(nc.psum_tensor("ps_s1", [128, 1024], F32))
    ps_o = E(nc.psum_tensor("ps_o", [128, 1024], F32))
    ps_mA = E(nc.psum_tensor("ps_mA", [128, 512], F32))
    ps_mB = E(nc.psum_tensor("ps_mB", [128, 512], F32))

    blk = E(nc.Block())

    @blk.sync
    def _(sync):
        T, V, Sc, Y, G = nc.tensor, nc.vector, nc.scalar, nc.sync, nc.gpsimd

        cp_xc = {}
        for xc in range(NCH):
            cp_xc[xc] = (f"dx{xc}", inc(Y.dma_start(
                bass.AP(x1f, xc * CW, [[NK * S, 128], [S, NK], [1, CW]]),
                bass.AP(xT, xc * CW, [[S, 128], [128 * S, NK], [1, CW]])), f"dx{xc}", 16))
        inc(Y.dma_start(bqkv_sb[:, :], bqkv[:, :]), "dq", 16)
        inc(Y.dma_start(bo_sb[:, :], bo[:, :]), "dq", 16)
        inc(Y.dma_start(mur_sb[:, :], mur[:, :]), "dq", 16)
        inc(Y.dma_start(muk_sb[:, :], muk[:, :]), "dq", 16)
        cp_dd = inc(Y.dma_start(dd_sb[:, :], ddv[:, :]), "dq", 16)

        V.memset(ones_c[:, :], 1.0)
        V.memset(onesr_bf[:, :], 1.0)
        cp_vones = inc(V.memset(onesr32[:, :], 1.0), "sv")

        wait(Sc, "dq", cp_dd)
        cp_abs = inc(Sc.activation(absd[:, :], dd_sb[:, :], AF.Abs), "ss")
        wait(T, "ss", cp_abs)
        wait(T, "sv", cp_vones)
        cp_negb = inc(T.matmul(ps_mA[0:128, 0:1], onesr32[:, :], absd[:, :], start=True, stop=True), "st")
        wait(V, "st", cp_negb)
        cp_nega = inc(V.tensor_scalar(nega[:, :], ps_mA[:, 0:1], -1.0, None, ALU.mult), "sv")
        gset("mA", "sv", cp_nega)

        kqv = contextlib.ExitStack()
        KT = kqv.enter_context(nc.sbuf_tensor("KT", [128, NK * S], BF16))
        QT = kqv.enter_context(nc.sbuf_tensor("QT", [128, NK * S], BF16))
        Vaug = kqv.enter_context(nc.sbuf_tensor("Vaug", [128, NT * 1024], BF16))

        with contextlib.ExitStack() as p1:
            E1 = p1.enter_context
            hTc = E1(nc.sbuf_tensor("hTc", [128, 2 * NK * CW], BF16))
            xtmp = E1(nc.sbuf_tensor("xtmp", [128, NK * 256], BF16))
            t1f = E1(nc.sbuf_tensor("t1f", [128, 512], F32))
            wblk = E1(nc.sbuf_tensor("wblk", [128, 2 * 2048], BF16))
            wvq = E1(nc.sbuf_tensor("wvq", [128, 2 * 2048], BF16))
            bvrow_bf = E1(nc.sbuf_tensor("bvrow_bf", [1, C], BF16))
            rows = E1(nc.sbuf_tensor("rows", [1, 1536], F32))

            cp_bv = inc(G.dma_start(bvrow_bf[0:1, :],
                                    bass.AP(bqkv_row, 2 * C, [[3 * C, 1], [1, C]])), "dg", 16)

            cp_qk = {}
            cp_wdma = {}
            cp_h = {}
            wait(V, "dq", cp_dd)
            wait(T, "dq", cp_dd)

            def ln1_half(ch, hf, first):
                wait(V, cp_xc[ch][0], cp_xc[ch][1])
                gwait(V, "xtmp")
                if first:
                    gwait(V, "mAr")
                for k in range(NK):
                    ii = V.tensor_copy(xtmp[:, k * 256:(k + 1) * 256],
                                       x1f[:, k * S + ch * CW + hf * 256: k * S + ch * CW + hf * 256 + 256])
                cp_xbf = inc(ii, "sv")
                wait(T, "sv", cp_xbf)
                if first:
                    gwait(T, "mA")
                for k in range(NK):
                    ii = T.matmul(ps_mA[0:1, hf * 256: hf * 256 + 256], ones_c[:, :],
                                  xtmp[:, k * 256:(k + 1) * 256],
                                  start=(k == 0), stop=(k == NK - 1))
                cp_sx = inc(ii, "st")
                wait(V, "st", cp_sx)
                for k in range(NK):
                    ii = V.tensor_tensor(xtmp[:, k * 256:(k + 1) * 256], xtmp[:, k * 256:(k + 1) * 256],
                                         xtmp[:, k * 256:(k + 1) * 256], ALU.mult)
                cp_xsq = inc(ii, "sv")
                wait(T, "sv", cp_xsq)
                if first:
                    gwait(T, "mB")
                for k in range(NK):
                    ii = T.matmul(ps_mB[0:1, hf * 256: hf * 256 + 256], ones_c[:, :],
                                  xtmp[:, k * 256:(k + 1) * 256],
                                  start=(k == 0), stop=(k == NK - 1))
                cp_sxx = inc(ii, "st")
                gset("xtmp", "st", cp_sxx)
                return cp_sxx

            def ln1_cast_sx(ch):
                return ln1_half(ch, 0, True)

            def ln1_sq_sxx(ch, cp_a):
                return ln1_half(ch, 1, False)

            def ln1_fin_bc(ch, cp_sxx):
                ra, rb, rc = rows[0:1, 0:512], rows[0:1, 512:1024], rows[0:1, 1024:1536]
                wait(V, "st", cp_sxx)
                V.tensor_scalar(ra, ps_mA[0:1, 0:512], 1.0 / C, None, ALU.mult)
                cw = inc(V.tensor_scalar(rc, ps_mB[0:1, 0:512], 1.0 / C, EPS, ALU.mult, ALU.add), "sv")
                wait(V, "sv", cw)
                cw = inc(V.tensor_tensor(rb, ra, ra, ALU.mult), "sv")
                wait(V, "sv", cw)
                cw = inc(V.tensor_tensor(rb, rc, rb, ALU.subtract), "sv")
                wait(V, "sv", cw)
                cp_rv = inc(V.reciprocal(rc, rb), "sv")
                wait(Sc, "sv", cp_rv)
                cp_rstd = inc(Sc.activation(rb, rc, AF.Sqrt), "ss")
                wait(T, "ss", cp_rstd)
                T.matmul(ps_mA[:, 0:512], onesr32[:, :], ra, start=True, stop=True)
                return inc(T.matmul(ps_mB[:, 0:512], onesr32[:, :], rb, start=True, stop=True), "st")

            def ln1_apply(ch, cp_bc):
                hb = (ch % 2) * NK * CW
                wait(V, "st", cp_bc)
                gwait(V, ("hTc", ch % 2))
                pairs = [(k, hf) for k in range(NK) for hf in range(2)]
                subcp = {}
                mulcp = {}

                def _sub(i):
                    k, hf = pairs[i]
                    sl = (i % 2) * 256
                    if i >= 2:
                        wait(V, "sv", mulcp[i - 2])
                    subcp[i] = inc(V.tensor_tensor(
                        t1f[:, sl:sl + 256],
                        x1f[:, k * S + ch * CW + hf * 256: k * S + ch * CW + hf * 256 + 256],
                        ps_mA[:, hf * 256:(hf + 1) * 256], ALU.subtract), "sv")

                def _mul(i, guard):
                    k, hf = pairs[i]
                    sl = (i % 2) * 256
                    if guard:
                        wait(V, "sv", subcp[i])
                    ii = V.tensor_tensor(
                        hTc[:, hb + k * CW + hf * 256: hb + k * CW + hf * 256 + 256],
                        t1f[:, sl:sl + 256], ps_mB[:, hf * 256:(hf + 1) * 256], ALU.mult)
                    mulcp[i] = inc(ii, "sv")
                    return ii

                _sub(0)
                for i in range(1, len(pairs)):
                    _sub(i)
                    _mul(i - 1, i == 1)
                ii = _mul(len(pairs) - 1, True)
                cp = mulcp[len(pairs) - 1]
                cp_h[ch] = cp
                gset("mAr", "sv", cp)
                gset("mA", "sv", cp)
                gset("mB", "sv", cp)
                return cp

            def emit_ln1_full(ch):
                a = ln1_cast_sx(ch)
                b = ln1_sq_sxx(ch, a)
                c_ = ln1_fin_bc(ch, b)
                ln1_apply(ch, c_)

            emit_ln1_full(0)

            for ch in range(NCH):
                hb = (ch % 2) * NK * CW
                for jp in range(2):
                    b = jp % 2
                    gwait(G, ("wblk", b))
                    cp_wdma[jp] = (f"w{b}", inc(G.dma_start(
                        bass.AP(wblk, b * 2048, [[4096, 128], [256, NK], [1, 256]]),
                        bass.AP(wqkvT, jp * 256, [[3 * C, 128], [128 * 3 * C, NK], [1, 256]])), f"w{b}", 16))
                cp_wv = {}
                for q in range(2):
                    b = q % 2
                    gwait(G, ("wvq", b))
                    cp_wv[q] = (f"v{b}", inc(G.dma_start(
                        bass.AP(wvq, b * 2048, [[4096, 128], [256, NK], [1, 256]]),
                        bass.AP(wqkvT, 2 * C + q * 256, [[3 * C, 128], [128 * 3 * C, NK], [1, 256]])),
                        f"v{b}", 16))

                wait(T, "sv", cp_h[ch])
                lnstate = {}
                for j in range(16):
                    jp, jh = j // 2, j % 2
                    b = jp % 2
                    bank = "s0a" if (j % 2 == 0) else "s0b"
                    pslice = ps_s0[:, 0:512] if (j % 2 == 0) else ps_s0[:, 512:1024]
                    if jh == 0:
                        wait(T, cp_wdma[jp][0], cp_wdma[jp][1])
                    gwait(T, bank)
                    for k in range(NK):
                        ii = T.matmul(pslice,
                                      wblk[:, b * 2048 + k * 256 + jh * 128: b * 2048 + k * 256 + jh * 128 + 128],
                                      hTc[:, hb + k * CW: hb + k * CW + CW], start=(k == 0), stop=(k == NK - 1))
                    cp_qk[j] = inc(ii, "st")
                    if jh == 1:
                        gset(("wblk", b), "st", cp_qk[j])
                        if jp + 2 < 8:
                            gwait(G, ("wblk", (jp + 2) % 2))
                            cp_wdma[jp + 2] = (f"w{(jp + 2) % 2}", inc(G.dma_start(
                                bass.AP(wblk, ((jp + 2) % 2) * 2048, [[4096, 128], [256, NK], [1, 256]]),
                                bass.AP(wqkvT, (jp + 2) * 256, [[3 * C, 128], [128 * 3 * C, NK], [1, 256]])),
                                f"w{(jp + 2) % 2}", 16))
                    tgt = QT if j < 8 else KT
                    jj = j % 8
                    wait(Sc, "st", cp_qk[j])
                    ii = Sc.activation(tgt[:, jj * S + ch * CW: jj * S + (ch + 1) * CW],
                                       pslice, AF.Identity, bias=bqkv_sb[:, j:j + 1])
                    gset(bank, "ss", inc(ii, "ss"))
                    if ch + 1 < NCH:
                        if j == 5:
                            lnstate["a"] = ln1_cast_sx(ch + 1)
                        elif j == 8:
                            lnstate["b"] = ln1_sq_sxx(ch + 1, lnstate["a"])
                        elif j == 11:
                            lnstate["c"] = ln1_fin_bc(ch + 1, lnstate["b"])
                        elif j == 14:
                            ln1_apply(ch + 1, lnstate["c"])

                if ch == 0:
                    wait(T, "dg", cp_bv)
                for q in range(4):
                    b = q % 2
                    wait(T, cp_wv[q][0], cp_wv[q][1])
                    for tt in range(4):
                        tau = ch * 4 + tt
                        vb_ = (q * 4 + tt) % 2
                        bank = "s1a" if vb_ == 0 else "s1b"
                        pslice = ps_s1[:, 0:256] if vb_ == 0 else ps_s1[:, 512:768]
                        gwait(T, bank)
                        T.matmul(pslice, onesr_bf[:, :], bvrow_bf[0:1, q * 256:(q + 1) * 256],
                                 start=True, stop=False)
                        for k in range(NK):
                            ii = T.matmul(pslice, hTc[:, hb + k * CW + tt * 128: hb + k * CW + tt * 128 + 128],
                                          wvq[:, b * 2048 + k * 256: b * 2048 + (k + 1) * 256],
                                          start=False, stop=(k == NK - 1))
                        cp_v = inc(ii, "st")
                        gset(("wvq", b), "st", cp_v)
                        wait(Sc, "st", cp_v)
                        ii = Sc.activation(Vaug[:, tau * 1024 + 256 * q: tau * 1024 + 256 * (q + 1)],
                                           ps_s1[:, vb_ * 512: vb_ * 512 + 256], AF.Identity)
                        gset(bank, "ss", inc(ii, "ss"))
                    if q + 2 < 4:
                        bnext = (q + 2) % 2
                        gwait(G, ("wvq", bnext))
                        cp_wv[q + 2] = (f"v{bnext}", inc(G.dma_start(
                            bass.AP(wvq, bnext * 2048, [[4096, 128], [256, NK], [1, 256]]),
                            bass.AP(wqkvT, 2 * C + (q + 2) * 256,
                                    [[3 * C, 128], [128 * 3 * C, NK], [1, 256]])), f"v{bnext}", 16))
                gset(("hTc", ch % 2), "st", cp_v)

        barrier()
        if "2" in phases:
          with contextlib.ExitStack() as p2:
            E2 = p2.enter_context
            SW2 = 896
            strip = E2(nc.sbuf_tensor("strip", [128, SW2], F32))
            etab = E2(nc.sbuf_tensor("etab", [128, S], BF16))
            rtab = E2(nc.sbuf_tensor("rtab", [128, 1544], BF16))
            fmt = E2(nc.sbuf_tensor("fmt", [128, 512], BF16))
            fpt = E2(nc.sbuf_tensor("fpt", [128, 512], BF16))
            kscale = E2(nc.sbuf_tensor("kscale", [128, S], BF16))
            qm = E2(nc.sbuf_tensor("qm", [128, 2 * 512], BF16))
            qp = E2(nc.sbuf_tensor("qp", [128, 2 * 512], BF16))
            Pbuf = E2(nc.sbuf_tensor("Pbuf", [128, 3 * 1024], BF16))
            Och = E2(nc.sbuf_tensor("Och", [128, NK * 512], BF16))
            wo_sb = E2(nc.sbuf_tensor("wo_sb", [128, 2 * 1024], BF16))
            recip_sb = E2(nc.sbuf_tensor("recip_sb", [128, 512], F32))
            ones32d = E2(nc.sbuf_tensor("ones32d", [128, 32], BF16))
            ind = E2(nc.sbuf_tensor("ind", [128, 128], F32))

            V.memset(recip_sb[:, :], 0.0)
            cw0 = inc(V.memset(ones32d[:, :], 0.0), "sv")
            wait(V, "sv", cw0)
            V.memset(ones32d[:, 0:1], 1.0)
            cw0 = inc(V.memset(ind[:, :], 0.0), "sv")
            wait(V, "sv", cw0)
            V.memset(ind[32:33, 0:64], 1.0)
            cp_rz = inc(V.memset(ind[0:1, 64:128], 1.0), "sv")
            wait(T, "sv", cp_rz)
            cp_dist = inc(Y.dma_start(strip[:, :],
                                      bass.AP(dist, 1536, [[STRIPW, 128], [1, SW2]])), "dq", 16)
            cp_lin = inc(G.dma_start(etab[:, :], lin[:, :]), "dg", 16)
            cp_linr = inc(G.dma_start(rtab[:, :], linr[:, :]), "dg", 16)
            wait(V, "dq", cp_dist)
            cp_s1 = inc(V.tensor_scalar(strip[:, :], strip[:, :], nega[:, 0:1], None, ALU.mult), "sv")
            wait(V, "dg", cp_linr)
            V.tensor_scalar(etab[:, :], etab[:, :], nega[:, 0:1], None, ALU.mult)
            cp_e1 = inc(V.tensor_scalar(rtab[:, :], rtab[:, :], nega[:, 0:1], None, ALU.mult), "sv")
            wait(Sc, "sv", cp_e1)
            cp_s2 = inc(Sc.activation(strip[:, :], strip[:, :], AF.Exp), "ss")
            Sc.activation(etab[:, :], etab[:, :], AF.Exp)
            cp_e2 = inc(Sc.activation(rtab[:, :], rtab[:, :], AF.Exp), "ss")
            wait(V, "ss", cp_e2)
            cp_strip = inc(V.tensor_scalar(strip[:, :], strip[:, :], 0.125, None, ALU.mult), "sv")
            V.tensor_scalar(fmt[:, :], etab[:, 0:512], 0.125, None, ALU.mult)
            cp_fmt = inc(V.tensor_scalar(fpt[:, :], rtab[:, 1025:1537], 0.125, None, ALU.mult), "sv")
            wait(T, "sv", cp_fmt)
            wait(Sc, "sv", cp_fmt)
            if kdbg:
                wait(G, "sv", cp_fmt)
                inc(G.dma_start(dbg_etab[:, :], etab[:, :]), "dg", 16)
                inc(G.dma_start(dbg_fmt[:, :], fmt[:, :]), "dg", 16)
                inc(G.dma_start(dbg_fpt[:, :], fpt[:, :]), "dg", 16)

            cp_wo = {}
            cp_oev = 0
            cp_smm_last = -1
            for ch in range(NCH):
                s0 = ch * CW
                lo_end = 4 * ch
                hi_beg = 4 * ch + 4
                for co in range(2):
                    b = co % 2
                    gwait(G, ("wo", b))
                    cp_wo[co] = (f"o{b}", inc(G.dma_start(
                        bass.AP(wo_sb, b * 1024, [[2048, 128], [128, NK], [1, 128]]),
                        bass.AP(woT, co * 128, [[C, 128], [128 * C, NK], [1, 128]])), f"o{b}", 16))

                def build_klo(gg):
                    return inc(V.tensor_tensor(
                        kscale[:, 0:s0], KT[:, gg * S: gg * S + s0],
                        bass.AP(rtab, 1536 - s0, [[1544, 128], [1, s0]]), ALU.mult), "sv")

                def build_khi(gg):
                    n = S - (s0 + CW)
                    return inc(V.tensor_tensor(
                        kscale[:, s0 + CW: S], KT[:, gg * S + s0 + CW: (gg + 1) * S],
                        bass.AP(etab, 1, [[S, 128], [1, n]]), ALU.mult), "sv")

                def build_q(gg):
                    sl = gg % 2
                    a = b_ = None
                    if lo_end > 0:
                        a = inc(V.tensor_tensor(qm[:, sl * 512:(sl + 1) * 512],
                                                QT[:, gg * S + s0: gg * S + s0 + CW],
                                                fmt[:, :], ALU.mult), "sv")
                    if hi_beg < NT:
                        b_ = inc(V.tensor_tensor(qp[:, sl * 512:(sl + 1) * 512],
                                                 QT[:, gg * S + s0: gg * S + s0 + CW],
                                                 fpt[:, :], ALU.mult), "sv")
                    return b_ or a

                if lo_end > 0:
                    wait(V, "st", cp_smm_last)
                    cp_klo = build_klo(0)
                else:
                    cp_klo = 0
                gwait(V, ("q", 0))
                if ch == 0:
                    wait(V, "sv", cp_fmt)
                cp_q = build_q(0)
                cp_khi = 0
                pending = None
                pend_cpb = None
                pend_cpc = [0]

                for g in range(NP):
                    cp_exp = {}
                    cp_pre = {}
                    cp_av = {}
                    ob = "oA" if g % 2 == 0 else "oB"
                    mb = "mA" if g % 2 == 0 else "mB"
                    pm = ps_mA if g % 2 == 0 else ps_mB
                    oc0 = 0 if g % 2 == 0 else 512
                    qsl = g % 2
                    cp_smm_lo = 0
                    cp_smm = 0
                    prev_smm_last = cp_smm_last

                    def emit_smm(tau):
                        nonlocal cp_smm_lo, cp_smm, cp_klo, cp_khi, cp_q
                        sl = tau % 2
                        pst = ps_s0 if sl == 0 else ps_s1
                        bko = ("s0a", "s0b") if sl == 0 else ("s1a", "s1b")
                        crossing = lo_end <= tau < hi_beg
                        gwait(T, bko[0], bko[1])
                        if tau < lo_end:
                            if tau == 0:
                                wait(T, "sv", max(cp_klo, cp_q))
                            lh, rh = kscale, qm
                            lo_, ro_ = tau * 128, qsl * 512
                        elif tau >= hi_beg:
                            if tau == hi_beg:
                                wait(T, "sv", max(cp_khi, cp_q))
                            lh, rh = kscale, qp
                            lo_, ro_ = tau * 128, qsl * 512
                        else:
                            lh, rh = KT, QT
                            lo_, ro_ = g * S + tau * 128, g * S + s0
                        T.matmul(pst[:, 0:512], lh[0:64, lo_: lo_ + 128], rh[0:64, ro_: ro_ + CW],
                                 start=True, stop=True, tile_position=(0, 0))
                        cp_smm = inc(T.matmul(pst[:, 512:1024], lh[64:128, lo_: lo_ + 128],
                                              rh[64:128, ro_: ro_ + CW],
                                              start=True, stop=True, tile_position=(64, 0)), "st")
                        if tau == lo_end - 1:
                            cp_smm_lo = cp_smm
                        if crossing:
                            off = 384 - (tau * 128 - s0)
                            wait(V, "st", cp_smm)
                            cp_mult = inc(V.tensor_tensor(
                                pst[:, 0:1024], pst[:, 0:1024],
                                bass.AP(strip, off, [[SW2, 128], [0, 2], [1, 512]]), ALU.mult), "sv")
                            if tau == lo_end:
                                if hi_beg < NT:
                                    wait(V, "st", prev_smm_last)
                                    cp_khi = build_khi(g)
                            elif tau == lo_end + 1 and g + 1 < NP:
                                if lo_end > 0:
                                    wait(V, "st", cp_smm_lo)
                                    cp_klo = build_klo(g + 1)
                            elif tau == lo_end + 2 and g + 1 < NP:
                                gwait(V, ("q", (g + 1) % 2))
                                cp_q = build_q(g + 1)
                            cp_pre[tau] = ("sv", cp_mult)
                        else:
                            cp_pre[tau] = ("st", cp_smm)

                    def emit_exp(tau):
                        sl = tau % 2
                        slp = tau % 3
                        pst = ps_s0 if sl == 0 else ps_s1
                        bko = ("s0a", "s0b") if sl == 0 else ("s1a", "s1b")
                        nm_, v_ = cp_pre[tau]
                        wait(Sc, nm_, v_)
                        gwait(Sc, ("P", slp))
                        cp_exp[tau] = inc(Sc.activation(Pbuf[:, slp * 1024:(slp + 1) * 1024],
                                                        pst[:, 0:1024], AF.Exp), "ss")
                        gset(bko[0], "ss", cp_exp[tau])
                        gset(bko[1], "ss", cp_exp[tau])

                    def emit_av(tau):
                        slp = tau % 3
                        wait(T, "ss", cp_exp[tau])
                        if tau == 0:
                            gwait(T, ob, "mA")
                        T.matmul(ps_o[0:64, oc0:oc0 + 512],
                                 bass.AP(Vaug, tau * 1024 + 64 * (2 * g), [[NT * 1024, 128], [1, 64]]),
                                 Pbuf[:, slp * 1024: slp * 1024 + 512],
                                 start=(tau == 0), stop=(tau == NT - 1),
                                 skip_group_check=True)
                        T.matmul(ps_o[64:128, oc0:oc0 + 512],
                                 bass.AP(Vaug, tau * 1024 + 64 * (2 * g + 1), [[NT * 1024, 128], [1, 64]]),
                                 Pbuf[:, slp * 1024 + 512: (slp + 1) * 1024],
                                 start=(tau == 0), stop=(tau == NT - 1), tile_position=(0, 64),
                                 skip_group_check=True)
                        T.matmul(pm[32:64, 0:512], ones32d[:, :],
                                 Pbuf[:, slp * 1024: slp * 1024 + 512],
                                 start=(tau == 0), stop=(tau == NT - 1), tile_position=(0, 32),
                                 skip_group_check=True)
                        cp_av[tau] = inc(T.matmul(pm[0:32, 0:512], ones32d[:, :],
                                                  Pbuf[:, slp * 1024 + 512: (slp + 1) * 1024],
                                                  start=(tau == 0), stop=(tau == NT - 1),
                                                  tile_position=(0, 0),
                                                  skip_group_check=True), "st")
                        gset(("P", slp), "st", cp_av[tau])

                    def emit_tail_recip():
                        wait(V, "st", cp_av[NT - 1])
                        gwait(V, "recip")
                        wait(V, "sv", cp_oev)
                        V.reciprocal(recip_sb[32:33, 0:512], pm[32:33, 0:512])
                        return inc(V.reciprocal(recip_sb[0:1, 0:512], pm[0:1, 0:512]), "sv")

                    def emit_tail_bcast(pend):
                        pg, ppm, prc = pend
                        wait(T, "sv", prc)
                        cpb = inc(T.matmul(ppm[:, 0:512], ind[:, :],
                                           recip_sb[:, 0:512], start=True, stop=True), "st")
                        gset("recip", "st", cpb)
                        return cpb

                    def emit_tail_copy(pend, cpb):
                        pg, ppm, prc = pend
                        pmb = "mA" if pg % 2 == 0 else "mB"
                        wait(V, "st", cpb)
                        cpc = inc(V.tensor_copy(recip_sb[:, :], ppm[:, 0:512]), "sv")
                        gset(pmb, "sv", cpc)
                        pend_cpc[0] = cpc
                        return cpc

                    def emit_tail_evict(pend):
                        pg, ppm, prc = pend
                        poc0 = 0 if pg % 2 == 0 else 512
                        pob = "oA" if pg % 2 == 0 else "oB"
                        gwait(V, ("och", pg))
                        wait(V, "sv", pend_cpc[0])
                        cpo = inc(V.tensor_tensor(Och[:, pg * 512:(pg + 1) * 512],
                                                  ps_o[:, poc0:poc0 + 512],
                                                  recip_sb[:, :], ALU.mult), "sv")
                        gset(pob, "sv", cpo)
                        if kdbg and ch == 0 and pg == 1:
                            wait(G, "sv", cpo)
                            inc(G.dma_start(dbg_p2[:, :512], recip_sb[:, :]), "dg", 16)
                            inc(G.dma_start(dbg_och[:, :], Och[:, :]), "dg", 16)
                            inc(G.dma_start(dbg_p[:, :], Pbuf[:, :]), "dg", 16)
                        return cpo

                    for tau in range(NT):
                        emit_smm(tau)
                        if tau >= 2:
                            emit_av(tau - 2)
                        emit_exp(tau)
                        if pending is not None:
                            if tau == 1:
                                pend_cpb = emit_tail_bcast(pending)
                            elif tau == 2:
                                emit_tail_copy(pending, pend_cpb)
                            elif tau == 3:
                                cp_oev = emit_tail_evict(pending)
                                pending = None
                    emit_av(NT - 2)
                    emit_av(NT - 1)
                    cp_smm_last = cp_smm
                    gset(("q", qsl), "st", cp_smm_last)
                    cp_rc = emit_tail_recip()
                    if kdbg and ch == 0 and g == 1:
                        wait(G, "sv", cp_rc)
                        cp_dk = inc(G.dma_start(dbg_p2[:, 512:1024], recip_sb[:, :]), "dg", 16)
                        wait(T, "dg", cp_dk)
                        inc(G.dma_start(dbg_p[:, :], Pbuf[:, :]), "dg", 16)
                    pending = (g, pm, cp_rc)
                    if g == NP - 1:
                        pend_cpb = emit_tail_bcast(pending)
                        emit_tail_copy(pending, pend_cpb)
                        cp_oev = emit_tail_evict(pending)
                        pending = None
                    if kdbg and ch == 0 and g == 0:
                        wait(G, "sv", cp_rc)
                        inc(G.dma_start(dbg_ks[:, :], kscale[:, :]), "dg", 16)
                        inc(G.dma_start(dbg_qp[:, :], qp[:, :]), "dg", 16)

                for co in range(8):
                    b = co % 2
                    bank = "s0a" if b == 0 else "s0b"
                    pslice = ps_s0[:, 0:512] if b == 0 else ps_s0[:, 512:1024]
                    wait(T, cp_wo[co][0], cp_wo[co][1])
                    wait(T, "sv", cp_oev)
                    gwait(T, bank)
                    for k in range(NK):
                        ii = T.matmul(pslice, wo_sb[:, b * 1024 + k * 128: b * 1024 + (k + 1) * 128],
                                      Och[:, k * 512:(k + 1) * 512], start=(k == 0), stop=(k == NK - 1))
                    cp_proj = inc(ii, "st")
                    gset(("wo", b), "st", cp_proj)
                    if co + 2 < 8:
                        gwait(G, ("wo", (co + 2) % 2))
                        cp_wo[co + 2] = (f"o{(co + 2) % 2}", inc(G.dma_start(
                            bass.AP(wo_sb, ((co + 2) % 2) * 1024, [[2048, 128], [128, NK], [1, 128]]),
                            bass.AP(woT, (co + 2) * 128, [[C, 128], [128 * C, NK], [1, 128]])),
                            f"o{(co + 2) % 2}", 16))
                    wait(V, "st", cp_proj)
                    ii = V.scalar_tensor_tensor(
                        x1f[:, co * S + ch * CW: co * S + (ch + 1) * CW], pslice, bo_sb[:, co:co + 1],
                        x1f[:, co * S + ch * CW: co * S + (ch + 1) * CW], ALU.add, ALU.add)
                    cp_x1 = inc(ii, "sv")
                    gset(bank, "sv", cp_x1)
                for g in range(NP):
                    gset(("och", g), "st", cp_proj)

        barrier()
        kqv.close()
        if "3" not in phases:
            cp_dbg = inc(Y.dma_start(
                bass.AP(yT, 0, [[S, 128], [128 * S, NK], [1, S]]),
                bass.AP(x1f, 0, [[NK * S, 128], [S, NK], [1, S]])), "dy", 16)
        if "3" in phases:
          with contextlib.ExitStack() as p3:
            E3 = p3.enter_context
            h2b = E3(nc.sbuf_tensor("h2b", [128, NK * 2052], BF16))
            wr_sb = E3(nc.sbuf_tensor("wr_sb", [128, NK * 1024], BF16))
            wk_sb = E3(nc.sbuf_tensor("wk_sb", [128, NK * 1024], BF16))
            wv2_sb = E3(nc.sbuf_tensor("wv2_sb", [128, NK * 1024], BF16))
            xrb = E3(nc.sbuf_tensor("xrb", [128, 2 * NK * CW], BF16))
            xkb = E3(nc.sbuf_tensor("xkb", [128, 2 * NK * CW], BF16))
            kact = E3(nc.sbuf_tensor("kact", [128, NK * CW], BF16))
            xtmp2 = E3(nc.sbuf_tensor("xtmp2", [128, NK * 256], BF16))
            t1g = E3(nc.sbuf_tensor("t1g", [128, 512], F32))
            dtmp = E3(nc.sbuf_tensor("dtmp", [128, 2 * CW], BF16))
            krelu = E3(nc.sbuf_tensor("krelu", [128, 2 * CW], BF16))
            rblk = E3(nc.sbuf_tensor("rblk", [128, 2 * CW], F32))
            t2f = E3(nc.sbuf_tensor("t2f", [128, CW], F32))
            rows = E3(nc.sbuf_tensor("rows3", [1, 1536], F32))

            cp_wk = inc(G.dma_start(
                bass.AP(wk_sb, 0, [[NK * 1024, 128], [1024, NK], [1, 1024]]),
                bass.AP(wkT, 0, [[C, 128], [128 * C, NK], [1, 1024]])), "fk", 16)
            cp_wr = inc(G.dma_start(
                bass.AP(wr_sb, 0, [[NK * 1024, 128], [1024, NK], [1, 1024]]),
                bass.AP(wrT, 0, [[C, 128], [128 * C, NK], [1, 1024]])), "fr", 16)
            cp_wv2 = inc(G.dma_start(
                bass.AP(wv2_sb, 0, [[NK * 1024, 128], [1024, NK], [1, 1024]]),
                bass.AP(wvT2, 0, [[C, 128], [128 * C, NK], [1, 1024]])), "fv", 16)

            cp_hb = inc(V.memset(bass.AP(h2b, 3, [[NK * 2052, 128], [2052, NK], [1, 1]]), 0.0), "sv")

            for ch in range(NCH):
                for hf in range(2):
                    gwait(V, "xtmp2")
                    for k in range(NK):
                        ii = V.tensor_copy(xtmp2[:, k * 256:(k + 1) * 256],
                                           x1f[:, k * S + ch * CW + hf * 256: k * S + ch * CW + hf * 256 + 256])
                    cp_xbf = inc(ii, "sv")
                    wait(T, "sv", cp_xbf)
                    if hf == 0:
                        gwait(T, "mA")
                    for k in range(NK):
                        ii = T.matmul(ps_mA[0:1, hf * 256: hf * 256 + 256], ones_c[:, :],
                                      xtmp2[:, k * 256:(k + 1) * 256], start=(k == 0), stop=(k == NK - 1))
                    cp_sx = inc(ii, "st")
                    wait(V, "st", cp_sx)
                    for k in range(NK):
                        ii = V.tensor_tensor(xtmp2[:, k * 256:(k + 1) * 256], xtmp2[:, k * 256:(k + 1) * 256],
                                             xtmp2[:, k * 256:(k + 1) * 256], ALU.mult)
                    cp_xsq = inc(ii, "sv")
                    wait(T, "sv", cp_xsq)
                    if hf == 0:
                        gwait(T, "mB")
                    for k in range(NK):
                        ii = T.matmul(ps_mB[0:1, hf * 256: hf * 256 + 256], ones_c[:, :],
                                      xtmp2[:, k * 256:(k + 1) * 256], start=(k == 0), stop=(k == NK - 1))
                    cp_sxx = inc(ii, "st")
                    gset("xtmp2", "st", cp_sxx)
                ra, rb, rc = rows[0:1, 0:512], rows[0:1, 512:1024], rows[0:1, 1024:1536]
                wait(V, "st", cp_sxx)
                V.tensor_scalar(ra, ps_mA[0:1, 0:512], 1.0 / C, None, ALU.mult)
                cw = inc(V.tensor_scalar(rc, ps_mB[0:1, 0:512], 1.0 / C, EPS, ALU.mult, ALU.add), "sv")
                wait(V, "sv", cw)
                cw = inc(V.tensor_tensor(rb, ra, ra, ALU.mult), "sv")
                wait(V, "sv", cw)
                cw = inc(V.tensor_tensor(rb, rc, rb, ALU.subtract), "sv")
                wait(V, "sv", cw)
                cp_rv = inc(V.reciprocal(rc, rb), "sv")
                wait(Sc, "sv", cp_rv)
                cp_rstd = inc(Sc.activation(rb, rc, AF.Sqrt), "ss")
                wait(T, "ss", cp_rstd)
                T.matmul(ps_mA[:, 0:512], onesr32[:, :], ra, start=True, stop=True)
                cp_bc = inc(T.matmul(ps_mB[:, 0:512], onesr32[:, :], rb, start=True, stop=True), "st")
                wait(V, "st", cp_bc)
                if ch == 0:
                    wait(V, "sv", cp_hb)
                pairs = [(k, hf) for k in range(NK) for hf in range(2)]
                subcp = {}
                mulcp = {}

                def _sub(i):
                    k, hf = pairs[i]
                    sl = (i % 2) * 256
                    if i >= 2:
                        wait(V, "sv", mulcp[i - 2])
                    subcp[i] = inc(V.tensor_tensor(
                        t1g[:, sl:sl + 256],
                        x1f[:, k * S + ch * CW + hf * 256: k * S + ch * CW + hf * 256 + 256],
                        ps_mA[:, hf * 256: hf * 256 + 256], ALU.subtract), "sv")

                def _mul(i, guard):
                    k, hf = pairs[i]
                    sl = (i % 2) * 256
                    if guard:
                        wait(V, "sv", subcp[i])
                    ii = V.tensor_tensor(
                        h2b[:, k * 2052 + 4 + ch * CW + hf * 256: k * 2052 + 4 + ch * CW + hf * 256 + 256],
                        t1g[:, sl:sl + 256], ps_mB[:, hf * 256: hf * 256 + 256], ALU.mult)
                    mulcp[i] = inc(ii, "sv")
                    return ii

                _sub(0)
                for i in range(1, len(pairs)):
                    _sub(i)
                    _mul(i - 1, i == 1)
                ii = _mul(len(pairs) - 1, True)
                cp_h2 = mulcp[len(pairs) - 1]
                gset("mA", "sv", cp_h2)
                gset("mB", "sv", cp_h2)

            wait(T, "fk", cp_wk)
            cp_blend = {}

            def emit_blends(ch):
                sb_ = (ch % 2) * NK * CW
                gwait(V, ("xrb", ch % 2), ("xkb", ch % 2))
                for k in range(NK):
                    cur = h2b[:, k * 2052 + 4 + ch * CW: k * 2052 + 4 + (ch + 1) * CW]
                    prev = h2b[:, k * 2052 + 3 + ch * CW: k * 2052 + 3 + (ch + 1) * CW]
                    dsl = (k % 2) * CW
                    cw_ = inc(V.tensor_tensor(dtmp[:, dsl:dsl + CW], cur, prev, ALU.subtract), "sv")
                    wait(V, "sv", cw_)
                    V.scalar_tensor_tensor(xrb[:, sb_ + k * CW: sb_ + (k + 1) * CW], dtmp[:, dsl:dsl + CW],
                                           mur_sb[:, k:k + 1], prev, ALU.mult, ALU.add)
                    ii = V.scalar_tensor_tensor(xkb[:, sb_ + k * CW: sb_ + (k + 1) * CW], dtmp[:, dsl:dsl + CW],
                                                muk_sb[:, k:k + 1], prev, ALU.mult, ALU.add)
                cp_blend[ch] = inc(ii, "sv")

            emit_blends(0)
            for ch in range(NCH):
                sb_ = (ch % 2) * NK * CW
                gwait(V, "kact")
                wait(T, "sv", cp_blend[ch])
                for j in range(NK):
                    b = j % 2
                    bank = "s0a" if b == 0 else "s0b"
                    pslice = ps_s0[:, 0:512] if b == 0 else ps_s0[:, 512:1024]
                    gwait(T, bank)
                    for k in range(NK):
                        ii = T.matmul(pslice, wk_sb[:, k * 1024 + j * 128: k * 1024 + (j + 1) * 128],
                                      xkb[:, sb_ + k * CW: sb_ + (k + 1) * CW],
                                      start=(k == 0), stop=(k == NK - 1))
                    cp_k = inc(ii, "st")
                    wait(V, "st", cp_k)
                    ksl = (j % 2) * CW
                    cw_ = inc(V.tensor_scalar(krelu[:, ksl:ksl + CW], pslice, 0.0, None, ALU.max), "sv")
                    wait(V, "sv", cw_)
                    ii = V.tensor_tensor(kact[:, j * CW:(j + 1) * CW], krelu[:, ksl:ksl + CW],
                                         krelu[:, ksl:ksl + CW], ALU.mult)
                    gset(bank, "sv", inc(ii, "sv"))
                cp_kact = cnt["sv"]
                if ch + 1 < NCH:
                    emit_blends(ch + 1)
                wait(T, "sv", cp_kact)
                if ch == 0:
                    wait(T, "fr", cp_wr)
                    wait(T, "fv", cp_wv2)

                for j in range(NK):
                    rb_ = j % 2
                    bank_r = "s1a" if rb_ == 0 else "s1b"
                    psr = ps_s1[:, 0:512] if rb_ == 0 else ps_s1[:, 512:1024]
                    bank_v = "oA" if rb_ == 0 else "oB"
                    psv = ps_o[:, 0:512] if rb_ == 0 else ps_o[:, 512:1024]
                    gwait(T, bank_r)
                    for k in range(NK):
                        ii = T.matmul(psr, wr_sb[:, k * 1024 + j * 128: k * 1024 + (j + 1) * 128],
                                      xrb[:, sb_ + k * CW: sb_ + (k + 1) * CW],
                                      start=(k == 0), stop=(k == NK - 1))
                    cp_r = inc(ii, "st")
                    gwait(T, bank_v)
                    for k in range(NK):
                        ii = T.matmul(psv, wv2_sb[:, k * 1024 + j * 128: k * 1024 + (j + 1) * 128],
                                      kact[:, k * CW:(k + 1) * CW], start=(k == 0), stop=(k == NK - 1))
                    cp_v2 = inc(ii, "st")
                    wait(Sc, "st", cp_r)
                    gwait(Sc, ("rblk", rb_))
                    cp_sig = inc(Sc.activation(rblk[:, rb_ * CW:(rb_ + 1) * CW], psr, AF.Sigmoid), "ss")
                    gset(bank_r, "ss", cp_sig)
                    wait(V, "ss", cp_sig)
                    wait(V, "st", cp_v2)
                    if j > 0:
                        wait(V, "sv", cp_y)
                    cw_ = inc(V.tensor_tensor(t2f[:, :], rblk[:, rb_ * CW:(rb_ + 1) * CW], psv, ALU.mult), "sv")
                    wait(V, "sv", cw_)
                    ii = V.tensor_tensor(x1f[:, j * S + ch * CW: j * S + (ch + 1) * CW],
                                         x1f[:, j * S + ch * CW: j * S + (ch + 1) * CW],
                                         t2f[:, :], ALU.add)
                    cp_y = inc(ii, "sv")
                    gset(bank_v, "sv", cp_y)
                    gset(("rblk", rb_), "sv", cp_y)
                gset(("xkb", ch % 2), "st", cp_k)
                gset(("xrb", ch % 2), "st", cp_r)
                gset("kact", "st", cp_v2)
                wait(Y, "sv", cp_y)
                inc(Y.dma_start(
                    bass.AP(yT, ch * CW, [[S, 128], [128 * S, NK], [1, CW]]),
                    bass.AP(x1f, ch * CW, [[NK * S, 128], [S, NK], [1, CW]])), "dy", 16)

        Y.wait_ge(sems["dy"], cnt["dy"])

    ctx.close()
    return nc


def kernel(L0_attn_x, L0_attn_Wqkv_w, L0_attn_Wqkv_b, L0_attn_out_w, L0_attn_out_b,
           L0_attn_distance_decay, L0_ffn_W_r, L0_ffn_W_k, L0_ffn_W_v, L0_ffn_mu_r, L0_ffn_mu_k):
    x = np.asarray(L0_attn_x, dtype=np.float32)
    wqkvT = np.ascontiguousarray(np.asarray(L0_attn_Wqkv_w, np.float32).T)
    bq = np.asarray(L0_attn_Wqkv_b, np.float32)
    bqkv = np.ascontiguousarray(bq.reshape(24, 128).T)
    bqkv_row = np.ascontiguousarray(bq.reshape(1, 3 * C))
    woT = np.ascontiguousarray(np.asarray(L0_attn_out_w, np.float32).T)
    bo = np.ascontiguousarray(np.asarray(L0_attn_out_b, np.float32).reshape(8, 128).T)
    ddv = np.asarray(L0_attn_distance_decay, np.float32).reshape(1, 1)
    wrT = np.ascontiguousarray(np.asarray(L0_ffn_W_r, np.float32).T)
    wkT = np.ascontiguousarray(np.asarray(L0_ffn_W_k, np.float32).T)
    wvT2 = np.ascontiguousarray(np.asarray(L0_ffn_W_v, np.float32).T)
    mur = np.ascontiguousarray(np.asarray(L0_ffn_mu_r, np.float32).reshape(8, 128).T)
    muk = np.ascontiguousarray(np.asarray(L0_ffn_mu_k, np.float32).reshape(8, 128).T)
    p = np.arange(128, dtype=np.float32)[:, None]
    u = np.arange(STRIPW, dtype=np.float32)[None, :]
    dist = np.abs(p - u + U0).astype(np.float32)
    lin = np.broadcast_to(np.arange(S, dtype=np.float32)[None, :], (128, S)).copy()
    lr = np.maximum(1536.0 - np.arange(1544, dtype=np.float32), 0.0)
    linr = np.broadcast_to(lr[None, :], (128, 1544)).copy()

    phases = os.environ.get("KPHASES", "123")
    if phases not in _CACHED:
        _CACHED[phases] = build_nc(phases)
    nc = _CACHED[phases]

    shared = dict(wqkvT=wqkvT, bqkv=bqkv, bqkv_row=bqkv_row, woT=woT, bo=bo, ddv=ddv,
                  wrT=wrT, wkT=wkT, wvT2=wvT2, mur=mur, muk=muk, dist=dist, lin=lin, linr=linr)
    in_maps = []
    for b in range(8):
        m = dict(shared)
        m["xT"] = np.ascontiguousarray(x[b].T)
        in_maps.append(m)

    res = run_bass_kernel_spmd(nc, in_maps, core_ids=list(range(8)))
    out = np.stack([np.ascontiguousarray(r["yT"].T) for r in res.results], axis=0)
    return out.astype(np.float32)
